# version 03
# baseline (speedup 1.0000x reference)
import numpy as np
from contextlib import ExitStack

import concourse.bass as bass
import concourse.mybir as mybir
from concourse.bass_utils import run_bass_kernel_spmd

F32 = mybir.dt.float32
BF16 = mybir.dt.bfloat16
AF = mybir.ActivationFunctionType
ALU = mybir.AluOpType
AX = mybir.AxisListType

T = 2048
KD = 8
NT = 512
NTT = 4
NE = 8
NFC = 32
NFQ = 4
NW = 4
CPW = 4
L1A_TRUNC = 99
LN_TRUNC = 99


class Sem:
    __slots__ = ("h", "v")

    def __init__(self, h):
        self.h = h
        self.v = 0


class PB:
    ENG = {"pe": "tensor", "dve": "vector", "act": "scalar", "pool": "gpsimd", "sp": "sync"}

    def __init__(self):
        self.nc = bass.Bass("TRN2", target_bir_lowering=False)
        self.stk = ExitStack()
        self.q = {k: [] for k in self.ENG}
        self.sem = {}
        for s in ("pe", "dve", "act", "pool", "sp", "wde", "xde", "cc", "ode",
                  "w1a", "w1b", "w2a", "w2b", "wsa", "wsb", "sla", "slb", "wrs"):
            self.sem[s] = Sem(self.stk.enter_context(self.nc.semaphore(s)))

    def sb(self, name, shape, dt):
        return self.stk.enter_context(self.nc.sbuf_tensor(name, list(shape), dt))

    def ps(self, name, shape, dt=F32):
        return self.stk.enter_context(self.nc.psum_tensor(name, list(shape), dt))

    def wait(self, eng, sem, v=None):
        vv = sem.v if v is None else v
        if vv <= 0:
            return
        self.q[eng].append(lambda E, h=sem.h, vv=vv: E.wait_ge(h, vv))

    SELF_SER = ("dve", "act")

    def do(self, eng, fn, sem=None, k=1):
        if eng in self.SELF_SER:
            own = self.sem[eng]
            if own.v > 0:
                self.q[eng].append(lambda E, h=own.h, vv=own.v: E.wait_ge(h, vv))
            if sem is None:
                sem = own
        if sem is None:
            self.q[eng].append(fn)
        else:
            sem.v += k
            self.q[eng].append(lambda E, fn=fn, h=sem.h, k=k: fn(E).then_inc(h, k))

    def build(self):
        with self.nc.Block() as blk:
            for short, attr in self.ENG.items():
                getattr(blk, attr)(lambda E, lst=self.q[short]: [f(E) for f in lst])
        return self.nc


def _fs(ap):
    return ap[tuple([slice(None)] * len(ap.shape))]


def build_program(blocks=("l0a", "l0f", "l1a", "l1f"), dump_names=()):
    P = PB()
    S = P.sem
    pe, dve, act, pool, sp = S["pe"], S["dve"], S["act"], S["pool"], S["sp"]
    wde, xde, ode, ccs = S["wde"], S["xde"], S["ode"], S["cc"]
    w1s = [S["w1a"], S["w1b"]]
    w2s = [S["w2a"], S["w2b"]]
    wss = [S["wsa"], S["wsb"]]
    sls = [S["sla"], S["slb"]]
    need = set(blocks)
    dumps_made = []

    xin = P.nc.dram_tensor("xin", [1024, T], F32, kind="ExternalInput").ap()
    xout = P.nc.dram_tensor("xout", [1024, T], F32, kind="ExternalOutput").ap()
    dr = {}

    def din(nm, shape):
        dr[nm] = P.nc.dram_tensor(nm, list(shape), F32, kind="ExternalInput").ap()

    for nm, shp in [("c_ones", [128, 16]), ("c_onesrow", [1, 128]), ("c_sel", [16, 8, 128]),
                    ("c_headsel", [128, 8, 16]), ("c_ss", [128, 16]), ("c_eones", [8, 8, 128]),
                    ("c_id", [128, 128]), ("c_mask", [128, 128]),
                    ("c_fb", [128, 6])]:
        din(nm, shp)
    l0a_mode = ("scores" if "l0a_scores" in need else
                ("fused" if "l0a_f" in need else ("ext" if "l0a" in need else None)))
    if l0a_mode is not None:
        din("w_l0blk", [128, 3, 128])
        din("c_mysel", [128, 16])
        if l0a_mode == "scores":
            s_my = P.nc.dram_tensor("s_my", [16, T], F32, kind="ExternalOutput").ap()
        elif l0a_mode == "ext":
            din("s_all", [128, T])
            s_all = dr["s_all"]
        else:
            din("xall", [NE, 1024, T])
            din("w_l0qbd", [128, 128])
            din("w_l0kbd", [128, 128])
            din("c_misel", [16, 8])
    for L in (0, 1):
        if ("l0f" if L == 0 else "l1f") in need:
            din(f"w_l{L}wr", [128, 8, 8])
            din(f"w_l{L}w1", [NE, NFC, 128, 8, 128])
            din(f"w_l{L}w2", [NE, NFQ, 8, 128, 8, 128])
            din(f"p_br{L}", [8, 1])
    if "l1f" in need:
        din("p_b1", [128, 8, 32])
        din("p_b2", [128, 8, 8])
        din("p_lng", [128, 2, 8])
    if "l1a" in need:
        import os as _os0
        if _os0.environ.get("SKIP_WQKVO") != "1":
            for w in ("wq", "wk", "wv", "wo"):
                din("w_l1" + w, [128, 8, 1024])
        din("w_l1wf", [128, 8, 16])
        din("p_delta", [16, 1])
        din("p_nw", [128, 8])

    xsb = P.sb("xsb", [128, KD, T], BF16)
    xn = P.sb("xn", [128, KD, T], BF16)
    U = P.sb("U", [128, 40960], BF16)
    W = P.sb("W", [128, 16384], BF16)
    SCRF = P.sb("SCRF", [128, 4096], F32)
    rows_sm = P.sb("rows_sm", [128, 2, NT], F32)
    grow_sm = P.sb("grow_sm", [128, NT], BF16)
    sqt = P.sb("sqt", [128, 2, NT], BF16)
    cones = P.sb("cones", [128, 16], BF16)
    conesrow = P.sb("conesrow", [1, 128], BF16)
    conesrow_f = P.sb("conesrow_f", [1, 128], F32)
    selb = P.sb("selb", [16, 8, 128], BF16)
    headsb = P.sb("headsb", [128, 8, 16], BF16)
    ssb = P.sb("ssb", [128, 16], BF16)
    eones = P.sb("eones", [8, 8, 128], BF16)
    idb = P.sb("idb", [128, 128], BF16)
    idf = P.sb("idf", [128, 128], F32)
    maskt = P.sb("maskt", [128, 128], BF16)
    fbc = P.sb("fbc", [128, 6], F32)
    wrsb = P.sb("wrsb", [128, 8, 8], F32)
    brsb = P.sb("brsb", [8, 2], F32)
    lngb = P.sb("lngb", [128, 2, 8], F32)
    nwsb = P.sb("nwsb", [128, 8], F32)
    dlt = P.sb("dlt", [16, 1], F32)
    b1sb = P.sb("b1sb", [128, 8, 32], F32)
    b2sb = P.sb("b2sb", [128, 8, 8], F32)
    ohtm = P.sb("ohtm", [128, 8], BF16)
    gcolb = P.sb("gcolb", [128, 16], BF16)
    wfTsb = P.sb("wfTsb", [128, 8, 16], BF16)
    miselb = P.sb("miselb", [16, 8], F32)
    wcolf = P.sb("wcolf", [128, 1], F32)
    m11f = P.sb("m11f", [128, 1], F32)

    PH = P.ps("PH", [128, 4, 512])
    PY = P.ps("PY", [128, 2, 512])
    PS1 = P.ps("PS1", [128, 512])
    PSM = P.ps("PSM", [128, 1024], BF16)
    pt_oh = PSM[0:8, 0:128]
    pt_lb = PSM[:, 128:144]
    ptrow = PSM[0:1, 144:272]

    def view(base, off, shape):
        n = int(np.prod(shape))
        ap = base[:, off:off + n]
        if len(shape) > 1:
            names = " ".join(f"a{i}" for i in range(len(shape)))
            ap = ap.rearrange(f"p ({names}) -> p {names}",
                              **{f"a{i}": s for i, s in enumerate(shape)})
        return ap

    hq_v = view(U, 0, (8, T))
    yac_v = view(U, 8 * T, (8, T))
    gateb_v = view(U, 16 * T, (T,))
    gmask_v = view(U, 17 * T, (T,))
    ohfm_v = view(U, 18 * T, (T,))
    ohfm_ct = view(U, 18 * T, (16, 128))
    l0_q = view(U, 0, (NT,))
    l0_qk = view(U, NT, (NT,))
    l0_vs = view(U, 2 * NT, (NT,))
    l0_e = view(U, 3 * NT, (NT,))
    l0_sall = view(U, 4 * NT, (2, NT))
    l0_smine = view(U, 6 * NT, (T,))
    l0_probs = view(U, 6 * NT + T, (T,))
    l0_emine = view(U, 6 * NT + 2 * T, (2, NT))
    l0_inv = view(U, 6 * NT + 3 * T, (2, NT))
    l0_wblk = view(U, 6 * NT + 4 * T, (3, 128))
    f_xb = view(U, 16384, (2, 8, NT))
    f_z = view(U, 24576, (NT,))
    f_xz = view(U, 24576 + NT, (NT,))
    f_sq = view(U, 24576 + 2 * NT, (NT,))
    f_et = view(U, 24576 + 3 * NT, (NT,))
    f_em = view(U, 26624, (T,))
    f_mblk = view(U, 28672, (128,))
    f_mblkT = view(U, 28672 + 128, (128,))
    f_qbd = view(U, 28672 + 256, (128,))
    f_kbd = view(U, 28672 + 384, (128,))
    f_wsel = view(U, 28672 + 512, (8, 16))
    f_m2x = view(U, 28672 + 640, (16,))
    qfm_v = view(U, 0, (8, NT))
    kfm_v = view(U, 4096, (8, NT))
    ktm_v = view(U, 8192, (CPW, 1024))
    vtm_v = view(U, 12288, (CPW, 1024))
    obf_v = view(U, 16384, (8, NT))
    a2_v = view(U, 20480, (16, 128))
    kds_v = view(U, 22528, (4, 128))
    elbbc_v = view(U, 23040, (8, 128))
    lbbf_v = view(U, 24064, (T,))
    elbbf_v = view(U, 24064 + T, (T,))
    sbf_v = view(U, 24064 + 2 * T, (16, 128))
    etot_bf = view(U, 24064 + 2 * T + 2048, (16,))
    sfx_v = view(U, 24064 + 2 * T + 2048 + 16, (T,))
    kmsk_v = view(U, 24064 + 3 * T + 2048 + 16, (2, 8, NT))
    qs_v = view(U, 24064 + 2 * T + 2048 + 16, (4, 128))
    w1c_v = view(W, 0, (2, 8, 128))
    w2c_v = view(W, 2048, (2, 8, 128))
    wstream_v = view(W, 0, (2, 8, 1024))
    stg32 = view(SCRF, 0, (2, NT))
    st16 = view(SCRF, 1024, (2, NT))
    lftt_v = view(SCRF, 2048, (2, NT))
    cols_v = view(SCRF, 3072, (8,))
    ex_v = view(SCRF, 3072 + 8, (8,))
    secol_v = view(SCRF, 3072 + 32, (16,))
    vsump_v = view(SCRF, 3072 + 48, (KD, NTT))
    vsum8_v = view(SCRF, 3072 + 80, (KD,))
    vsc2_v = view(SCRF, 3072 + 88, (KD,))
    ginv_v = view(SCRF, 3072 + 96, (16,))
    enltm_v = view(SCRF, 2048, (16, 16))
    ktab_v = view(SCRF, 2048 + 256, (16, 16))
    Sf_v = view(SCRF, 2048 + 512, (8, 64))
    etotbc_v = view(SCRF, 2048 + 1024, (8, 16))
    totf_v = view(SCRF, 2048 + 1024 + 128, (16,))

    TTS = [slice(tt * NT, (tt + 1) * NT) for tt in range(NTT)]

    for nm, dst in [("c_ones", cones), ("c_onesrow", conesrow), ("c_sel", selb),
                    ("c_headsel", headsb), ("c_ss", ssb), ("c_eones", eones),
                    ("c_id", idb), ("c_mask", maskt)]:
        P.do("pool", lambda E, d=dst, s=dr[nm]: E.dma_start(out=_fs(d), in_=_fs(s)), wde, 16)
    P.do("sp", lambda E: E.dma_start(out=_fs(idf), in_=_fs(dr["c_id"])), xde, 16)
    P.do("sp", lambda E: E.dma_start(out=_fs(fbc), in_=_fs(dr["c_fb"])), xde, 16)
    P.do("sp", lambda E: E.dma_start(out=_fs(conesrow_f), in_=_fs(dr["c_onesrow"])), xde, 16)
    P.do("pool", lambda E: E.dma_start(
        out=xsb[:, :, :], in_=xin.rearrange("(kd p) t -> p kd t", p=128)), wde, 16)
    if l0a_mode is not None:
        P.do("pool", lambda E: E.dma_start(out=l0_wblk, in_=_fs(dr["w_l0blk"])), wde, 16)
        myselb = P.sb("myselb", [128, 16], BF16)
        P.do("pool", lambda E: E.dma_start(out=_fs(myselb), in_=_fs(dr["c_mysel"])), wde, 16)
    if l0a_mode == "fused":
        P.do("pool", lambda E: E.dma_start(out=f_qbd, in_=_fs(dr["w_l0qbd"])), wde, 16)
        P.do("pool", lambda E: E.dma_start(out=f_kbd, in_=_fs(dr["w_l0kbd"])), wde, 16)
        P.do("sp", lambda E: E.dma_start(out=_fs(miselb), in_=_fs(dr["c_misel"])), xde, 16)
    if "l1a" in need:
        import os as _os
        if _os.environ.get("SKIP_WF") != "1":
            P.do("pool", lambda E: E.dma_start(out=_fs(wfTsb), in_=_fs(dr["w_l1wf"])), wde, 16)
        if _os.environ.get("SKIP_DLT") != "1":
            P.do("sp", lambda E: E.dma_start(out=_fs(dlt), in_=_fs(dr["p_delta"])), xde, 16)
        if _os.environ.get("SKIP_NW") != "1":
            P.do("sp", lambda E: E.dma_start(out=_fs(nwsb), in_=_fs(dr["p_nw"])), xde, 16)
    if "l1f" in need:
        P.do("sp", lambda E: E.dma_start(out=_fs(b1sb), in_=_fs(dr["p_b1"])), xde, 16)
        P.do("sp", lambda E: E.dma_start(out=_fs(b2sb), in_=_fs(dr["p_b2"])), xde, 16)
        P.do("sp", lambda E: E.dma_start(out=_fs(lngb), in_=_fs(dr["p_lng"])), xde, 16)
    if "l0f" in need:
        P.do("sp", lambda E: E.dma_start(out=brsb[:, 0:1], in_=_fs(dr["p_br0"])), xde, 16)
    if "l1f" in need:
        P.do("sp", lambda E: E.dma_start(out=brsb[:, 1:2], in_=_fs(dr["p_br1"])), xde, 16)
    for e in ("pe", "dve", "act"):
        P.wait(e, wde)
        P.wait(e, xde)
    P.wait("sp", wde)
    P.wait("pool", xde)

    def dump(name, ap, shape, dt=F32):
        if name not in dump_names:
            return
        d = P.nc.dram_tensor("dump_" + name, list(shape), dt, kind="ExternalOutput").ap()
        for e in ("pe", "dve", "act"):
            P.do(e, lambda E: E.nop(), S[e], 1)
            P.wait("sp", S[e])
        P.do("sp", lambda E, d=d, ap=ap: E.dma_start(out=_fs(d), in_=ap), xde, 16)
        P.wait("sp", xde)
        dumps_made.append(name)

    ph_use = [{"dve": 0, "act": 0} for _ in range(4)]
    py_use = [{"dve": 0, "act": 0} for _ in range(2)]
    ps1_use = {"dve": 0, "act": 0}

    def pe_claim(d):
        P.wait("pe", dve, d["dve"])
        P.wait("pe", act, d["act"])

    pt_oh_use = [0]
    ptrow_use = [0]
    rows_war = [0]

    def emit_ln(src, out_bf, router=None, affine=False, eps=1e-5):
        P.wait("act", dve)
        P.wait("act", pe)
        P.wait("pe", dve)
        stg_pe_use = [0, 0]
        sq_pe_use = [0, 0]
        rows_pe_use = [rows_war[0]]
        for tt in range(NTT):
            if LN_TRUNC < 1 + tt:
                break
            ts = TTS[tt]
            for kd in range(KD):
                par = kd % 2
                P.wait("act", pe, sq_pe_use[par])
                P.do("act", lambda E, kd=kd, ts=ts, par=par: E.activation(
                    sqt[:, par, :], src[:, kd, ts], AF.Square), act, 1)
                sq_done = act.v
                if kd == 0:
                    pe_claim(ps1_use)
                    pe_claim(py_use[0])
                P.do("pe", lambda E, kd=kd, ts=ts: E.matmul(
                    PS1[0:16, :], cones[:, :], src[:, kd, ts],
                    start=(kd == 0), stop=(kd == KD - 1)))
                P.wait("pe", act, sq_done)
                P.do("pe", lambda E, kd=kd, par=par: E.matmul(
                    PY[0:16, 0, :], cones[:, :], sqt[:, par, :],
                    start=(kd == 0), stop=(kd == KD - 1)), pe, 1)
                sq_pe_use[par] = pe.v
            stats_done = pe.v
            if LN_TRUNC < 10 + tt:
                continue
            P.wait("act", pe, stats_done)
            P.do("act", lambda E: E.activation(st16[0:16, 0, :], PS1[0:16, :], AF.Copy,
                                               scale=1.0 / 1024.0), act, 1)
            ps1_use["act"] = act.v
            P.do("act", lambda E: E.activation(st16[0:16, 1, :], PY[0:16, 0, :], AF.Copy,
                                               scale=1.0 / 1024.0), act, 1)
            py_use[0]["act"] = act.v
            pe_claim_needed = py_use[1]
            P.wait("act", dve, pe_claim_needed["dve"])
            P.do("act", lambda E: E.activation(PY[0:16, 1, :], st16[0:16, 0, :],
                                               AF.Square), act, 1)
            m2_done = act.v
            P.wait("dve", act, m2_done)
            P.do("dve", lambda E: E.tensor_tensor(st16[0:16, 1, :], st16[0:16, 1, :],
                                                  PY[0:16, 1, :], ALU.subtract), dve, 1)
            py_use[1]["dve"] = dve.v
            var_done = dve.v
            P.wait("act", dve, var_done)
            P.do("act", lambda E: E.activation(st16[0:16, 1, :], st16[0:16, 1, :],
                                                AF.Ln, bias=fbc[0:16, 0:1]), act, 1)
            P.wait("act", pe, rows_pe_use[0])
            P.do("act", lambda E: E.activation(rows_sm[0:1, 1, :], st16[0:1, 1, :],
                                               AF.Exp, scale=-0.5), act, 1)
            P.do("act", lambda E: E.activation(rows_sm[0:1, 0, :], st16[0:1, 0, :],
                                               AF.Copy), act, 1)
            rows_done = act.v
            if LN_TRUNC < 20 + tt:
                continue
            P.wait("pe", act, rows_done)
            pe_claim(py_use[0])
            pe_claim(py_use[1])
            P.do("pe", lambda E: E.matmul(PY[:, 0, :], conesrow_f[:, :],
                                          rows_sm[0:1, 0, :], start=True, stop=True))
            P.do("pe", lambda E: E.matmul(PY[:, 1, :], conesrow_f[:, :],
                                          rows_sm[0:1, 1, :], start=True, stop=True), pe, 1)
            bc_done = pe.v
            rows_pe_use[0] = pe.v
            if LN_TRUNC < 30 + tt:
                continue
            P.wait("dve", pe, bc_done)
            for kd in range(KD):
                spar = kd % 2
                if router is not None:
                    P.wait("dve", pe, stg_pe_use[spar])
                P.do("dve", lambda E, kd=kd, ts=ts, spar=spar: E.scalar_tensor_tensor(
                    stg32[:, spar, :], PY[:, 0, :], -1.0, src[:, kd, ts],
                    ALU.mult, ALU.add), dve, 1)
                if router is not None:
                    P.do("dve", lambda E, spar=spar: E.tensor_tensor(
                        stg32[:, spar, :], stg32[:, spar, :], PY[:, 1, :],
                        ALU.mult), dve, 1)
                    P.do("dve", lambda E, kd=kd, ts=ts, spar=spar: E.tensor_copy(
                        out_bf[:, kd, ts], stg32[:, spar, :]), dve, 1)
                    st_done = dve.v
                    P.wait("pe", dve, st_done)
                    router(kd, tt, stg32[:, spar, :])
                    stg_pe_use[spar] = pe.v
                elif affine:
                    P.do("dve", lambda E, spar=spar: E.tensor_tensor(
                        stg32[:, spar, :], stg32[:, spar, :], PY[:, 1, :],
                        ALU.mult), dve, 1)
                    P.do("dve", lambda E, kd=kd, ts=ts, spar=spar: E.tensor_scalar(
                        out_bf[:, kd, ts], stg32[:, spar, :],
                        lngb[:, 0, kd:kd + 1], lngb[:, 1, kd:kd + 1],
                        ALU.mult, ALU.add), dve, 1)
                else:
                    P.do("dve", lambda E, kd=kd, ts=ts, spar=spar: E.tensor_tensor(
                        out_bf[:, kd, ts], stg32[:, spar, :], PY[:, 1, :],
                        ALU.mult), dve, 1)
            py_use[0]["dve"] = py_use[1]["dve"] = dve.v
        rows_war[0] = rows_pe_use[0]
        P.wait("pe", dve)

    def emit_l0a():
        emit_ln(xsb, xn)
        if l0a_mode == "scores":
            smine_f = view(SCRF, 2048, (T,))
            for tt in range(NTT):
                ts = TTS[tt]
                for kd in range(KD):
                    for b in range(2):
                        pe_claim(ph_use[b])
                    P.do("pe", lambda E, kd=kd, ts=ts: E.matmul(
                        PH[:, 0, :], l0_wblk[:, 0, :], xn[:, kd, ts],
                        start=True, stop=True))
                    P.do("pe", lambda E, kd=kd, ts=ts: E.matmul(
                        PH[:, 1, :], l0_wblk[:, 1, :], xn[:, kd, ts],
                        start=True, stop=True), pe, 1)
                    qk_mm = pe.v
                    P.wait("dve", pe, qk_mm)
                    P.do("dve", lambda E: E.tensor_copy(l0_q, PH[:, 0, :]), dve, 1)
                    P.do("dve", lambda E: E.tensor_tensor(l0_qk, l0_q, PH[:, 1, :],
                                                          ALU.mult), dve, 1)
                    qk_done = dve.v
                    ph_use[0]["dve"] = ph_use[1]["dve"] = dve.v
                    P.wait("pe", dve, qk_done)
                    if kd == 0:
                        pe_claim(ps1_use)
                    P.do("pe", lambda E, kd=kd: E.matmul(
                        PS1[0:16, :], headsb[:, kd, :], l0_qk,
                        start=(kd == 0), stop=(kd == KD - 1)),
                        pe if kd == KD - 1 else None, 1)
                s_done = pe.v
                P.wait("dve", pe, s_done)
                P.do("dve", lambda E, ts=ts: E.tensor_copy(smine_f[0:16, ts],
                                                           PS1[0:16, :]), dve, 1)
                ps1_use["dve"] = dve.v
            P.wait("sp", dve)
            P.do("sp", lambda E, smine_f=smine_f: E.dma_start(
                out=_fs(s_my), in_=smine_f[0:16, :]), xde, 16)
            P.wait("sp", xde)
            return
        for tt in range(NTT):
            ts = TTS[tt]
            for kd in range(KD):
                pe_claim(ph_use[2])
                P.do("pe", lambda E, kd=kd, ts=ts: E.matmul(
                    PH[:, 2, :], l0_wblk[:, 2, :], xn[:, kd, ts],
                    start=True, stop=True), pe, 1)
                v_mm = pe.v
                P.wait("act", pe, v_mm)
                P.do("act", lambda E, kd=kd, tt=tt: E.activation(
                    l0_vs, PH[:, 2, :], AF.Copy,
                    accum_out=vsump_v[:, kd, tt:tt + 1]), act, 1)
                ph_use[2]["act"] = act.v
        P.wait("dve", act)
        P.do("dve", lambda E: E.tensor_tensor(vsum8_v, vsump_v[:, :, 0],
                                              vsump_v[:, :, 1], ALU.add), dve, 1)
        P.do("dve", lambda E: E.tensor_tensor(vsc2_v, vsump_v[:, :, 2],
                                              vsump_v[:, :, 3], ALU.add), dve, 1)
        P.do("dve", lambda E: E.tensor_tensor(vsum8_v, vsum8_v, vsc2_v, ALU.add), dve, 1)
        sall_act_use = [0, 0]
        l0e_pe_use = [0]
        inv_dve_use = [0]
        for tt in range(NTT):
            ts = TTS[tt]
            par = tt % 2
            P.wait("pool", act, sall_act_use[par])
            P.do("pool", lambda E, ts=ts, par=par: E.dma_start(
                out=l0_sall[:, par, :], in_=s_all[:, ts]), sls[par], 16)
            P.wait("act", sls[par])
            P.wait("act", pe, l0e_pe_use[0])
            P.do("act", lambda E, par=par: E.activation(
                l0_e, l0_sall[:, par, :], AF.Exp, scale=1.0 / 32.0), act, 1)
            e_done = act.v
            sall_act_use[par] = act.v
            P.wait("pe", act, e_done)
            pe_claim(ps1_use)
            pe_claim(py_use[0])
            P.do("pe", lambda E: E.matmul(PS1[0:16, :], ssb[:, :], l0_e,
                                          start=True, stop=True))
            P.do("pe", lambda E: E.matmul(PY[0:16, 0, :], myselb[:, :], l0_e,
                                          start=True, stop=True), pe, 1)
            se_done = pe.v
            l0e_pe_use[0] = pe.v
            P.wait("act", pe, se_done)
            P.do("act", lambda E: E.activation(st16[0:16, 0, :], PS1[0:16, :],
                                               AF.Ln), act, 1)
            ps1_use["act"] = act.v
            P.wait("act", dve, inv_dve_use[0])
            P.do("act", lambda E: E.activation(st16[0:16, 1, :], st16[0:16, 0, :],
                                               AF.Exp, scale=-1.0), act, 1)
            inv_done = act.v
            P.wait("dve", act, inv_done)
            P.do("dve", lambda E, ts=ts: E.tensor_tensor(
                l0_probs[0:16, ts], PY[0:16, 0, :], st16[0:16, 1, :], ALU.mult), dve, 1)
            py_use[0]["dve"] = dve.v
            inv_dve_use[0] = dve.v
        probs_done = dve.v
        P.wait("pe", dve, probs_done)
        for tt in range(NTT):
            ts = TTS[tt]
            for kd in range(KD):
                bk = kd % 4
                pe_claim(ph_use[bk])
                P.do("pe", lambda E, kd=kd, ts=ts, bk=bk: E.matmul(
                    PH[:, bk, :], selb[:, kd, :], l0_probs[0:16, ts],
                    start=True, stop=True), pe, 1)
                mm_done = pe.v
                P.wait("dve", pe, mm_done)
                P.do("dve", lambda E, kd=kd, ts=ts, bk=bk: E.scalar_tensor_tensor(
                    xsb[:, kd, ts], PH[:, bk, :], vsum8_v[:, kd:kd + 1], xsb[:, kd, ts],
                    ALU.mult, ALU.add), dve, 1)
                ph_use[bk]["dve"] = dve.v

    def emit_l0a_fused():
        emit_ln(xsb, xn)
        seacc = view(SCRF, 2048, (T,))
        P.wait("pe", wde)
        P.wait("pe", xde)
        pe_claim(ph_use[0])
        pe_claim(ph_use[1])
        P.do("pe", lambda E: E.matmul(PH[:, 0, 0:128], f_qbd, f_kbd,
                                      start=True, stop=True))
        P.do("pe", lambda E: E.matmul(PH[:, 1, 0:128], f_kbd, f_qbd,
                                      start=True, stop=True), pe, 1)
        mb = pe.v
        P.wait("dve", pe, mb)
        P.do("dve", lambda E: E.tensor_copy(f_mblk, PH[:, 0, 0:128]), dve, 1)
        P.do("dve", lambda E: E.tensor_copy(f_mblkT, PH[:, 1, 0:128]), dve, 1)
        ph_use[0]["dve"] = ph_use[1]["dve"] = dve.v
        mev = dve.v
        P.wait("pe", dve, mev)
        pe_claim(py_use[0])
        pe_claim(py_use[1])
        P.do("pe", lambda E: E.matmul(PY[:, 0, 0:1], f_mblk, cones[:, 0:1],
                                      start=True, stop=True))
        P.do("pe", lambda E: E.matmul(PY[:, 1, 0:1], f_mblkT, cones[:, 0:1],
                                      start=True, stop=True), pe, 1)
        wv_ = pe.v
        P.wait("dve", pe, wv_)
        P.do("dve", lambda E: E.tensor_copy(wcolf[:, :], PY[:, 0, 0:1]), dve, 1)
        P.do("dve", lambda E: E.tensor_tensor(wcolf[:, :], wcolf[:, :],
                                              PY[:, 1, 0:1], ALU.add), dve, 1)
        py_use[0]["dve"] = py_use[1]["dve"] = dve.v
        P.do("dve", lambda E: E.tensor_copy(f_m2x[:, 1:2], wcolf[:, :]), dve, 1)
        P.do("dve", lambda E: E.tensor_copy(f_m2x[:, 2:3], fbc[:, 4:5]), dve, 1)
        for kd in range(KD):
            P.do("dve", lambda E, kd=kd: E.tensor_scalar(
                f_wsel[:, kd, :], headsb[:, kd, :], wcolf[:, 0:1], None,
                ALU.mult), dve, 1)
        wsel_done = dve.v
        P.wait("pe", dve, wsel_done)
        pe_claim(ps1_use)
        P.do("pe", lambda E: E.matmul(PS1[0:1, 0:1], f_m2x[:, 1:2], f_m2x[:, 2:3],
                                      start=True, stop=True), pe, 1)
        m2s = pe.v
        P.wait("dve", pe, m2s)
        P.do("dve", lambda E: E.tensor_copy(f_m2x[0:1, 0:1], PS1[0:1, 0:1]), dve, 1)
        ps1_use["dve"] = dve.v
        m2e = dve.v
        P.wait("pe", dve, m2e)
        pe_claim(py_use[0])
        P.do("pe", lambda E: E.matmul(PY[:, 0, 0:1], conesrow[:, :],
                                      f_m2x[0:1, 0:1], start=True, stop=True), pe, 1)
        m2b = pe.v
        P.wait("dve", pe, m2b)
        P.do("dve", lambda E: E.tensor_scalar(m11f[:, :], PY[:, 0, 0:1], 0.5, None,
                                              ALU.mult), dve, 1)
        py_use[0]["dve"] = dve.v
        P.do("dve", lambda E: E.memset(seacc[0:16, :], 0.0), dve, 1)
        P.do("dve", lambda E: E.memset(f_em[0:16, :], 0.0), dve, 1)
        xb_pool_use = [0, 0]
        xb_pe_use = [0, 0]
        for ci in range(NE):
            for tt in range(NTT):
                ts = TTS[tt]
                par = (ci * NTT + tt) % 2
                P.wait("pool", dve, xb_pool_use[par])
                P.wait("pool", pe, xb_pe_use[par])
                P.do("pool", lambda E, ci=ci, ts=ts, par=par: E.dma_start(
                    out=f_xb[:, par, :, :],
                    in_=dr["xall"][ci].rearrange("(kd p) t -> p kd t", p=128)[:, :, ts]),
                    sls[par], 16)
                P.wait("pe", sls[par])
                last_stat = [0]
                for kd in range(KD):
                    zb = kd % 2
                    pe_claim(ph_use[zb])
                    if kd == 0:
                        pe_claim(ps1_use)
                        pe_claim(py_use[0])
                        pe_claim(py_use[1])
                        pe_claim(ph_use[2])
                    P.do("pe", lambda E, kd=kd, zb=zb, par=par: E.matmul(
                        PH[:, zb, :], f_mblkT, f_xb[:, par, kd, :],
                        start=True, stop=True), pe, 1)
                    zmm = pe.v
                    P.wait("dve", pe, zmm)
                    P.wait("dve", pe, last_stat[0])
                    P.do("dve", lambda E, kd=kd, zb=zb, par=par: E.tensor_tensor(
                        f_xz, f_xb[:, par, kd, :], PH[:, zb, :], ALU.mult), dve, 1)
                    ph_use[zb]["dve"] = dve.v
                    P.do("dve", lambda E, kd=kd, par=par: E.tensor_tensor(
                        f_sq, f_xb[:, par, kd, :], f_xb[:, par, kd, :],
                        ALU.mult), dve, 1)
                    sx = dve.v
                    P.wait("pe", dve, sx)
                    P.do("pe", lambda E, kd=kd, par=par: E.matmul(
                        PS1[0:16, :], cones[:, :], f_xb[:, par, kd, :],
                        start=(kd == 0), stop=(kd == KD - 1)))
                    P.do("pe", lambda E, kd=kd: E.matmul(
                        PY[0:16, 0, :], cones[:, :], f_sq,
                        start=(kd == 0), stop=(kd == KD - 1)))
                    P.do("pe", lambda E, kd=kd: E.matmul(
                        PY[0:16, 1, :], headsb[:, kd, :], f_xz,
                        start=(kd == 0), stop=(kd == KD - 1)))
                    P.do("pe", lambda E, kd=kd, par=par: E.matmul(
                        PH[0:16, 2, 0:512], f_wsel[:, kd, :], f_xb[:, par, kd, :],
                        start=(kd == 0), stop=(kd == KD - 1)), pe, 1)
                    last_stat[0] = pe.v
                stats_done = pe.v
                xb_pool_use[par] = dve.v
                xb_pe_use[par] = pe.v
                P.wait("act", pe, stats_done)
                P.do("act", lambda E: E.activation(st16[0:16, 0, :], PS1[0:16, :],
                                                   AF.Copy, scale=1.0 / 1024.0), act, 1)
                P.do("act", lambda E: E.activation(st16[0:16, 1, :], PY[0:16, 0, :],
                                                   AF.Copy, scale=1.0 / 1024.0), act, 1)
                P.wait("act", dve)
                P.do("act", lambda E: E.activation(PH[0:16, 3, 0:512], st16[0:16, 0, :],
                                                   AF.Square), act, 1)
                m2d = act.v
                P.wait("dve", act, m2d)
                P.do("dve", lambda E: E.tensor_tensor(st16[0:16, 1, :], st16[0:16, 1, :],
                                                      PH[0:16, 3, 0:512],
                                                      ALU.subtract), dve, 1)
                vd = dve.v
                P.wait("act", dve, vd)
                P.do("act", lambda E: E.activation(st16[0:16, 1, :], st16[0:16, 1, :],
                                                   AF.Ln, bias=fbc[0:16, 0:1]), act, 1)
                P.do("act", lambda E: E.activation(st16[0:16, 1, :], st16[0:16, 1, :],
                                                   AF.Exp, scale=-1.0), act, 1)
                ivd = act.v
                P.wait("dve", act, ivd)
                P.do("dve", lambda E: E.tensor_tensor(stg32[0:16, 0, :], st16[0:16, 0, :],
                                                      PH[0:16, 2, 0:512],
                                                      ALU.mult), dve, 1)
                P.do("dve", lambda E: E.tensor_tensor(stg32[0:16, 0, :], PY[0:16, 1, :],
                                                      stg32[0:16, 0, :],
                                                      ALU.subtract), dve, 1)
                P.do("dve", lambda E: E.scalar_tensor_tensor(
                    stg32[0:16, 0, :], PH[0:16, 3, 0:512], m11f[0:16, 0:1],
                    stg32[0:16, 0, :], ALU.mult, ALU.add), dve, 1)
                P.do("dve", lambda E: E.tensor_tensor(stg32[0:16, 0, :],
                                                      stg32[0:16, 0, :],
                                                      st16[0:16, 1, :], ALU.mult),
                     dve, 1)
                sdone = dve.v
                ps1_use["dve"] = py_use[0]["dve"] = py_use[1]["dve"] = dve.v
                ph_use[2]["dve"] = ph_use[3]["dve"] = dve.v
                ps1_use["act"] = py_use[0]["act"] = act.v
                P.wait("act", dve, sdone)
                P.do("act", lambda E: E.activation(f_et[0:16, :], stg32[0:16, 0, :],
                                                   AF.Exp, scale=1.0 / 32.0), act, 1)
                etd = act.v
                P.wait("dve", act, etd)
                P.do("dve", lambda E, ts=ts: E.tensor_tensor(
                    seacc[0:16, ts], seacc[0:16, ts], f_et[0:16, :], ALU.add), dve, 1)
                P.do("dve", lambda E, ci=ci, ts=ts: E.scalar_tensor_tensor(
                    f_em[0:16, ts], f_et[0:16, :], miselb[:, ci:ci + 1],
                    f_em[0:16, ts], ALU.mult, ALU.add), dve, 1)
        for tt in range(NTT):
            ts = TTS[tt]
            P.wait("act", dve)
            P.do("act", lambda E, ts=ts: E.activation(st16[0:16, 0, :], seacc[0:16, ts],
                                                      AF.Ln), act, 1)
            P.do("act", lambda E: E.activation(st16[0:16, 0, :], st16[0:16, 0, :],
                                               AF.Exp, scale=-1.0), act, 1)
            ivse = act.v
            P.wait("dve", act, ivse)
            P.do("dve", lambda E, ts=ts: E.tensor_tensor(
                l0_probs[0:16, ts], f_em[0:16, ts], st16[0:16, 0, :], ALU.mult),
                dve, 1)
        probs_done = dve.v
        for tt in range(NTT):
            ts = TTS[tt]
            for kd in range(KD):
                pe_claim(ph_use[2])
                P.do("pe", lambda E, kd=kd, ts=ts: E.matmul(
                    PH[:, 2, :], l0_wblk[:, 2, :], xn[:, kd, ts],
                    start=True, stop=True), pe, 1)
                v_mm = pe.v
                P.wait("act", pe, v_mm)
                P.do("act", lambda E, kd=kd, tt=tt: E.activation(
                    l0_vs, PH[:, 2, :], AF.Copy,
                    accum_out=vsump_v[:, kd, tt:tt + 1]), act, 1)
                ph_use[2]["act"] = act.v
        P.wait("dve", act)
        P.do("dve", lambda E: E.tensor_tensor(vsum8_v, vsump_v[:, :, 0],
                                              vsump_v[:, :, 1], ALU.add), dve, 1)
        P.do("dve", lambda E: E.tensor_tensor(vsc2_v, vsump_v[:, :, 2],
                                              vsump_v[:, :, 3], ALU.add), dve, 1)
        P.do("dve", lambda E: E.tensor_tensor(vsum8_v, vsum8_v, vsc2_v, ALU.add),
             dve, 1)
        vs_done = dve.v
        P.wait("pe", dve, max(probs_done, vs_done))
        for tt in range(NTT):
            ts = TTS[tt]
            for kd in range(KD):
                bk = kd % 4
                pe_claim(ph_use[bk])
                P.do("pe", lambda E, kd=kd, ts=ts, bk=bk: E.matmul(
                    PH[:, bk, :], selb[:, kd, :], l0_probs[0:16, ts],
                    start=True, stop=True), pe, 1)
                mm_done = pe.v
                P.wait("dve", pe, mm_done)
                P.do("dve", lambda E, kd=kd, ts=ts, bk=bk: E.scalar_tensor_tensor(
                    xsb[:, kd, ts], PH[:, bk, :], vsum8_v[:, kd:kd + 1], xsb[:, kd, ts],
                    ALU.mult, ALU.add), dve, 1)
                ph_use[bk]["dve"] = dve.v

    def emit_ffn(L):
        P.wait("sp", pe)
        P.do("sp", lambda E: E.dma_start(out=_fs(wrsb), in_=_fs(dr[f"w_l{L}wr"])),
             S["wrs"], 16)
        P.wait("pe", S["wrs"])

        def router(kd, tt, stage_ap):
            if kd == 0:
                pe_claim(ph_use[tt])
            P.do("pe", lambda E, kd=kd, tt=tt, ap=stage_ap: E.matmul(
                PH[0:8, tt, :], wrsb[:, kd, :], ap,
                start=(kd == 0), stop=(kd == KD - 1)), pe, 1)

        emit_ln(xsb, xn, router=router)
        if L == 1:
            pass
        for tt in range(NTT):
            ts = TTS[tt]
            P.wait("dve", pe)
            P.do("dve", lambda E, tt=tt: E.tensor_scalar(
                lftt_v[0:8, tt % 2, :], PH[0:8, tt, :], brsb[:, L:L + 1], None,
                ALU.add), dve, 1)
            ph_use[tt]["dve"] = dve.v
            lf_done = dve.v
            for ci in range(4):
                ct = tt * 4 + ci
                cs = slice(ci * 128, (ci + 1) * 128)
                slot = ci % 2
                P.wait("pe", dve, lf_done)
                pe_claim(py_use[slot])
                P.do("pe", lambda E, cs=cs, slot=slot, tt=tt: E.transpose(
                    PY[:, slot, 0:8], lftt_v[0:8, tt % 2, cs], idf[0:8, 0:8]), pe, 1)
                tr_done = pe.v
                P.wait("dve", pe, tr_done)
                P.do("dve", lambda E, slot=slot: E.tensor_reduce(
                    cols_v[:, 0:1], PY[:, slot, 0:8], AX.X, ALU.max), dve, 1)
                if L == 1:
                    P.do("dve", lambda E, slot=slot: E.tensor_reduce(
                        cols_v[:, 1:2], PY[:, slot, 0:8], AX.X, ALU.max,
                        negate=True), dve, 1)
                P.do("dve", lambda E, slot=slot: E.tensor_scalar(
                    ohtm[:, :], PY[:, slot, 0:8], cols_v[:, 0:1], None,
                    ALU.is_equal), dve, 1)
                oh_done = dve.v
                py_use[slot]["dve"] = dve.v
                if L == 0:
                    P.wait("act", dve, oh_done)
                    P.do("act", lambda E, ct=ct: E.activation(
                        gcolb[:, ct:ct + 1], cols_v[:, 0:1], AF.Sigmoid), act, 1)
                else:
                    P.wait("act", dve, oh_done)
                    P.do("act", lambda E, slot=slot: E.activation(
                        ex_v[:, :], PY[:, slot, 0:8], AF.Exp,
                        bias=cols_v[:, 1:2]), act, 1)
                    exd = act.v
                    py_use[slot]["act"] = act.v
                    P.wait("dve", act, exd)
                    P.do("dve", lambda E, ct=ct: E.tensor_reduce(
                        secol_v[:, ct:ct + 1], ex_v[:, :], AX.X, ALU.add), dve, 1)
                P.wait("pe", dve, oh_done)
                P.wait("pe", dve, pt_oh_use[0])
                P.do("pe", lambda E: E.transpose(pt_oh[:, :], ohtm[:, :], idb[:, :]), pe, 1)
                trb = pe.v
                P.wait("dve", pe, trb)
                P.do("dve", lambda E, ct=ct: E.tensor_copy(
                    ohfm_ct[0:8, ct, :], pt_oh[0:8, :]), dve, 1)
                pt_oh_use[0] = dve.v
        if L == 1:
            P.do("dve", lambda E: E.reciprocal(ginv_v[:, :], secol_v[:, :]), dve, 1)
            P.do("dve", lambda E: E.tensor_copy(gcolb[:, :], ginv_v[:, :]), dve, 1)
        g_done = dve.v
        P.wait("pe", dve, g_done)
        P.wait("pe", act)
        grow_pe_use = [0]
        for tt in range(NTT):
            for ci in range(4):
                ct = tt * 4 + ci
                P.wait("pe", dve, ptrow_use[0])
                P.do("pe", lambda E, ct=ct: E.transpose(
                    ptrow[:, :], gcolb[:, ct:ct + 1], idb[:, :]), pe, 1)
                trg = pe.v
                P.wait("dve", pe, trg)
                P.wait("dve", pe, grow_pe_use[0])
                P.do("dve", lambda E, ci=ci: E.tensor_copy(
                    grow_sm[0:1, ci * 128:(ci + 1) * 128], ptrow[0:1, :]), dve, 1)
                ptrow_use[0] = dve.v
            gr = dve.v
            P.wait("pe", dve, gr)
            pe_claim(ps1_use)
            P.do("pe", lambda E: E.matmul(PS1[:, :], conesrow[:, :], grow_sm[0:1, :],
                                          start=True, stop=True), pe, 1)
            gb = pe.v
            grow_pe_use[0] = pe.v
            P.wait("dve", pe, gb)
            P.do("dve", lambda E, tt=tt: E.tensor_copy(gateb_v[:, TTS[tt]], PS1[:, :]),
                 dve, 1)
            ps1_use["dve"] = dve.v
        if L == 1:
            emit_ln(xn, xn, affine=True)
        oh_ready = dve.v
        w1_dr, w2_dr = dr[f"w_l{L}w1"], dr[f"w_l{L}w2"]
        w1_pe_use = [pe.v, pe.v]
        w2_pe_use = [pe.v, pe.v]
        w1n, w2n = [0], [0]
        for e in range(NE):
            P.wait("pe", dve, oh_ready)
            for tt in range(NTT):
                pe_claim(ps1_use)
                P.do("pe", lambda E, e=e, tt=tt: E.matmul(
                    PS1[:, :], eones[:, e, :], ohfm_v[0:8, TTS[tt]],
                    start=True, stop=True), pe, 1)
                gm = pe.v
                P.wait("dve", pe, gm)
                P.do("dve", lambda E, tt=tt: E.tensor_tensor(
                    gmask_v[:, TTS[tt]], PS1[:, :], gateb_v[:, TTS[tt]], ALU.mult),
                    dve, 1)
                ps1_use["dve"] = dve.v
            gmask_done = dve.v
            for fq in range(NFQ):
                for fc8 in range(8):
                    fc = fq * 8 + fc8
                    par = w1n[0] % 2
                    P.wait("pool", pe, w1_pe_use[par])
                    P.do("pool", lambda E, e=e, fc=fc, par=par: E.dma_start(
                        out=w1c_v[:, par, :, :], in_=w1_dr[e, fc, :, :, :]),
                        w1s[par], 16)
                    w1n[0] += 1
                    P.wait("pe", w1s[par])
                    for tt in range(NTT):
                        pe_claim(ph_use[tt])
                    for kd in range(KD):
                        for tt in range(NTT):
                            P.do("pe", lambda E, kd=kd, tt=tt, par=par: E.matmul(
                                PH[:, tt, :], w1c_v[:, par, kd, :], xn[:, kd, TTS[tt]],
                                start=(kd == 0), stop=(kd == KD - 1)),
                                pe if (kd == KD - 1 and tt == NTT - 1) else None, 1)
                    w1_pe_use[par] = pe.v
                    p1_done = pe.v
                    P.wait("act", pe, p1_done)
                    for tt in range(NTT):
                        if L == 0:
                            P.do("act", lambda E, fc8=fc8, tt=tt: E.activation(
                                hq_v[:, fc8, TTS[tt]], PH[:, tt, :], AF.Gelu), act, 1)
                        else:
                            P.do("act", lambda E, fc8=fc8, tt=tt, e=e, fc=fc: E.activation(
                                hq_v[:, fc8, TTS[tt]], PH[:, tt, :], AF.Relu,
                                bias=b1sb[:, e, fc:fc + 1]), act, 1)
                        ph_use[tt]["act"] = act.v
                h_done = act.v
                for dout in range(8):
                    par = w2n[0] % 2
                    P.wait("pool", pe, w2_pe_use[par])
                    P.do("pool", lambda E, e=e, fq=fq, dout=dout, par=par: E.dma_start(
                        out=w2c_v[:, par, :, :], in_=w2_dr[e, fq, dout, :, :, :]),
                        w2s[par], 16)
                    w2n[0] += 1
                    P.wait("pe", w2s[par])
                    P.wait("pe", act, h_done)
                    if dout == 0 and fq == 0:
                        P.wait("pe", dve, gmask_done)
                    for tt in range(NTT):
                        pe_claim(ph_use[tt])
                    for fc8 in range(8):
                        for tt in range(NTT):
                            P.do("pe", lambda E, fc8=fc8, tt=tt, par=par: E.matmul(
                                PH[:, tt, :], w2c_v[:, par, fc8, :],
                                hq_v[:, fc8, TTS[tt]],
                                start=(fc8 == 0), stop=(fc8 == 7)),
                                pe if (fc8 == 7 and tt == NTT - 1) else None, 1)
                    w2_pe_use[par] = pe.v
                    p2_done = pe.v
                    P.wait("dve", pe, p2_done)
                    for tt in range(NTT):
                        if fq == 0:
                            P.do("dve", lambda E, dout=dout, tt=tt: E.tensor_copy(
                                yac_v[:, dout, TTS[tt]], PH[:, tt, :]), dve, 1)
                        elif fq < 3 or L == 0:
                            P.do("dve", lambda E, dout=dout, tt=tt: E.tensor_tensor(
                                yac_v[:, dout, TTS[tt]], yac_v[:, dout, TTS[tt]],
                                PH[:, tt, :], ALU.add), dve, 1)
                        else:
                            P.do("dve", lambda E, dout=dout, tt=tt, e=e:
                                 E.scalar_tensor_tensor(
                                     yac_v[:, dout, TTS[tt]], PH[:, tt, :],
                                     b2sb[:, e, dout:dout + 1], yac_v[:, dout, TTS[tt]],
                                     ALU.add, ALU.add), dve, 1)
                        ph_use[tt]["dve"] = dve.v
            for dout in range(8):
                for tt in range(NTT):
                    sl = tt % 2
                    P.do("dve", lambda E, dout=dout, tt=tt, sl=sl: E.tensor_tensor(
                        sqt[:, sl, :], yac_v[:, dout, TTS[tt]], gmask_v[:, TTS[tt]],
                        ALU.mult), dve, 1)
                    P.do("dve", lambda E, dout=dout, tt=tt, sl=sl: E.tensor_tensor(
                        xsb[:, dout, TTS[tt]], xsb[:, dout, TTS[tt]], sqt[:, sl, :],
                        ALU.add), dve, 1)

    def emit_l1a():
        if L1A_TRUNC < 0:
            return
        emit_ln(xsb, xn)
        if L1A_TRUNC < 1:
            return
        P.do("dve", lambda E: E.memset(Sf_v, 0.0), dve, 1)
        P.do("dve", lambda E: E.memset(sbf_v, 0.0), dve, 1)
        P.do("dve", lambda E: E.memset(kds_v, 0.0), dve, 1)
        for tt in range(NTT):
            ts = TTS[tt]
            pe_claim(ps1_use)
            for kd in range(KD):
                P.do("pe", lambda E, kd=kd, ts=ts: E.matmul(
                    PS1[0:16, :], wfTsb[:, kd, :], xn[:, kd, ts],
                    start=(kd == 0), stop=(kd == KD - 1)),
                    pe if kd == KD - 1 else None, 1)
            f_done = pe.v
            P.wait("dve", pe, f_done)
            P.do("dve", lambda E: E.tensor_scalar(st16[0:16, 1, :], PS1[0:16, :],
                                                  dlt[:, 0:1], 0.0, ALU.add, ALU.min),
                 dve, 1)
            min_done = dve.v
            ps1_use["dve"] = dve.v
            P.wait("act", pe, f_done)
            P.wait("act", dve, min_done)
            P.do("act", lambda E: E.activation(st16[0:16, 0, :], PS1[0:16, :], AF.Abs,
                                               bias=dlt[:, 0:1]), act, 1)
            P.do("act", lambda E: E.activation(st16[0:16, 0, :], st16[0:16, 0, :],
                                               AF.Exp, scale=-1.0), act, 1)
            P.do("act", lambda E: E.activation(st16[0:16, 0, :], st16[0:16, 0, :],
                                               AF.Ln, bias=fbc[0:16, 2:3]), act, 1)
            t3_done = act.v
            ps1_use["act"] = t3_done
            P.wait("dve", act, t3_done)
            P.do("dve", lambda E: E.tensor_tensor(st16[0:16, 1, :], st16[0:16, 1, :],
                                                  st16[0:16, 0, :], ALU.subtract), dve, 1)
            for ci in range(4):
                c0 = ci * 128
                P.do("dve", lambda E, c0=c0, ts=ts: E.tensor_tensor_scan(
                    lbbf_v[0:16, ts.start + c0:ts.start + c0 + 128],
                    st16[0:16, 1, c0:c0 + 128], st16[0:16, 1, c0:c0 + 128],
                    0.0, ALU.add, ALU.bypass), dve, 1)
        lb_done = dve.v
        if L1A_TRUNC < 2:
            return
        P.wait("act", dve, lb_done)
        P.do("act", lambda E: E.activation(elbbf_v[0:16, :], lbbf_v[0:16, :], AF.Exp),
             act, 1)
        P.do("dve", lambda E: E.tensor_copy(totf_v[0:16, :], lbbf_v[0:16, 127::128]),
             dve, 1)
        tot_done = dve.v
        P.wait("act", dve, tot_done)
        P.do("act", lambda E: E.activation(etot_bf[0:16, :], totf_v[0:16, :], AF.Exp),
             act, 1)
        etot_done = act.v
        for c in range(16):
            P.do("dve", lambda E, c=c: E.tensor_scalar(
                sfx_v[0:16, c * 128:(c + 1) * 128], lbbf_v[0:16, c * 128:(c + 1) * 128],
                totf_v[0:16, c:c + 1], -1.0, ALU.subtract, ALU.mult), dve, 1)
        sfx_done = dve.v
        P.wait("pe", act, etot_done)
        for kd in range(KD):
            bk = kd % 2
            pe_claim(py_use[bk])
            P.do("pe", lambda E, kd=kd, bk=bk: E.matmul(
                PY[:, bk, 0:16], selb[:, kd, :], etot_bf[0:16, :],
                start=True, stop=True), pe, 1)
            bc = pe.v
            P.wait("dve", pe, bc)
            P.do("dve", lambda E, kd=kd, bk=bk: E.tensor_copy(
                etotbc_v[:, kd, :], PY[:, bk, 0:16]), dve, 1)
            py_use[bk]["dve"] = dve.v
        P.wait("pe", dve, sfx_done)
        for c in range(16):
            P.wait("pe", act)
            P.do("pe", lambda E, c=c: E.transpose(
                pt_lb[:, :], lbbf_v[0:16, c * 128:(c + 1) * 128], idb[0:16, 0:16]), pe, 1)
            t1 = pe.v
            P.wait("act", pe, t1)
            P.do("act", lambda E, c=c: E.activation(enltm_v[:, c, :], pt_lb[:, :],
                                                    AF.Exp, scale=-1.0), act, 1)
            en_done = act.v
            P.wait("pe", act, en_done)
            P.do("pe", lambda E, c=c: E.transpose(
                pt_lb[:, :], sfx_v[0:16, c * 128:(c + 1) * 128], idb[0:16, 0:16]), pe, 1)
            t2 = pe.v
            P.wait("act", pe, t2)
            P.do("act", lambda E, c=c: E.activation(ktab_v[:, c, :], pt_lb[:, :],
                                                    AF.Exp), act, 1)
        tables_act = act.v
        if L1A_TRUNC < 3:
            return

        wq_dr, wk_dr, wv_dr, wo_dr = (dr["w_l1wq"], dr["w_l1wk"], dr["w_l1wv"],
                                      dr["w_l1wo"])
        ws_pe_use = [pe.v, pe.v]
        wsn = [0]

        def stream_w(src):
            par = wsn[0] % 2
            P.wait("pool", pe, ws_pe_use[par])
            P.do("pool", lambda E, src=src, par=par: E.dma_start(
                out=wstream_v[:, par, :, :], in_=_fs(src)), wss[par], 16)
            wsn[0] += 1
            return par, wss[par].v

        def pA(h):
            return PH[:, h // 4, (h % 4) * 128:(h % 4) * 128 + 128]

        def pO(h):
            half, r = 64 * (h % 2), h // 2
            return PY[half:half + 64, r // 4, (r % 4) * 128:(r % 4) * 128 + 128]

        def pS(h):
            half, r = 64 * (h % 2), h // 2
            return PS1[half:half + 64, r * 64:r * 64 + 64]

        def hsl(h):
            return slice(64 * (h % 2), 64 * (h % 2) + 64)

        first_wave = True
        for w in range(NW):
            wts = slice(w * NT, (w + 1) * NT)
            par, wv_ = stream_w(wq_dr)
            P.wait("pe", wss[par], wv_)
            for ec in range(KD):
                bk = ec % 4
                pe_claim(ph_use[bk])
                for kd in range(KD):
                    P.do("pe", lambda E, ec=ec, kd=kd, bk=bk, par=par, wts=wts: E.matmul(
                        PH[:, bk, :], wstream_v[:, par, kd, ec * 128:(ec + 1) * 128],
                        xn[:, kd, wts], start=(kd == 0), stop=(kd == KD - 1)),
                        pe if kd == KD - 1 else None, 1)
                qd = pe.v
                P.wait("act", pe, qd)
                P.do("act", lambda E, ec=ec, bk=bk: E.activation(
                    qfm_v[:, ec, :], PH[:, bk, :], AF.Silu), act, 1)
                ph_use[bk]["act"] = act.v
            ws_pe_use[par] = pe.v
            par, wv_ = stream_w(wk_dr)
            P.wait("pe", wss[par], wv_)
            for ec in range(KD):
                bk = ec % 4
                pe_claim(ph_use[bk])
                for kd in range(KD):
                    P.do("pe", lambda E, ec=ec, kd=kd, bk=bk, par=par, wts=wts: E.matmul(
                        PH[:, bk, :], wstream_v[:, par, kd, ec * 128:(ec + 1) * 128],
                        xn[:, kd, wts], start=(kd == 0), stop=(kd == KD - 1)),
                        pe if kd == KD - 1 else None, 1)
                kdn = pe.v
                P.wait("act", pe, kdn)
                P.do("act", lambda E, ec=ec, bk=bk: E.activation(
                    kfm_v[:, ec, :], PH[:, bk, :], AF.Silu), act, 1)
                ph_use[bk]["act"] = act.v
                kf_done = act.v
                P.wait("dve", act, kf_done)
                P.do("dve", lambda E, ec=ec: E.tensor_scalar(
                    kmsk_v[:, 0, ec, :], kfm_v[:, ec, :], fbc[:, 4:5], None,
                    ALU.mult), dve, 1)
                P.do("dve", lambda E, ec=ec: E.tensor_scalar(
                    kmsk_v[:, 1, ec, :], kfm_v[:, ec, :], fbc[:, 5:6], None,
                    ALU.mult), dve, 1)
            for ct in range(CPW):
                c0 = wts.start + ct * 128
                for eh in range(2):
                    bk = (ct * 2 + eh) % 4
                    pe_claim(ph_use[bk])
                    for kd in range(KD):
                        P.do("pe", lambda E, kd=kd, bk=bk, par=par, c0=c0, eh=eh:
                             E.matmul(PH[:, bk, :], xn[:, kd, c0:c0 + 128],
                                      wstream_v[:, par, kd, eh * 512:(eh + 1) * 512],
                                      start=(kd == 0), stop=(kd == KD - 1)),
                             pe if kd == KD - 1 else None, 1)
                    ktd = pe.v
                    P.wait("act", pe, ktd)
                    P.do("act", lambda E, ct=ct, eh=eh, bk=bk: E.activation(
                        ktm_v[:, ct, eh * 512:(eh + 1) * 512], PH[:, bk, :], AF.Silu),
                        act, 1)
                    ph_use[bk]["act"] = act.v
            ws_pe_use[par] = pe.v
            par, wv_ = stream_w(wv_dr)
            P.wait("pe", wss[par], wv_)
            for ct in range(CPW):
                c0 = wts.start + ct * 128
                for eh in range(2):
                    bk = (ct * 2 + eh) % 4
                    pe_claim(ph_use[bk])
                    for kd in range(KD):
                        P.do("pe", lambda E, kd=kd, bk=bk, par=par, c0=c0, eh=eh:
                             E.matmul(PH[:, bk, :], xn[:, kd, c0:c0 + 128],
                                      wstream_v[:, par, kd, eh * 512:(eh + 1) * 512],
                                      start=(kd == 0), stop=(kd == KD - 1)),
                             pe if kd == KD - 1 else None, 1)
                    vtd = pe.v
                    P.wait("dve", pe, vtd)
                    P.do("dve", lambda E, ct=ct, eh=eh, bk=bk: E.tensor_copy(
                        vtm_v[:, ct, eh * 512:(eh + 1) * 512], PH[:, bk, :]), dve, 1)
                    ph_use[bk]["dve"] = dve.v
            ws_pe_use[par] = pe.v
            proj_act = act.v
            proj_dve = dve.v
            if L1A_TRUNC < 4:
                continue
            def pA(k):
                return PH[:, k // 4, (k % 4) * 128:(k % 4) * 128 + 128]

            def pO(k):
                return PY[:, k // 4, (k % 4) * 128:(k % 4) * 128 + 128]

            def pS(k):
                return PS1[:, k * 64:k * 64 + 64]

            def hsl(h):
                return slice(64 * (h % 2), 64 * (h % 2) + 64)

            if first_wave:
                P.wait("dve", act, tables_act)
                first_wave = False
            import os as _o8
            _scan_stop = int(_o8.environ.get("SCAN_STOP", "99"))
            for cl in range(CPW):
                c = w * CPW + cl
                if c > _scan_stop:
                    continue
                csl = slice(cl * 128, (cl + 1) * 128)
                P.wait("pe", act, proj_act)
                P.wait("pe", dve, dve.v)
                for kd in range(KD):
                    if c > int(_o8.environ.get("SKIP_ELB_AFTER", "99")):
                        continue
                    bk = kd % 2
                    P.wait("pe", dve)
                    P.do("pe", lambda E, kd=kd, bk=bk, csl=csl, w=w: E.matmul(
                        PH[:, bk, 0:128], selb[:, kd, :],
                        elbbf_v[0:16, w * NT + cl * 128:w * NT + cl * 128 + 128]
                        if False else elbbf_v[0:16, csl.start + w * NT:csl.stop + w * NT],
                        start=True, stop=True), pe, 1)
                    eb = pe.v
                    P.wait("dve", pe, eb)
                    P.do("dve", lambda E, kd=kd, bk=bk: E.tensor_copy(
                        elbbc_v[:, kd, :], PH[:, bk, 0:128]), dve, 1)
                for hg in range(2):
                    hb = hg * 8
                    P.wait("pe", dve, dve.v)
                    for k in range(8):
                        h = hb + k
                        P.do("pe", lambda E, h=h, k=k, hg=hg, csl=csl: E.matmul(
                            pA(hg * 8 + k), kmsk_v[:, h % 2, h // 2, csl],
                            qfm_v[:, h // 2, csl], start=True, stop=True),
                            pe if k == 7 else None, 1)
                    mma_done = pe.v
                    P.wait("dve", pe, mma_done)
                    for k in range(8):
                        h = hb + k
                        if c > int(_o8.environ.get("SKIP_A2_AFTER", "99")):
                            continue
                        P.do("dve", lambda E, h=h, k=k, hg=hg, c=c: E.scalar_tensor_tensor(
                            a2_v[:, h, :], pA(hg * 8 + k), enltm_v[:, c, h:h + 1],
                            maskt[:, :], ALU.mult, ALU.mult), dve, 1)
                    a2_done = dve.v
                    P.wait("pe", dve, a2_done)
                    for k in range(8):
                        h = hb + k
                        P.do("pe", lambda E, h=h, k=k, cl=cl: E.matmul(
                            pO(k), vtm_v[:, cl, (h // 2) * 128:(h // 2) * 128 + 128],
                            a2_v[:, h, :], start=True, stop=True),
                            pe if k == 7 else None, 1)
                    mmoi_done = pe.v
                    for pr in range(hb // 2, hb // 2 + 4):
                        P.do("dve", lambda E, pr=pr, csl=csl: E.tensor_tensor(
                            qs_v[:, pr % 4, :], qfm_v[:, pr, csl],
                            elbbc_v[:, pr, :], ALU.mult), dve, 1)
                    qs_done = dve.v
                    P.wait("pe", dve, qs_done)
                    for k in range(8):
                        h = hb + k
                        P.do("pe", lambda E, h=h, k=k, hg=hg: E.matmul(
                            pA(hg * 8 + k), sbf_v[:, h, :], qs_v[:, (h // 2) % 4, :],
                            start=True, stop=True), pe if k == 7 else None, 1)
                    mmoe_done = pe.v
                    P.wait("dve", pe, mmoe_done)
                    for k in range(8):
                        h = hb + k
                        P.do("dve", lambda E, h=h, k=k, csl=csl: E.tensor_tensor(
                            obf_v[hsl(h), h // 2, csl], pO(k)[hsl(h), :],
                            elbbc_v[hsl(h), h // 2, :], ALU.mult), dve, 1)
                    for k in range(8):
                        h = hb + k
                        P.do("dve", lambda E, h=h, k=k, hg=hg, csl=csl: E.tensor_tensor(
                            obf_v[hsl(h), h // 2, csl], obf_v[hsl(h), h // 2, csl],
                            pA(hg * 8 + k)[hsl(h), :], ALU.add), dve, 1)
                    for k in range(8):
                        h = hb + k
                        if c > int(_o8.environ.get("SKIP_KDS_AFTER", "99")):
                            continue
                        P.do("dve", lambda E, h=h, cl=cl, c=c: E.tensor_scalar(
                            kds_v[:, h // 2 % 4, hsl(h)],
                            ktm_v[:, cl, h * 64:h * 64 + 64],
                            ktab_v[:, c, h:h + 1], None, ALU.mult), dve, 1)
                    kds_done = dve.v
                    P.wait("pe", dve, kds_done)
                    for k in range(8):
                        h = hb + k
                        P.do("pe", lambda E, h=h, k=k, cl=cl: E.matmul(
                            pS(k), kds_v[:, h // 2 % 4, :],
                            vtm_v[:, cl, h * 64:h * 64 + 64],
                            start=True, stop=True), pe if k == 7 else None, 1)
                    mms_done = pe.v
                    P.wait("dve", pe, mms_done)
                    for k in range(8):
                        h = hb + k
                        if c > int(_o8.environ.get("SKIP_SF_AFTER", "99")):
                            continue
                        P.do("dve", lambda E, h=h, k=k, c=c: E.scalar_tensor_tensor(
                            Sf_v[hsl(h), h // 2, :], Sf_v[hsl(h), h // 2, :],
                            etotbc_v[hsl(h), h // 2, c:c + 1], pS(k)[hsl(h), :],
                            ALU.mult, ALU.add), dve, 1)
                    for k in range(8):
                        h = hb + k
                        if c > int(_o8.environ.get("SKIP_SBF_AFTER", "99")):
                            continue
                        P.do("dve", lambda E, h=h: E.tensor_copy(
                            sbf_v[hsl(h), h, hsl(h)], Sf_v[hsl(h), h // 2, :]), dve, 1)
                for i in range(4):
                    ph_use[i]["dve"] = dve.v
                py_use[0]["dve"] = py_use[1]["dve"] = dve.v
                ps1_use["dve"] = dve.v
            if L1A_TRUNC < 5:
                continue
            scan_done = dve.v
            sq_pe = [0, 0]
            pe_claim(ps1_use)
            P.wait("act", dve, scan_done)
            for kd in range(KD):
                par2 = kd % 2
                P.wait("act", pe, sq_pe[par2])
                P.do("act", lambda E, kd=kd, par2=par2: E.activation(
                    sqt[:, par2, :], obf_v[:, kd, :], AF.Square), act, 1)
                sqd = act.v
                P.wait("pe", act, sqd)
                P.do("pe", lambda E, par2=par2, kd=kd: E.matmul(
                    PS1[0:16, :], cones[:, :], sqt[:, par2, :],
                    start=(kd == 0), stop=(kd == KD - 1)), pe, 1)
                sq_pe[par2] = pe.v
            oss_done = pe.v
            P.wait("act", pe, oss_done)
            P.do("act", lambda E: E.activation(st16[0:16, 0, :], PS1[0:16, :], AF.Copy,
                                               scale=1.0 / 1024.0), act, 1)
            ps1_use["act"] = act.v
            P.do("act", lambda E: E.activation(st16[0:16, 0, :], st16[0:16, 0, :],
                                               AF.Ln, bias=fbc[0:16, 1:2]), act, 1)
            P.wait("act", pe, rows_war[0])
            P.do("act", lambda E: E.activation(rows_sm[0:1, 1, :], st16[0:1, 0, :],
                                               AF.Exp, scale=-0.5), act, 1)
            orstd_done = act.v
            P.wait("pe", act, orstd_done)
            pe_claim(py_use[0])
            P.do("pe", lambda E: E.matmul(PY[:, 0, :], conesrow_f[:, :],
                                          rows_sm[0:1, 1, :], start=True, stop=True),
                 pe, 1)
            ob = pe.v
            rows_war[0] = pe.v
            P.wait("dve", pe, ob)
            import os as _o9
            for kd in range(KD):
                if _o9.environ.get("SKIP_RMS") == "1":
                    continue
                P.do("dve", lambda E, kd=kd: E.scalar_tensor_tensor(
                    obf_v[:, kd, :], PY[:, 0, :], nwsb[:, kd:kd + 1], obf_v[:, kd, :],
                    ALU.mult, ALU.mult), dve, 1)
            py_use[0]["dve"] = dve.v
            on_done = dve.v
            par, wv_ = stream_w(wo_dr)
            P.wait("pe", wss[par], wv_)
            P.wait("pe", dve, on_done)
            for ec in range(KD):
                bk = ec % 4
                pe_claim(ph_use[bk])
                for kd in range(KD):
                    P.do("pe", lambda E, ec=ec, kd=kd, bk=bk, par=par: E.matmul(
                        PH[:, bk, :], wstream_v[:, par, kd, ec * 128:(ec + 1) * 128],
                        obf_v[:, kd, :], start=(kd == 0), stop=(kd == KD - 1)),
                        pe if kd == KD - 1 else None, 1)
                od = pe.v
                P.wait("dve", pe, od)
                P.do("dve", lambda E, ec=ec, bk=bk, wts=wts: E.tensor_tensor(
                    xsb[:, ec, wts], xsb[:, ec, wts], PH[:, bk, :], ALU.add), dve, 1)
                ph_use[bk]["dve"] = dve.v
            ws_pe_use[par] = pe.v

    emits = {"l0a": emit_l0a, "l0a_scores": emit_l0a, "l0a_f": emit_l0a_fused,
             "l0f": lambda: emit_ffn(0), "l1f": lambda: emit_ffn(1), "l1a": emit_l1a}
    for b in blocks:
        emits[b]()
        dump("x_" + b, xsb[:, :, :], [128, KD, T], BF16)

    P.wait("pool", pe)
    P.wait("pool", dve)
    P.wait("pool", act)
    P.do("pool", lambda E: E.dma_start(
        out=xout.rearrange("(kd p) t -> p kd t", p=128), in_=xsb[:, :, :]), ode, 16)
    P.wait("pool", ode)
    return P, dumps_made



def _consts():
    c = {}
    c["c_ones"] = np.ones((128, 16), np.float32)
    c["c_onesrow"] = np.ones((1, 128), np.float32)
    sel = np.zeros((16, 8, 128), np.float32)
    for kd in range(8):
        sel[2 * kd, kd, 0:64] = 1.0
        sel[2 * kd + 1, kd, 64:128] = 1.0
    c["c_sel"] = sel
    hs = np.zeros((128, 8, 16), np.float32)
    for kd in range(8):
        for p in range(128):
            hs[p, kd, 2 * kd + (p >= 64)] = 1.0
    c["c_headsel"] = hs
    ss = np.zeros((128, 16), np.float32)
    for p in range(128):
        ss[p, p % 16] = 1.0
    c["c_ss"] = ss
    eo = np.zeros((8, 8, 128), np.float32)
    for e in range(8):
        eo[e, e, :] = 1.0
    c["c_eones"] = eo
    c["c_id"] = np.eye(128, dtype=np.float32)
    m = np.zeros((128, 128), np.float32)
    for j in range(128):
        m[j, j:] = 1.0
    c["c_mask"] = m
    fb = np.zeros((128, 6), np.float32)
    fb[:, 0] = 1e-5
    fb[:, 1] = 1e-6
    fb[:, 2] = 1.0
    fb[0:64, 4] = 1.0
    fb[64:128, 5] = 1.0
    c["c_fb"] = fb
    return c


def _stage_weights(inp):
    st = {}
    A = np.ascontiguousarray

    def w1tile(W1):
        return A(W1.reshape(8, 8, 128, 32, 128).transpose(0, 3, 2, 1, 4))

    def w2tile(W2):
        return A(W2.reshape(8, 4, 8, 128, 8, 128).transpose(0, 1, 4, 3, 2, 5))

    blk = np.zeros((128, 3, 128), np.float32)
    for i, nm in enumerate(("L0_attn_Wq", "L0_attn_Wk", "L0_attn_Wv")):
        Wt = np.asarray(inp[nm], np.float32).T
        blk[0:64, i, 0:64] = Wt
        blk[64:128, i, 64:128] = Wt
    st["w_l0blk"] = blk
    qbd = np.zeros((128, 128), np.float32)
    kbd = np.zeros((128, 128), np.float32)
    for bb in range(2):
        sl = slice(64 * bb, 64 * bb + 64)
        qbd[sl, sl] = np.asarray(inp["L0_attn_Wq"], np.float32)
        kbd[sl, sl] = np.asarray(inp["L0_attn_Wk"], np.float32)
    st["w_l0qbd"] = qbd
    st["w_l0kbd"] = kbd
    st["w_l0wr"] = A(np.asarray(inp["L0_ffn_Wr"], np.float32).reshape(8, 128, 8)
                     .transpose(1, 0, 2))
    st["w_l0w1"] = w1tile(np.asarray(inp["L0_ffn_W1"], np.float32))
    st["w_l0w2"] = w2tile(np.asarray(inp["L0_ffn_W2"], np.float32))
    st["p_br0"] = np.asarray(inp["L0_ffn_br"], np.float32).reshape(8, 1)
    for nm, key in (("Wq", "wq"), ("Wk", "wk"), ("Wv", "wv"), ("Wo", "wo")):
        Wt = np.asarray(inp["L1_attn_" + nm], np.float32).T
        st["w_l1" + key] = A(Wt.reshape(8, 128, 1024).transpose(1, 0, 2))
    st["w_l1wf"] = A(np.asarray(inp["L1_attn_Wf"], np.float32).T
                     .reshape(8, 128, 16).transpose(1, 0, 2))
    st["p_delta"] = np.asarray(inp["L1_attn_delta"], np.float32).reshape(16, 1)
    st["p_nw"] = A(np.asarray(inp["L1_attn_norm_weight"], np.float32).reshape(8, 128).T)
    st["w_l1wr"] = A(np.asarray(inp["L1_ffn_Wr"], np.float32).reshape(8, 128, 8)
                     .transpose(1, 0, 2))
    st["w_l1w1"] = w1tile(np.asarray(inp["L1_ffn_W1"], np.float32))
    st["w_l1w2"] = w2tile(np.asarray(inp["L1_ffn_W2"], np.float32))
    st["p_br1"] = np.asarray(inp["L1_ffn_br"], np.float32).reshape(8, 1)
    st["p_b1"] = A(np.asarray(inp["L1_ffn_b1"], np.float32).reshape(8, 32, 128)
                   .transpose(2, 0, 1))
    st["p_b2"] = A(np.asarray(inp["L1_ffn_b2"], np.float32).reshape(8, 8, 128)
                   .transpose(2, 0, 1))
    st["p_lng"] = A(np.stack(
        [np.asarray(inp["L1_ffn_ln_g"], np.float32).reshape(8, 128).T,
         np.asarray(inp["L1_ffn_ln_b"], np.float32).reshape(8, 128).T], axis=1))
    return st


_PROG_CACHE = {}


def _get_prog(blocks, dump_names=()):
    key = (tuple(blocks), tuple(dump_names), L1A_TRUNC, LN_TRUNC)
    if key not in _PROG_CACHE:
        P, dumps = build_program(blocks, dump_names)
        nc = P.build()
        _PROG_CACHE[key] = (nc, dumps)
    return _PROG_CACHE[key]


def run_blocks(inp, blocks, xcur=None, dump_names=(), s_all=None):
    nc, dumps = _get_prog(tuple(blocks), tuple(dump_names))
    st = _stage_weights(inp)
    st.update(_consts())
    x = np.asarray(xcur if xcur is not None else inp["L0_attn_x"], np.float32)
    xall_fm = (np.ascontiguousarray(x.transpose(0, 2, 1))
               if any(b == "l0a_f" for b in blocks) else None)
    in_maps = []
    for ci in range(8):
        m = dict(st)
        m["xin"] = np.ascontiguousarray(x[ci].T)
        if "l0a" in blocks or "l0a_scores" in blocks or "l0a_f" in blocks:
            sel = np.zeros((128, 16), np.float32)
            for h in range(16):
                sel[16 * ci + h, h] = 1.0
            m["c_mysel"] = sel
        if "l0a_f" in blocks:
            mis = np.zeros((16, 8), np.float32)
            mis[:, ci] = 1.0
            m["c_misel"] = mis
            m["xall"] = xall_fm
        if s_all is not None:
            m["s_all"] = s_all
        in_maps.append(m)
    res = run_bass_kernel_spmd(nc, in_maps, core_ids=list(range(8)))
    keys = set()
    for r in res.results:
        keys |= set(r.keys())
    if "xout" in keys:
        out = np.stack([np.asarray(res.results[ci]["xout"]).T for ci in range(8)])
        out = out.astype(np.float32)
    else:
        out = None
    extra = {nm: [np.asarray(res.results[ci]["dump_" + nm]) for ci in range(8)]
             for nm in dumps}
    if "s_my" in keys:
        extra["s_my"] = [np.asarray(res.results[ci]["s_my"]) for ci in range(8)]
    return out, extra


def kernel(**inp):
    out, _ = run_blocks(inp, ("l0a_f", "l0f", "l1a", "l1f"))
    return out


# version 04
# speedup vs baseline: 1.4570x; 1.4570x over previous
import numpy as np
from contextlib import ExitStack

import concourse.bass as bass
import concourse.mybir as mybir
from concourse.bass_utils import run_bass_kernel_spmd

F32 = mybir.dt.float32
BF16 = mybir.dt.bfloat16
AF = mybir.ActivationFunctionType
ALU = mybir.AluOpType
AX = mybir.AxisListType

T = 2048
KD = 8
NT = 512
NTT = 4
NE = 8
NFC = 32
NFQ = 4
NW = 4
CPW = 4
L1A_TRUNC = 99
LN_TRUNC = 99


class Sem:
    __slots__ = ("h", "v")

    def __init__(self, h):
        self.h = h
        self.v = 0


class PB:
    ENG = {"pe": "tensor", "dve": "vector", "act": "scalar", "pool": "gpsimd", "sp": "sync"}

    def __init__(self):
        self.nc = bass.Bass("TRN2", target_bir_lowering=False)
        self.stk = ExitStack()
        self.q = {k: [] for k in self.ENG}
        self.sem = {}
        for s in ("pe", "dve", "act", "pool", "sp", "wde", "xde", "cc", "ode",
                  "w1a", "w1b", "w2a", "w2b", "wsa", "wsb", "sla", "slb", "wrs"):
            self.sem[s] = Sem(self.stk.enter_context(self.nc.semaphore(s)))

    def sb(self, name, shape, dt):
        return self.stk.enter_context(self.nc.sbuf_tensor(name, list(shape), dt))

    def ps(self, name, shape, dt=F32):
        return self.stk.enter_context(self.nc.psum_tensor(name, list(shape), dt))

    def wait(self, eng, sem, v=None):
        vv = sem.v if v is None else v
        if vv <= 0:
            return
        self.q[eng].append(lambda E, h=sem.h, vv=vv: E.wait_ge(h, vv))

    SELF_SER = ("dve", "act")

    def do(self, eng, fn, sem=None, k=1):
        if eng in self.SELF_SER:
            own = self.sem[eng]
            if own.v > 0:
                self.q[eng].append(lambda E, h=own.h, vv=own.v: E.wait_ge(h, vv))
            if sem is None:
                sem = own
        if sem is None:
            self.q[eng].append(fn)
        else:
            sem.v += k
            self.q[eng].append(lambda E, fn=fn, h=sem.h, k=k: fn(E).then_inc(h, k))

    def build(self):
        with self.nc.Block() as blk:
            for short, attr in self.ENG.items():
                getattr(blk, attr)(lambda E, lst=self.q[short]: [f(E) for f in lst])
        return self.nc


def _fs(ap):
    return ap[tuple([slice(None)] * len(ap.shape))]


def build_program(blocks=("l0a", "l0f", "l1a", "l1f"), dump_names=()):
    P = PB()
    S = P.sem
    pe, dve, act, pool, sp = S["pe"], S["dve"], S["act"], S["pool"], S["sp"]
    wde, xde, ode, ccs = S["wde"], S["xde"], S["ode"], S["cc"]
    w1s = [S["w1a"], S["w1b"]]
    w2s = [S["w2a"], S["w2b"]]
    wss = [S["wsa"], S["wsb"]]
    sls = [S["sla"], S["slb"]]
    need = set(blocks)
    dumps_made = []

    xin = P.nc.dram_tensor("xin", [1024, T], F32, kind="ExternalInput").ap()
    xout = P.nc.dram_tensor("xout", [1024, T], F32, kind="ExternalOutput").ap()
    dr = {}

    def din(nm, shape):
        dr[nm] = P.nc.dram_tensor(nm, list(shape), F32, kind="ExternalInput").ap()

    for nm, shp in [("c_ones", [128, 16]), ("c_onesrow", [1, 128]), ("c_sel", [16, 8, 128]),
                    ("c_headsel", [128, 8, 16]), ("c_ss", [128, 16]), ("c_eones", [8, 8, 128]),
                    ("c_id", [128, 128]), ("c_mask", [128, 128]),
                    ("c_fb", [128, 6])]:
        din(nm, shp)
    l0a_mode = ("scores" if "l0a_scores" in need else
                ("fused" if "l0a_f" in need else ("ext" if "l0a" in need else None)))
    if l0a_mode is not None:
        din("w_l0blk", [128, 3, 128])
        din("c_mysel", [128, 16])
        if l0a_mode == "scores":
            s_my = P.nc.dram_tensor("s_my", [16, T], F32, kind="ExternalOutput").ap()
        elif l0a_mode == "ext":
            din("s_all", [128, T])
            s_all = dr["s_all"]
        else:
            din("xall", [NE, 1024, T])
            din("w_l0qbd", [128, 128])
            din("w_l0kbd", [128, 128])
            din("c_misel", [16, 8])
    for L in (0, 1):
        if ("l0f" if L == 0 else "l1f") in need:
            din(f"w_l{L}wr", [128, 8, 8])
            din(f"w_l{L}w1", [NE, NFC, 128, 8, 128])
            din(f"w_l{L}w2", [NE, NFQ, 8, 128, 8, 128])
            din(f"p_br{L}", [8, 1])
    if "l1f" in need:
        din("p_b1", [128, 8, 32])
        din("p_b2", [128, 8, 8])
        din("p_lng", [128, 2, 8])
    if "l1a" in need:
        import os as _os0
        if _os0.environ.get("SKIP_WQKVO") != "1":
            for w in ("wq", "wk", "wv", "wo"):
                din("w_l1" + w, [128, 8, 1024])
        din("w_l1wf", [128, 8, 16])
        din("p_delta", [16, 1])
        din("p_nw", [128, 8])

    xsb = P.sb("xsb", [128, KD, T], BF16)
    xn = P.sb("xn", [128, KD, T], BF16)
    U = P.sb("U", [128, 40960], BF16)
    W = P.sb("W", [128, 16384], BF16)
    SCRF = P.sb("SCRF", [128, 4096], F32)
    rows_sm = P.sb("rows_sm", [128, 2, NT], F32)
    grow_sm = P.sb("grow_sm", [128, NT], BF16)
    sqt = P.sb("sqt", [128, 2, NT], BF16)
    cones = P.sb("cones", [128, 16], BF16)
    conesrow = P.sb("conesrow", [1, 128], BF16)
    conesrow_f = P.sb("conesrow_f", [1, 128], F32)
    selb = P.sb("selb", [16, 8, 128], BF16)
    headsb = P.sb("headsb", [128, 8, 16], BF16)
    ssb = P.sb("ssb", [128, 16], BF16)
    eones = P.sb("eones", [8, 8, 128], BF16)
    idb = P.sb("idb", [128, 128], BF16)
    idf = P.sb("idf", [128, 128], F32)
    maskt = P.sb("maskt", [128, 128], BF16)
    fbc = P.sb("fbc", [128, 6], F32)
    wrsb = P.sb("wrsb", [128, 8, 8], F32)
    brsb = P.sb("brsb", [8, 2], F32)
    lngb = P.sb("lngb", [128, 2, 8], F32)
    nwsb = P.sb("nwsb", [128, 8], F32)
    dlt = P.sb("dlt", [16, 1], F32)
    b1sb = P.sb("b1sb", [128, 8, 32], F32)
    b2sb = P.sb("b2sb", [128, 8, 8], F32)
    ohtm = P.sb("ohtm", [128, 8], BF16)
    gcolb = P.sb("gcolb", [128, 16], BF16)
    wfTsb = P.sb("wfTsb", [128, 8, 16], BF16)
    miselb = P.sb("miselb", [16, 8], F32)
    wcolf = P.sb("wcolf", [128, 1], F32)
    m11f = P.sb("m11f", [128, 1], F32)

    PH = P.ps("PH", [128, 4, 512])
    PY = P.ps("PY", [128, 2, 512])
    PS1 = P.ps("PS1", [128, 512])
    PSM = P.ps("PSM", [128, 1024], BF16)
    pt_oh = PSM[0:8, 0:128]
    pt_lb = PSM[:, 128:144]
    ptrow = PSM[0:1, 144:272]

    def view(base, off, shape):
        n = int(np.prod(shape))
        ap = base[:, off:off + n]
        if len(shape) > 1:
            names = " ".join(f"a{i}" for i in range(len(shape)))
            ap = ap.rearrange(f"p ({names}) -> p {names}",
                              **{f"a{i}": s for i, s in enumerate(shape)})
        return ap

    hq_v = view(U, 0, (8, T))
    yac_v = view(U, 8 * T, (8, T))
    gateb_v = view(U, 16 * T, (T,))
    gmask_v = view(U, 17 * T, (T,))
    ohfm_v = view(U, 18 * T, (T,))
    ohfm_ct = view(U, 18 * T, (16, 128))
    l0_q = view(U, 0, (NT,))
    l0_qk = view(U, NT, (NT,))
    l0_vs = view(U, 2 * NT, (NT,))
    l0_e = view(U, 3 * NT, (NT,))
    l0_sall = view(U, 4 * NT, (2, NT))
    l0_smine = view(U, 6 * NT, (T,))
    l0_probs = view(U, 6 * NT + T, (T,))
    l0_emine = view(U, 6 * NT + 2 * T, (2, NT))
    l0_inv = view(U, 6 * NT + 3 * T, (2, NT))
    l0_wblk = view(U, 6 * NT + 4 * T, (3, 128))
    f_xb = view(U, 16384, (2, 8, NT))
    f_z = view(U, 24576, (NT,))
    f_xz = view(U, 24576 + NT, (NT,))
    f_sq = view(U, 24576 + 2 * NT, (NT,))
    f_et = view(U, 24576 + 3 * NT, (NT,))
    f_em = view(U, 26624, (T,))
    f_mblk = view(U, 28672, (128,))
    f_mblkT = view(U, 28672 + 128, (128,))
    f_qbd = view(U, 28672 + 256, (128,))
    f_kbd = view(U, 28672 + 384, (128,))
    f_wsel = view(U, 28672 + 512, (8, 16))
    f_m2x = view(U, 28672 + 640, (16,))
    qfm_v = view(U, 0, (8, NT))
    kfm_v = view(U, 4096, (8, NT))
    ktm_v = view(U, 8192, (CPW, 1024))
    vtm_v = view(U, 12288, (CPW, 1024))
    obf_v = view(U, 16384, (8, NT))
    a2_v = view(U, 20480, (16, 128))
    kds_v = view(U, 22528, (4, 128))
    elbbc_v = view(U, 23040, (8, 128))
    lbbf_v = view(U, 24064, (T,))
    elbbf_v = view(U, 24064 + T, (T,))
    sbf_v = view(U, 24064 + 2 * T, (16, 128))
    etot_bf = view(U, 24064 + 2 * T + 2048, (16,))
    sfx_v = view(U, 24064 + 2 * T + 2048 + 16, (T,))
    kmsk_v = view(U, 24064 + 3 * T + 2048 + 16, (2, 8, NT))
    qs_v = view(U, 24064 + 2 * T + 2048 + 16, (4, 128))
    w1c_v = view(W, 0, (2, 8, 128))
    w2c_v = view(W, 2048, (2, 8, 128))
    wstream_v = view(W, 0, (2, 8, 1024))
    stg32 = view(SCRF, 0, (2, NT))
    st16 = view(SCRF, 1024, (2, NT))
    lftt_v = view(SCRF, 2048, (2, NT))
    cols_v = view(SCRF, 3072, (8,))
    ex_v = view(SCRF, 3072 + 8, (8,))
    secol_v = view(SCRF, 3072 + 32, (16,))
    vsump_v = view(SCRF, 3072 + 48, (KD, NTT))
    vsum8_v = view(SCRF, 3072 + 80, (KD,))
    vsc2_v = view(SCRF, 3072 + 88, (KD,))
    ginv_v = view(SCRF, 3072 + 96, (16,))
    enltm_v = view(SCRF, 2048, (16, 16))
    ktab_v = view(SCRF, 2048 + 256, (16, 16))
    Sf_v = view(SCRF, 2048 + 512, (8, 64))
    etotbc_v = view(SCRF, 2048 + 1024, (8, 16))
    totf_v = view(SCRF, 2048 + 1024 + 128, (16,))

    TTS = [slice(tt * NT, (tt + 1) * NT) for tt in range(NTT)]

    for nm, dst in [("c_ones", cones), ("c_onesrow", conesrow), ("c_sel", selb),
                    ("c_headsel", headsb), ("c_ss", ssb), ("c_eones", eones),
                    ("c_id", idb), ("c_mask", maskt)]:
        P.do("pool", lambda E, d=dst, s=dr[nm]: E.dma_start(out=_fs(d), in_=_fs(s)), wde, 16)
    P.do("sp", lambda E: E.dma_start(out=_fs(idf), in_=_fs(dr["c_id"])), xde, 16)
    P.do("sp", lambda E: E.dma_start(out=_fs(fbc), in_=_fs(dr["c_fb"])), xde, 16)
    P.do("sp", lambda E: E.dma_start(out=_fs(conesrow_f), in_=_fs(dr["c_onesrow"])), xde, 16)
    P.do("pool", lambda E: E.dma_start(
        out=xsb[:, :, :], in_=xin.rearrange("(kd p) t -> p kd t", p=128)), wde, 16)
    if l0a_mode is not None:
        P.do("pool", lambda E: E.dma_start(out=l0_wblk, in_=_fs(dr["w_l0blk"])), wde, 16)
        myselb = P.sb("myselb", [128, 16], BF16)
        P.do("pool", lambda E: E.dma_start(out=_fs(myselb), in_=_fs(dr["c_mysel"])), wde, 16)
    if l0a_mode == "fused":
        P.do("pool", lambda E: E.dma_start(out=f_qbd, in_=_fs(dr["w_l0qbd"])), wde, 16)
        P.do("pool", lambda E: E.dma_start(out=f_kbd, in_=_fs(dr["w_l0kbd"])), wde, 16)
        P.do("sp", lambda E: E.dma_start(out=_fs(miselb), in_=_fs(dr["c_misel"])), xde, 16)
    if "l1a" in need:
        import os as _os
        if _os.environ.get("SKIP_WF") != "1":
            P.do("pool", lambda E: E.dma_start(out=_fs(wfTsb), in_=_fs(dr["w_l1wf"])), wde, 16)
        if _os.environ.get("SKIP_DLT") != "1":
            P.do("sp", lambda E: E.dma_start(out=_fs(dlt), in_=_fs(dr["p_delta"])), xde, 16)
        if _os.environ.get("SKIP_NW") != "1":
            P.do("sp", lambda E: E.dma_start(out=_fs(nwsb), in_=_fs(dr["p_nw"])), xde, 16)
    if "l1f" in need:
        P.do("sp", lambda E: E.dma_start(out=_fs(b1sb), in_=_fs(dr["p_b1"])), xde, 16)
        P.do("sp", lambda E: E.dma_start(out=_fs(b2sb), in_=_fs(dr["p_b2"])), xde, 16)
        P.do("sp", lambda E: E.dma_start(out=_fs(lngb), in_=_fs(dr["p_lng"])), xde, 16)
    if "l0f" in need:
        P.do("sp", lambda E: E.dma_start(out=brsb[:, 0:1], in_=_fs(dr["p_br0"])), xde, 16)
    if "l1f" in need:
        P.do("sp", lambda E: E.dma_start(out=brsb[:, 1:2], in_=_fs(dr["p_br1"])), xde, 16)
    for e in ("pe", "dve", "act"):
        P.wait(e, wde)
        P.wait(e, xde)
    P.wait("sp", wde)
    P.wait("pool", xde)

    def dump(name, ap, shape, dt=F32):
        if name not in dump_names:
            return
        d = P.nc.dram_tensor("dump_" + name, list(shape), dt, kind="ExternalOutput").ap()
        for e in ("pe", "dve", "act"):
            P.do(e, lambda E: E.nop(), S[e], 1)
            P.wait("sp", S[e])
        P.do("sp", lambda E, d=d, ap=ap: E.dma_start(out=_fs(d), in_=ap), xde, 16)
        P.wait("sp", xde)
        dumps_made.append(name)

    ph_use = [{"dve": 0, "act": 0} for _ in range(4)]
    py_use = [{"dve": 0, "act": 0} for _ in range(2)]
    ps1_use = {"dve": 0, "act": 0}

    def pe_claim(d):
        P.wait("pe", dve, d["dve"])
        P.wait("pe", act, d["act"])

    pt_oh_use = [0]
    ptrow_use = [0]
    rows_war = [0]

    def emit_ln(src, out_bf, router=None, affine=False, eps=1e-5):
        P.wait("act", dve)
        P.wait("act", pe)
        P.wait("pe", dve)
        stg_pe_use = [0, 0]
        sq_pe_use = [0, 0]
        rows_pe_use = [rows_war[0]]
        for tt in range(NTT):
            if LN_TRUNC < 1 + tt:
                break
            ts = TTS[tt]
            for kd in range(KD):
                par = kd % 2
                P.wait("act", pe, sq_pe_use[par])
                P.do("act", lambda E, kd=kd, ts=ts, par=par: E.activation(
                    sqt[:, par, :], src[:, kd, ts], AF.Square), act, 1)
                sq_done = act.v
                if kd == 0:
                    pe_claim(ps1_use)
                    pe_claim(py_use[0])
                P.do("pe", lambda E, kd=kd, ts=ts: E.matmul(
                    PS1[0:16, :], cones[:, :], src[:, kd, ts],
                    start=(kd == 0), stop=(kd == KD - 1)))
                P.wait("pe", act, sq_done)
                P.do("pe", lambda E, kd=kd, par=par: E.matmul(
                    PY[0:16, 0, :], cones[:, :], sqt[:, par, :],
                    start=(kd == 0), stop=(kd == KD - 1)), pe, 1)
                sq_pe_use[par] = pe.v
            stats_done = pe.v
            if LN_TRUNC < 10 + tt:
                continue
            P.wait("act", pe, stats_done)
            P.do("act", lambda E: E.activation(st16[0:16, 0, :], PS1[0:16, :], AF.Copy,
                                               scale=1.0 / 1024.0), act, 1)
            ps1_use["act"] = act.v
            P.do("act", lambda E: E.activation(st16[0:16, 1, :], PY[0:16, 0, :], AF.Copy,
                                               scale=1.0 / 1024.0), act, 1)
            py_use[0]["act"] = act.v
            pe_claim_needed = py_use[1]
            P.wait("act", dve, pe_claim_needed["dve"])
            P.do("act", lambda E: E.activation(PY[0:16, 1, :], st16[0:16, 0, :],
                                               AF.Square), act, 1)
            m2_done = act.v
            P.wait("dve", act, m2_done)
            P.do("dve", lambda E: E.tensor_tensor(st16[0:16, 1, :], st16[0:16, 1, :],
                                                  PY[0:16, 1, :], ALU.subtract), dve, 1)
            py_use[1]["dve"] = dve.v
            var_done = dve.v
            P.wait("act", dve, var_done)
            P.do("act", lambda E: E.activation(st16[0:16, 1, :], st16[0:16, 1, :],
                                                AF.Ln, bias=fbc[0:16, 0:1]), act, 1)
            P.wait("act", pe, rows_pe_use[0])
            P.do("act", lambda E: E.activation(rows_sm[0:1, 1, :], st16[0:1, 1, :],
                                               AF.Exp, scale=-0.5), act, 1)
            P.do("act", lambda E: E.activation(rows_sm[0:1, 0, :], st16[0:1, 0, :],
                                               AF.Copy), act, 1)
            rows_done = act.v
            if LN_TRUNC < 20 + tt:
                continue
            P.wait("pe", act, rows_done)
            pe_claim(py_use[0])
            pe_claim(py_use[1])
            P.do("pe", lambda E: E.matmul(PY[:, 0, :], conesrow_f[:, :],
                                          rows_sm[0:1, 0, :], start=True, stop=True))
            P.do("pe", lambda E: E.matmul(PY[:, 1, :], conesrow_f[:, :],
                                          rows_sm[0:1, 1, :], start=True, stop=True), pe, 1)
            bc_done = pe.v
            rows_pe_use[0] = pe.v
            if LN_TRUNC < 30 + tt:
                continue
            P.wait("dve", pe, bc_done)
            for kd in range(KD):
                spar = kd % 2
                if router is not None:
                    P.wait("dve", pe, stg_pe_use[spar])
                P.do("dve", lambda E, kd=kd, ts=ts, spar=spar: E.scalar_tensor_tensor(
                    stg32[:, spar, :], PY[:, 0, :], -1.0, src[:, kd, ts],
                    ALU.mult, ALU.add), dve, 1)
                if router is not None:
                    P.do("dve", lambda E, spar=spar: E.tensor_tensor(
                        stg32[:, spar, :], stg32[:, spar, :], PY[:, 1, :],
                        ALU.mult), dve, 1)
                    P.do("dve", lambda E, kd=kd, ts=ts, spar=spar: E.tensor_copy(
                        out_bf[:, kd, ts], stg32[:, spar, :]), dve, 1)
                    st_done = dve.v
                    P.wait("pe", dve, st_done)
                    router(kd, tt, stg32[:, spar, :])
                    stg_pe_use[spar] = pe.v
                elif affine:
                    P.do("dve", lambda E, spar=spar: E.tensor_tensor(
                        stg32[:, spar, :], stg32[:, spar, :], PY[:, 1, :],
                        ALU.mult), dve, 1)
                    P.do("dve", lambda E, kd=kd, ts=ts, spar=spar: E.tensor_scalar(
                        out_bf[:, kd, ts], stg32[:, spar, :],
                        lngb[:, 0, kd:kd + 1], lngb[:, 1, kd:kd + 1],
                        ALU.mult, ALU.add), dve, 1)
                else:
                    P.do("dve", lambda E, kd=kd, ts=ts, spar=spar: E.tensor_tensor(
                        out_bf[:, kd, ts], stg32[:, spar, :], PY[:, 1, :],
                        ALU.mult), dve, 1)
            py_use[0]["dve"] = py_use[1]["dve"] = dve.v
        rows_war[0] = rows_pe_use[0]
        P.wait("pe", dve)

    def emit_l0a():
        emit_ln(xsb, xn)
        if l0a_mode == "scores":
            smine_f = view(SCRF, 2048, (T,))
            for tt in range(NTT):
                ts = TTS[tt]
                for kd in range(KD):
                    for b in range(2):
                        pe_claim(ph_use[b])
                    P.do("pe", lambda E, kd=kd, ts=ts: E.matmul(
                        PH[:, 0, :], l0_wblk[:, 0, :], xn[:, kd, ts],
                        start=True, stop=True))
                    P.do("pe", lambda E, kd=kd, ts=ts: E.matmul(
                        PH[:, 1, :], l0_wblk[:, 1, :], xn[:, kd, ts],
                        start=True, stop=True), pe, 1)
                    qk_mm = pe.v
                    P.wait("dve", pe, qk_mm)
                    P.do("dve", lambda E: E.tensor_copy(l0_q, PH[:, 0, :]), dve, 1)
                    P.do("dve", lambda E: E.tensor_tensor(l0_qk, l0_q, PH[:, 1, :],
                                                          ALU.mult), dve, 1)
                    qk_done = dve.v
                    ph_use[0]["dve"] = ph_use[1]["dve"] = dve.v
                    P.wait("pe", dve, qk_done)
                    if kd == 0:
                        pe_claim(ps1_use)
                    P.do("pe", lambda E, kd=kd: E.matmul(
                        PS1[0:16, :], headsb[:, kd, :], l0_qk,
                        start=(kd == 0), stop=(kd == KD - 1)),
                        pe if kd == KD - 1 else None, 1)
                s_done = pe.v
                P.wait("dve", pe, s_done)
                P.do("dve", lambda E, ts=ts: E.tensor_copy(smine_f[0:16, ts],
                                                           PS1[0:16, :]), dve, 1)
                ps1_use["dve"] = dve.v
            P.wait("sp", dve)
            P.do("sp", lambda E, smine_f=smine_f: E.dma_start(
                out=_fs(s_my), in_=smine_f[0:16, :]), xde, 16)
            P.wait("sp", xde)
            return
        for tt in range(NTT):
            ts = TTS[tt]
            for kd in range(KD):
                pe_claim(ph_use[2])
                P.do("pe", lambda E, kd=kd, ts=ts: E.matmul(
                    PH[:, 2, :], l0_wblk[:, 2, :], xn[:, kd, ts],
                    start=True, stop=True), pe, 1)
                v_mm = pe.v
                P.wait("act", pe, v_mm)
                P.do("act", lambda E, kd=kd, tt=tt: E.activation(
                    l0_vs, PH[:, 2, :], AF.Copy,
                    accum_out=vsump_v[:, kd, tt:tt + 1]), act, 1)
                ph_use[2]["act"] = act.v
        P.wait("dve", act)
        P.do("dve", lambda E: E.tensor_tensor(vsum8_v, vsump_v[:, :, 0],
                                              vsump_v[:, :, 1], ALU.add), dve, 1)
        P.do("dve", lambda E: E.tensor_tensor(vsc2_v, vsump_v[:, :, 2],
                                              vsump_v[:, :, 3], ALU.add), dve, 1)
        P.do("dve", lambda E: E.tensor_tensor(vsum8_v, vsum8_v, vsc2_v, ALU.add), dve, 1)
        sall_act_use = [0, 0]
        l0e_pe_use = [0]
        inv_dve_use = [0]
        for tt in range(NTT):
            ts = TTS[tt]
            par = tt % 2
            P.wait("pool", act, sall_act_use[par])
            P.do("pool", lambda E, ts=ts, par=par: E.dma_start(
                out=l0_sall[:, par, :], in_=s_all[:, ts]), sls[par], 16)
            P.wait("act", sls[par])
            P.wait("act", pe, l0e_pe_use[0])
            P.do("act", lambda E, par=par: E.activation(
                l0_e, l0_sall[:, par, :], AF.Exp, scale=1.0 / 32.0), act, 1)
            e_done = act.v
            sall_act_use[par] = act.v
            P.wait("pe", act, e_done)
            pe_claim(ps1_use)
            pe_claim(py_use[0])
            P.do("pe", lambda E: E.matmul(PS1[0:16, :], ssb[:, :], l0_e,
                                          start=True, stop=True))
            P.do("pe", lambda E: E.matmul(PY[0:16, 0, :], myselb[:, :], l0_e,
                                          start=True, stop=True), pe, 1)
            se_done = pe.v
            l0e_pe_use[0] = pe.v
            P.wait("act", pe, se_done)
            P.do("act", lambda E: E.activation(st16[0:16, 0, :], PS1[0:16, :],
                                               AF.Ln), act, 1)
            ps1_use["act"] = act.v
            P.wait("act", dve, inv_dve_use[0])
            P.do("act", lambda E: E.activation(st16[0:16, 1, :], st16[0:16, 0, :],
                                               AF.Exp, scale=-1.0), act, 1)
            inv_done = act.v
            P.wait("dve", act, inv_done)
            P.do("dve", lambda E, ts=ts: E.tensor_tensor(
                l0_probs[0:16, ts], PY[0:16, 0, :], st16[0:16, 1, :], ALU.mult), dve, 1)
            py_use[0]["dve"] = dve.v
            inv_dve_use[0] = dve.v
        probs_done = dve.v
        P.wait("pe", dve, probs_done)
        for tt in range(NTT):
            ts = TTS[tt]
            for kd in range(KD):
                bk = kd % 4
                pe_claim(ph_use[bk])
                P.do("pe", lambda E, kd=kd, ts=ts, bk=bk: E.matmul(
                    PH[:, bk, :], selb[:, kd, :], l0_probs[0:16, ts],
                    start=True, stop=True), pe, 1)
                mm_done = pe.v
                P.wait("dve", pe, mm_done)
                P.do("dve", lambda E, kd=kd, ts=ts, bk=bk: E.scalar_tensor_tensor(
                    xsb[:, kd, ts], PH[:, bk, :], vsum8_v[:, kd:kd + 1], xsb[:, kd, ts],
                    ALU.mult, ALU.add), dve, 1)
                ph_use[bk]["dve"] = dve.v

    def emit_l0a_fused():
        emit_ln(xsb, xn)
        seacc = view(SCRF, 2048, (T,))
        P.wait("pe", wde)
        P.wait("pe", xde)
        pe_claim(ph_use[0])
        pe_claim(ph_use[1])
        P.do("pe", lambda E: E.matmul(PH[:, 0, 0:128], f_qbd, f_kbd,
                                      start=True, stop=True))
        P.do("pe", lambda E: E.matmul(PH[:, 1, 0:128], f_kbd, f_qbd,
                                      start=True, stop=True), pe, 1)
        mb = pe.v
        P.wait("dve", pe, mb)
        P.do("dve", lambda E: E.tensor_copy(f_mblk, PH[:, 0, 0:128]), dve, 1)
        P.do("dve", lambda E: E.tensor_copy(f_mblkT, PH[:, 1, 0:128]), dve, 1)
        ph_use[0]["dve"] = ph_use[1]["dve"] = dve.v
        mev = dve.v
        P.wait("pe", dve, mev)
        pe_claim(py_use[0])
        pe_claim(py_use[1])
        P.do("pe", lambda E: E.matmul(PY[:, 0, 0:1], f_mblk, cones[:, 0:1],
                                      start=True, stop=True))
        P.do("pe", lambda E: E.matmul(PY[:, 1, 0:1], f_mblkT, cones[:, 0:1],
                                      start=True, stop=True), pe, 1)
        wv_ = pe.v
        P.wait("dve", pe, wv_)
        P.do("dve", lambda E: E.tensor_copy(wcolf[:, :], PY[:, 0, 0:1]), dve, 1)
        P.do("dve", lambda E: E.tensor_tensor(wcolf[:, :], wcolf[:, :],
                                              PY[:, 1, 0:1], ALU.add), dve, 1)
        py_use[0]["dve"] = py_use[1]["dve"] = dve.v
        P.do("dve", lambda E: E.tensor_copy(f_m2x[:, 1:2], wcolf[:, :]), dve, 1)
        P.do("dve", lambda E: E.tensor_copy(f_m2x[:, 2:3], fbc[:, 4:5]), dve, 1)
        for kd in range(KD):
            P.do("dve", lambda E, kd=kd: E.tensor_scalar(
                f_wsel[:, kd, :], headsb[:, kd, :], wcolf[:, 0:1], None,
                ALU.mult), dve, 1)
        wsel_done = dve.v
        P.wait("pe", dve, wsel_done)
        pe_claim(ps1_use)
        P.do("pe", lambda E: E.matmul(PS1[0:1, 0:1], f_m2x[:, 1:2], f_m2x[:, 2:3],
                                      start=True, stop=True), pe, 1)
        m2s = pe.v
        P.wait("dve", pe, m2s)
        P.do("dve", lambda E: E.tensor_copy(f_m2x[0:1, 0:1], PS1[0:1, 0:1]), dve, 1)
        ps1_use["dve"] = dve.v
        m2e = dve.v
        P.wait("pe", dve, m2e)
        pe_claim(py_use[0])
        P.do("pe", lambda E: E.matmul(PY[:, 0, 0:1], conesrow[:, :],
                                      f_m2x[0:1, 0:1], start=True, stop=True), pe, 1)
        m2b = pe.v
        P.wait("dve", pe, m2b)
        P.do("dve", lambda E: E.tensor_scalar(m11f[:, :], PY[:, 0, 0:1], 0.5, None,
                                              ALU.mult), dve, 1)
        py_use[0]["dve"] = dve.v
        P.do("dve", lambda E: E.memset(seacc[0:16, :], 0.0), dve, 1)
        P.do("dve", lambda E: E.memset(f_em[0:16, :], 0.0), dve, 1)
        xb_pool_use = [0, 0]
        xb_pe_use = [0, 0]
        for ci in range(NE):
            for tt in range(NTT):
                ts = TTS[tt]
                par = (ci * NTT + tt) % 2
                P.wait("pool", dve, xb_pool_use[par])
                P.wait("pool", pe, xb_pe_use[par])
                P.do("pool", lambda E, ci=ci, ts=ts, par=par: E.dma_start(
                    out=f_xb[:, par, :, :],
                    in_=dr["xall"][ci].rearrange("(kd p) t -> p kd t", p=128)[:, :, ts]),
                    sls[par], 16)
                P.wait("pe", sls[par])
                last_stat = [0]
                for kd in range(KD):
                    zb = kd % 2
                    pe_claim(ph_use[zb])
                    if kd == 0:
                        pe_claim(ps1_use)
                        pe_claim(py_use[0])
                        pe_claim(py_use[1])
                        pe_claim(ph_use[2])
                    P.do("pe", lambda E, kd=kd, zb=zb, par=par: E.matmul(
                        PH[:, zb, :], f_mblkT, f_xb[:, par, kd, :],
                        start=True, stop=True), pe, 1)
                    zmm = pe.v
                    P.wait("dve", pe, zmm)
                    P.wait("dve", pe, last_stat[0])
                    P.do("dve", lambda E, kd=kd, zb=zb, par=par: E.tensor_tensor(
                        f_xz, f_xb[:, par, kd, :], PH[:, zb, :], ALU.mult), dve, 1)
                    ph_use[zb]["dve"] = dve.v
                    P.do("dve", lambda E, kd=kd, par=par: E.tensor_tensor(
                        f_sq, f_xb[:, par, kd, :], f_xb[:, par, kd, :],
                        ALU.mult), dve, 1)
                    sx = dve.v
                    P.wait("pe", dve, sx)
                    P.do("pe", lambda E, kd=kd, par=par: E.matmul(
                        PS1[0:16, :], cones[:, :], f_xb[:, par, kd, :],
                        start=(kd == 0), stop=(kd == KD - 1)))
                    P.do("pe", lambda E, kd=kd: E.matmul(
                        PY[0:16, 0, :], cones[:, :], f_sq,
                        start=(kd == 0), stop=(kd == KD - 1)))
                    P.do("pe", lambda E, kd=kd: E.matmul(
                        PY[0:16, 1, :], headsb[:, kd, :], f_xz,
                        start=(kd == 0), stop=(kd == KD - 1)))
                    P.do("pe", lambda E, kd=kd, par=par: E.matmul(
                        PH[0:16, 2, 0:512], f_wsel[:, kd, :], f_xb[:, par, kd, :],
                        start=(kd == 0), stop=(kd == KD - 1)), pe, 1)
                    last_stat[0] = pe.v
                stats_done = pe.v
                xb_pool_use[par] = dve.v
                xb_pe_use[par] = pe.v
                P.wait("act", pe, stats_done)
                P.do("act", lambda E: E.activation(st16[0:16, 0, :], PS1[0:16, :],
                                                   AF.Copy, scale=1.0 / 1024.0), act, 1)
                P.do("act", lambda E: E.activation(st16[0:16, 1, :], PY[0:16, 0, :],
                                                   AF.Copy, scale=1.0 / 1024.0), act, 1)
                P.wait("act", dve)
                P.do("act", lambda E: E.activation(PH[0:16, 3, 0:512], st16[0:16, 0, :],
                                                   AF.Square), act, 1)
                m2d = act.v
                P.wait("dve", act, m2d)
                P.do("dve", lambda E: E.tensor_tensor(st16[0:16, 1, :], st16[0:16, 1, :],
                                                      PH[0:16, 3, 0:512],
                                                      ALU.subtract), dve, 1)
                vd = dve.v
                P.wait("act", dve, vd)
                P.do("act", lambda E: E.activation(st16[0:16, 1, :], st16[0:16, 1, :],
                                                   AF.Ln, bias=fbc[0:16, 0:1]), act, 1)
                P.do("act", lambda E: E.activation(st16[0:16, 1, :], st16[0:16, 1, :],
                                                   AF.Exp, scale=-1.0), act, 1)
                ivd = act.v
                P.wait("dve", act, ivd)
                P.do("dve", lambda E: E.tensor_tensor(stg32[0:16, 0, :], st16[0:16, 0, :],
                                                      PH[0:16, 2, 0:512],
                                                      ALU.mult), dve, 1)
                P.do("dve", lambda E: E.tensor_tensor(stg32[0:16, 0, :], PY[0:16, 1, :],
                                                      stg32[0:16, 0, :],
                                                      ALU.subtract), dve, 1)
                P.do("dve", lambda E: E.scalar_tensor_tensor(
                    stg32[0:16, 0, :], PH[0:16, 3, 0:512], m11f[0:16, 0:1],
                    stg32[0:16, 0, :], ALU.mult, ALU.add), dve, 1)
                P.do("dve", lambda E: E.tensor_tensor(stg32[0:16, 0, :],
                                                      stg32[0:16, 0, :],
                                                      st16[0:16, 1, :], ALU.mult),
                     dve, 1)
                sdone = dve.v
                ps1_use["dve"] = py_use[0]["dve"] = py_use[1]["dve"] = dve.v
                ph_use[2]["dve"] = ph_use[3]["dve"] = dve.v
                ps1_use["act"] = py_use[0]["act"] = act.v
                P.wait("act", dve, sdone)
                P.do("act", lambda E: E.activation(f_et[0:16, :], stg32[0:16, 0, :],
                                                   AF.Exp, scale=1.0 / 32.0), act, 1)
                etd = act.v
                P.wait("dve", act, etd)
                P.do("dve", lambda E, ts=ts: E.tensor_tensor(
                    seacc[0:16, ts], seacc[0:16, ts], f_et[0:16, :], ALU.add), dve, 1)
                P.do("dve", lambda E, ci=ci, ts=ts: E.scalar_tensor_tensor(
                    f_em[0:16, ts], f_et[0:16, :], miselb[:, ci:ci + 1],
                    f_em[0:16, ts], ALU.mult, ALU.add), dve, 1)
        for tt in range(NTT):
            ts = TTS[tt]
            P.wait("act", dve)
            P.do("act", lambda E, ts=ts: E.activation(st16[0:16, 0, :], seacc[0:16, ts],
                                                      AF.Ln), act, 1)
            P.do("act", lambda E: E.activation(st16[0:16, 0, :], st16[0:16, 0, :],
                                               AF.Exp, scale=-1.0), act, 1)
            ivse = act.v
            P.wait("dve", act, ivse)
            P.do("dve", lambda E, ts=ts: E.tensor_tensor(
                l0_probs[0:16, ts], f_em[0:16, ts], st16[0:16, 0, :], ALU.mult),
                dve, 1)
        probs_done = dve.v
        for tt in range(NTT):
            ts = TTS[tt]
            for kd in range(KD):
                pe_claim(ph_use[2])
                P.do("pe", lambda E, kd=kd, ts=ts: E.matmul(
                    PH[:, 2, :], l0_wblk[:, 2, :], xn[:, kd, ts],
                    start=True, stop=True), pe, 1)
                v_mm = pe.v
                P.wait("act", pe, v_mm)
                P.do("act", lambda E, kd=kd, tt=tt: E.activation(
                    l0_vs, PH[:, 2, :], AF.Copy,
                    accum_out=vsump_v[:, kd, tt:tt + 1]), act, 1)
                ph_use[2]["act"] = act.v
        P.wait("dve", act)
        P.do("dve", lambda E: E.tensor_tensor(vsum8_v, vsump_v[:, :, 0],
                                              vsump_v[:, :, 1], ALU.add), dve, 1)
        P.do("dve", lambda E: E.tensor_tensor(vsc2_v, vsump_v[:, :, 2],
                                              vsump_v[:, :, 3], ALU.add), dve, 1)
        P.do("dve", lambda E: E.tensor_tensor(vsum8_v, vsum8_v, vsc2_v, ALU.add),
             dve, 1)
        vs_done = dve.v
        P.wait("pe", dve, max(probs_done, vs_done))
        for tt in range(NTT):
            ts = TTS[tt]
            for kd in range(KD):
                bk = kd % 4
                pe_claim(ph_use[bk])
                P.do("pe", lambda E, kd=kd, ts=ts, bk=bk: E.matmul(
                    PH[:, bk, :], selb[:, kd, :], l0_probs[0:16, ts],
                    start=True, stop=True), pe, 1)
                mm_done = pe.v
                P.wait("dve", pe, mm_done)
                P.do("dve", lambda E, kd=kd, ts=ts, bk=bk: E.scalar_tensor_tensor(
                    xsb[:, kd, ts], PH[:, bk, :], vsum8_v[:, kd:kd + 1], xsb[:, kd, ts],
                    ALU.mult, ALU.add), dve, 1)
                ph_use[bk]["dve"] = dve.v

    def emit_ffn(L):
        P.wait("sp", pe)
        P.do("sp", lambda E: E.dma_start(out=_fs(wrsb), in_=_fs(dr[f"w_l{L}wr"])),
             S["wrs"], 16)
        P.wait("pe", S["wrs"])

        def router(kd, tt, stage_ap):
            if kd == 0:
                pe_claim(ph_use[tt])
            P.do("pe", lambda E, kd=kd, tt=tt, ap=stage_ap: E.matmul(
                PH[0:8, tt, :], wrsb[:, kd, :], ap,
                start=(kd == 0), stop=(kd == KD - 1)), pe, 1)

        emit_ln(xsb, xn, router=router)
        if L == 1:
            pass
        for tt in range(NTT):
            ts = TTS[tt]
            P.wait("dve", pe)
            P.do("dve", lambda E, tt=tt: E.tensor_scalar(
                lftt_v[0:8, tt % 2, :], PH[0:8, tt, :], brsb[:, L:L + 1], None,
                ALU.add), dve, 1)
            ph_use[tt]["dve"] = dve.v
            lf_done = dve.v
            for ci in range(4):
                ct = tt * 4 + ci
                cs = slice(ci * 128, (ci + 1) * 128)
                slot = ci % 2
                P.wait("pe", dve, lf_done)
                pe_claim(py_use[slot])
                P.do("pe", lambda E, cs=cs, slot=slot, tt=tt: E.transpose(
                    PY[:, slot, 0:8], lftt_v[0:8, tt % 2, cs], idf[0:8, 0:8]), pe, 1)
                tr_done = pe.v
                P.wait("dve", pe, tr_done)
                P.do("dve", lambda E, slot=slot: E.tensor_reduce(
                    cols_v[:, 0:1], PY[:, slot, 0:8], AX.X, ALU.max), dve, 1)
                if L == 1:
                    P.do("dve", lambda E, slot=slot: E.tensor_reduce(
                        cols_v[:, 1:2], PY[:, slot, 0:8], AX.X, ALU.max,
                        negate=True), dve, 1)
                P.do("dve", lambda E, slot=slot: E.tensor_scalar(
                    ohtm[:, :], PY[:, slot, 0:8], cols_v[:, 0:1], None,
                    ALU.is_equal), dve, 1)
                oh_done = dve.v
                py_use[slot]["dve"] = dve.v
                if L == 0:
                    P.wait("act", dve, oh_done)
                    P.do("act", lambda E, ct=ct: E.activation(
                        gcolb[:, ct:ct + 1], cols_v[:, 0:1], AF.Sigmoid), act, 1)
                else:
                    P.wait("act", dve, oh_done)
                    P.do("act", lambda E, slot=slot: E.activation(
                        ex_v[:, :], PY[:, slot, 0:8], AF.Exp,
                        bias=cols_v[:, 1:2]), act, 1)
                    exd = act.v
                    py_use[slot]["act"] = act.v
                    P.wait("dve", act, exd)
                    P.do("dve", lambda E, ct=ct: E.tensor_reduce(
                        secol_v[:, ct:ct + 1], ex_v[:, :], AX.X, ALU.add), dve, 1)
                P.wait("pe", dve, oh_done)
                P.wait("pe", dve, pt_oh_use[0])
                P.do("pe", lambda E: E.transpose(pt_oh[:, :], ohtm[:, :], idb[:, :]), pe, 1)
                trb = pe.v
                P.wait("dve", pe, trb)
                P.do("dve", lambda E, ct=ct: E.tensor_copy(
                    ohfm_ct[0:8, ct, :], pt_oh[0:8, :]), dve, 1)
                pt_oh_use[0] = dve.v
        if L == 1:
            P.do("dve", lambda E: E.reciprocal(ginv_v[:, :], secol_v[:, :]), dve, 1)
            P.do("dve", lambda E: E.tensor_copy(gcolb[:, :], ginv_v[:, :]), dve, 1)
        g_done = dve.v
        P.wait("pe", dve, g_done)
        P.wait("pe", act)
        grow_pe_use = [0]
        for tt in range(NTT):
            for ci in range(4):
                ct = tt * 4 + ci
                P.wait("pe", dve, ptrow_use[0])
                P.do("pe", lambda E, ct=ct: E.transpose(
                    ptrow[:, :], gcolb[:, ct:ct + 1], idb[:, :]), pe, 1)
                trg = pe.v
                P.wait("dve", pe, trg)
                P.wait("dve", pe, grow_pe_use[0])
                P.do("dve", lambda E, ci=ci: E.tensor_copy(
                    grow_sm[0:1, ci * 128:(ci + 1) * 128], ptrow[0:1, :]), dve, 1)
                ptrow_use[0] = dve.v
            gr = dve.v
            P.wait("pe", dve, gr)
            pe_claim(ps1_use)
            P.do("pe", lambda E: E.matmul(PS1[:, :], conesrow[:, :], grow_sm[0:1, :],
                                          start=True, stop=True), pe, 1)
            gb = pe.v
            grow_pe_use[0] = pe.v
            P.wait("dve", pe, gb)
            P.do("dve", lambda E, tt=tt: E.tensor_copy(gateb_v[:, TTS[tt]], PS1[:, :]),
                 dve, 1)
            ps1_use["dve"] = dve.v
        if L == 1:
            emit_ln(xn, xn, affine=True)
        oh_ready = dve.v
        w1_dr, w2_dr = dr[f"w_l{L}w1"], dr[f"w_l{L}w2"]
        w1_pe_use = [pe.v, pe.v]
        w2_pe_use = [pe.v, pe.v]
        w1n, w2n = [0], [0]
        for e in range(NE):
            P.wait("pe", dve, oh_ready)
            for tt in range(NTT):
                pe_claim(ps1_use)
                P.do("pe", lambda E, e=e, tt=tt: E.matmul(
                    PS1[:, :], eones[:, e, :], ohfm_v[0:8, TTS[tt]],
                    start=True, stop=True), pe, 1)
                gm = pe.v
                P.wait("dve", pe, gm)
                P.do("dve", lambda E, tt=tt: E.tensor_tensor(
                    gmask_v[:, TTS[tt]], PS1[:, :], gateb_v[:, TTS[tt]], ALU.mult),
                    dve, 1)
                ps1_use["dve"] = dve.v
            gmask_done = dve.v
            for fq in range(NFQ):
                for fc8 in range(8):
                    fc = fq * 8 + fc8
                    par = w1n[0] % 2
                    P.wait("pool", pe, w1_pe_use[par])
                    P.do("pool", lambda E, e=e, fc=fc, par=par: E.dma_start(
                        out=w1c_v[:, par, :, :], in_=w1_dr[e, fc, :, :, :]),
                        w1s[par], 16)
                    w1n[0] += 1
                    P.wait("pe", w1s[par])
                    for tt in range(NTT):
                        pe_claim(ph_use[tt])
                        for kd in range(KD):
                            P.do("pe", lambda E, kd=kd, tt=tt, par=par: E.matmul(
                                PH[:, tt, :], w1c_v[:, par, kd, :], xn[:, kd, TTS[tt]],
                                start=(kd == 0), stop=(kd == KD - 1)),
                                pe if kd == KD - 1 else None, 1)
                        mmv = pe.v
                        P.wait("act", pe, mmv)
                        if L == 0:
                            P.do("act", lambda E, fc8=fc8, tt=tt: E.activation(
                                hq_v[:, fc8, TTS[tt]], PH[:, tt, :], AF.Gelu), act, 1)
                        else:
                            P.do("act", lambda E, fc8=fc8, tt=tt, e=e, fc=fc: E.activation(
                                hq_v[:, fc8, TTS[tt]], PH[:, tt, :], AF.Relu,
                                bias=b1sb[:, e, fc:fc + 1]), act, 1)
                        ph_use[tt]["act"] = act.v
                    w1_pe_use[par] = pe.v
                h_done = act.v
                for dout in range(8):
                    par = w2n[0] % 2
                    P.wait("pool", pe, w2_pe_use[par])
                    P.do("pool", lambda E, e=e, fq=fq, dout=dout, par=par: E.dma_start(
                        out=w2c_v[:, par, :, :], in_=w2_dr[e, fq, dout, :, :, :]),
                        w2s[par], 16)
                    w2n[0] += 1
                    P.wait("pe", w2s[par])
                    P.wait("pe", act, h_done)
                    if dout == 0 and fq == 0:
                        P.wait("pe", dve, gmask_done)
                    for tt in range(NTT):
                        pe_claim(ph_use[tt])
                        for fc8 in range(8):
                            P.do("pe", lambda E, fc8=fc8, tt=tt, par=par: E.matmul(
                                PH[:, tt, :], w2c_v[:, par, fc8, :],
                                hq_v[:, fc8, TTS[tt]],
                                start=(fc8 == 0), stop=(fc8 == 7)),
                                pe if fc8 == 7 else None, 1)
                        mmv = pe.v
                        P.wait("dve", pe, mmv)
                        if fq == 0:
                            P.do("dve", lambda E, dout=dout, tt=tt: E.tensor_copy(
                                yac_v[:, dout, TTS[tt]], PH[:, tt, :]), dve, 1)
                        elif fq < 3 or L == 0:
                            P.do("dve", lambda E, dout=dout, tt=tt: E.tensor_tensor(
                                yac_v[:, dout, TTS[tt]], yac_v[:, dout, TTS[tt]],
                                PH[:, tt, :], ALU.add), dve, 1)
                        else:
                            P.do("dve", lambda E, dout=dout, tt=tt, e=e:
                                 E.scalar_tensor_tensor(
                                     yac_v[:, dout, TTS[tt]], PH[:, tt, :],
                                     b2sb[:, e, dout:dout + 1], yac_v[:, dout, TTS[tt]],
                                     ALU.add, ALU.add), dve, 1)
                        ph_use[tt]["dve"] = dve.v
                    w2_pe_use[par] = pe.v
            for dout in range(8):
                for tt in range(NTT):
                    sl = tt % 2
                    P.do("dve", lambda E, dout=dout, tt=tt, sl=sl: E.tensor_tensor(
                        sqt[:, sl, :], yac_v[:, dout, TTS[tt]], gmask_v[:, TTS[tt]],
                        ALU.mult), dve, 1)
                    P.do("dve", lambda E, dout=dout, tt=tt, sl=sl: E.tensor_tensor(
                        xsb[:, dout, TTS[tt]], xsb[:, dout, TTS[tt]], sqt[:, sl, :],
                        ALU.add), dve, 1)

    def emit_l1a():
        if L1A_TRUNC < 0:
            return
        emit_ln(xsb, xn)
        if L1A_TRUNC < 1:
            return
        P.do("dve", lambda E: E.memset(Sf_v, 0.0), dve, 1)
        P.do("dve", lambda E: E.memset(sbf_v, 0.0), dve, 1)
        P.do("dve", lambda E: E.memset(kds_v, 0.0), dve, 1)
        for tt in range(NTT):
            ts = TTS[tt]
            pe_claim(ps1_use)
            for kd in range(KD):
                P.do("pe", lambda E, kd=kd, ts=ts: E.matmul(
                    PS1[0:16, :], wfTsb[:, kd, :], xn[:, kd, ts],
                    start=(kd == 0), stop=(kd == KD - 1)),
                    pe if kd == KD - 1 else None, 1)
            f_done = pe.v
            P.wait("dve", pe, f_done)
            P.do("dve", lambda E: E.tensor_scalar(st16[0:16, 1, :], PS1[0:16, :],
                                                  dlt[:, 0:1], 0.0, ALU.add, ALU.min),
                 dve, 1)
            min_done = dve.v
            ps1_use["dve"] = dve.v
            P.wait("act", pe, f_done)
            P.wait("act", dve, min_done)
            P.do("act", lambda E: E.activation(st16[0:16, 0, :], PS1[0:16, :], AF.Abs,
                                               bias=dlt[:, 0:1]), act, 1)
            P.do("act", lambda E: E.activation(st16[0:16, 0, :], st16[0:16, 0, :],
                                               AF.Exp, scale=-1.0), act, 1)
            P.do("act", lambda E: E.activation(st16[0:16, 0, :], st16[0:16, 0, :],
                                               AF.Ln, bias=fbc[0:16, 2:3]), act, 1)
            t3_done = act.v
            ps1_use["act"] = t3_done
            P.wait("dve", act, t3_done)
            P.do("dve", lambda E: E.tensor_tensor(st16[0:16, 1, :], st16[0:16, 1, :],
                                                  st16[0:16, 0, :], ALU.subtract), dve, 1)
            for ci in range(4):
                c0 = ci * 128
                P.do("dve", lambda E, c0=c0, ts=ts: E.tensor_tensor_scan(
                    lbbf_v[0:16, ts.start + c0:ts.start + c0 + 128],
                    st16[0:16, 1, c0:c0 + 128], st16[0:16, 1, c0:c0 + 128],
                    0.0, ALU.add, ALU.bypass), dve, 1)
        lb_done = dve.v
        if L1A_TRUNC < 2:
            return
        P.wait("act", dve, lb_done)
        P.do("act", lambda E: E.activation(elbbf_v[0:16, :], lbbf_v[0:16, :], AF.Exp),
             act, 1)
        P.do("dve", lambda E: E.tensor_copy(totf_v[0:16, :], lbbf_v[0:16, 127::128]),
             dve, 1)
        tot_done = dve.v
        P.wait("act", dve, tot_done)
        P.do("act", lambda E: E.activation(etot_bf[0:16, :], totf_v[0:16, :], AF.Exp),
             act, 1)
        etot_done = act.v
        for c in range(16):
            P.do("dve", lambda E, c=c: E.tensor_scalar(
                sfx_v[0:16, c * 128:(c + 1) * 128], lbbf_v[0:16, c * 128:(c + 1) * 128],
                totf_v[0:16, c:c + 1], -1.0, ALU.subtract, ALU.mult), dve, 1)
        sfx_done = dve.v
        P.wait("pe", act, etot_done)
        for kd in range(KD):
            bk = kd % 2
            pe_claim(py_use[bk])
            P.do("pe", lambda E, kd=kd, bk=bk: E.matmul(
                PY[:, bk, 0:16], selb[:, kd, :], etot_bf[0:16, :],
                start=True, stop=True), pe, 1)
            bc = pe.v
            P.wait("dve", pe, bc)
            P.do("dve", lambda E, kd=kd, bk=bk: E.tensor_copy(
                etotbc_v[:, kd, :], PY[:, bk, 0:16]), dve, 1)
            py_use[bk]["dve"] = dve.v
        P.wait("pe", dve, sfx_done)
        for c in range(16):
            P.wait("pe", act)
            P.do("pe", lambda E, c=c: E.transpose(
                pt_lb[:, :], lbbf_v[0:16, c * 128:(c + 1) * 128], idb[0:16, 0:16]), pe, 1)
            t1 = pe.v
            P.wait("act", pe, t1)
            P.do("act", lambda E, c=c: E.activation(enltm_v[:, c, :], pt_lb[:, :],
                                                    AF.Exp, scale=-1.0), act, 1)
            en_done = act.v
            P.wait("pe", act, en_done)
            P.do("pe", lambda E, c=c: E.transpose(
                pt_lb[:, :], sfx_v[0:16, c * 128:(c + 1) * 128], idb[0:16, 0:16]), pe, 1)
            t2 = pe.v
            P.wait("act", pe, t2)
            P.do("act", lambda E, c=c: E.activation(ktab_v[:, c, :], pt_lb[:, :],
                                                    AF.Exp), act, 1)
        tables_act = act.v
        if L1A_TRUNC < 3:
            return

        wq_dr, wk_dr, wv_dr, wo_dr = (dr["w_l1wq"], dr["w_l1wk"], dr["w_l1wv"],
                                      dr["w_l1wo"])
        ws_pe_use = [pe.v, pe.v]
        wsn = [0]

        def stream_w(src):
            par = wsn[0] % 2
            P.wait("pool", pe, ws_pe_use[par])
            P.do("pool", lambda E, src=src, par=par: E.dma_start(
                out=wstream_v[:, par, :, :], in_=_fs(src)), wss[par], 16)
            wsn[0] += 1
            return par, wss[par].v

        def pA(h):
            return PH[:, h // 4, (h % 4) * 128:(h % 4) * 128 + 128]

        def pO(h):
            half, r = 64 * (h % 2), h // 2
            return PY[half:half + 64, r // 4, (r % 4) * 128:(r % 4) * 128 + 128]

        def pS(h):
            half, r = 64 * (h % 2), h // 2
            return PS1[half:half + 64, r * 64:r * 64 + 64]

        def hsl(h):
            return slice(64 * (h % 2), 64 * (h % 2) + 64)

        first_wave = True
        for w in range(NW):
            wts = slice(w * NT, (w + 1) * NT)
            par, wv_ = stream_w(wq_dr)
            P.wait("pe", wss[par], wv_)
            for ec in range(KD):
                bk = ec % 4
                pe_claim(ph_use[bk])
                for kd in range(KD):
                    P.do("pe", lambda E, ec=ec, kd=kd, bk=bk, par=par, wts=wts: E.matmul(
                        PH[:, bk, :], wstream_v[:, par, kd, ec * 128:(ec + 1) * 128],
                        xn[:, kd, wts], start=(kd == 0), stop=(kd == KD - 1)),
                        pe if kd == KD - 1 else None, 1)
                qd = pe.v
                P.wait("act", pe, qd)
                P.do("act", lambda E, ec=ec, bk=bk: E.activation(
                    qfm_v[:, ec, :], PH[:, bk, :], AF.Silu), act, 1)
                ph_use[bk]["act"] = act.v
            ws_pe_use[par] = pe.v
            par, wv_ = stream_w(wk_dr)
            P.wait("pe", wss[par], wv_)
            for ec in range(KD):
                bk = ec % 4
                pe_claim(ph_use[bk])
                for kd in range(KD):
                    P.do("pe", lambda E, ec=ec, kd=kd, bk=bk, par=par, wts=wts: E.matmul(
                        PH[:, bk, :], wstream_v[:, par, kd, ec * 128:(ec + 1) * 128],
                        xn[:, kd, wts], start=(kd == 0), stop=(kd == KD - 1)),
                        pe if kd == KD - 1 else None, 1)
                kdn = pe.v
                P.wait("act", pe, kdn)
                P.do("act", lambda E, ec=ec, bk=bk: E.activation(
                    kfm_v[:, ec, :], PH[:, bk, :], AF.Silu), act, 1)
                ph_use[bk]["act"] = act.v
                kf_done = act.v
                P.wait("dve", act, kf_done)
                P.do("dve", lambda E, ec=ec: E.tensor_scalar(
                    kmsk_v[:, 0, ec, :], kfm_v[:, ec, :], fbc[:, 4:5], None,
                    ALU.mult), dve, 1)
                P.do("dve", lambda E, ec=ec: E.tensor_scalar(
                    kmsk_v[:, 1, ec, :], kfm_v[:, ec, :], fbc[:, 5:6], None,
                    ALU.mult), dve, 1)
            for ct in range(CPW):
                c0 = wts.start + ct * 128
                for eh in range(2):
                    bk = (ct * 2 + eh) % 4
                    pe_claim(ph_use[bk])
                    for kd in range(KD):
                        P.do("pe", lambda E, kd=kd, bk=bk, par=par, c0=c0, eh=eh:
                             E.matmul(PH[:, bk, :], xn[:, kd, c0:c0 + 128],
                                      wstream_v[:, par, kd, eh * 512:(eh + 1) * 512],
                                      start=(kd == 0), stop=(kd == KD - 1)),
                             pe if kd == KD - 1 else None, 1)
                    ktd = pe.v
                    P.wait("act", pe, ktd)
                    P.do("act", lambda E, ct=ct, eh=eh, bk=bk: E.activation(
                        ktm_v[:, ct, eh * 512:(eh + 1) * 512], PH[:, bk, :], AF.Silu),
                        act, 1)
                    ph_use[bk]["act"] = act.v
            ws_pe_use[par] = pe.v
            par, wv_ = stream_w(wv_dr)
            P.wait("pe", wss[par], wv_)
            for ct in range(CPW):
                c0 = wts.start + ct * 128
                for eh in range(2):
                    bk = (ct * 2 + eh) % 4
                    pe_claim(ph_use[bk])
                    for kd in range(KD):
                        P.do("pe", lambda E, kd=kd, bk=bk, par=par, c0=c0, eh=eh:
                             E.matmul(PH[:, bk, :], xn[:, kd, c0:c0 + 128],
                                      wstream_v[:, par, kd, eh * 512:(eh + 1) * 512],
                                      start=(kd == 0), stop=(kd == KD - 1)),
                             pe if kd == KD - 1 else None, 1)
                    vtd = pe.v
                    P.wait("dve", pe, vtd)
                    P.do("dve", lambda E, ct=ct, eh=eh, bk=bk: E.tensor_copy(
                        vtm_v[:, ct, eh * 512:(eh + 1) * 512], PH[:, bk, :]), dve, 1)
                    ph_use[bk]["dve"] = dve.v
            ws_pe_use[par] = pe.v
            proj_act = act.v
            proj_dve = dve.v
            if L1A_TRUNC < 4:
                continue
            def pA(k):
                return PH[:, k // 4, (k % 4) * 128:(k % 4) * 128 + 128]

            def pO(k):
                return PY[:, k // 4, (k % 4) * 128:(k % 4) * 128 + 128]

            def pS(k):
                return PS1[:, k * 64:k * 64 + 64]

            def hsl(h):
                return slice(64 * (h % 2), 64 * (h % 2) + 64)

            if first_wave:
                P.wait("dve", act, tables_act)
                first_wave = False
            import os as _o8
            _scan_stop = int(_o8.environ.get("SCAN_STOP", "99"))
            for cl in range(CPW):
                c = w * CPW + cl
                if c > _scan_stop:
                    continue
                csl = slice(cl * 128, (cl + 1) * 128)
                P.wait("pe", act, proj_act)
                P.wait("pe", dve, dve.v)
                for kd in range(KD):
                    if c > int(_o8.environ.get("SKIP_ELB_AFTER", "99")):
                        continue
                    bk = kd % 2
                    P.wait("pe", dve)
                    P.do("pe", lambda E, kd=kd, bk=bk, csl=csl, w=w: E.matmul(
                        PH[:, bk, 0:128], selb[:, kd, :],
                        elbbf_v[0:16, w * NT + cl * 128:w * NT + cl * 128 + 128]
                        if False else elbbf_v[0:16, csl.start + w * NT:csl.stop + w * NT],
                        start=True, stop=True), pe, 1)
                    eb = pe.v
                    P.wait("dve", pe, eb)
                    P.do("dve", lambda E, kd=kd, bk=bk: E.tensor_copy(
                        elbbc_v[:, kd, :], PH[:, bk, 0:128]), dve, 1)
                for hg in range(2):
                    hb = hg * 8
                    P.wait("pe", dve, dve.v)
                    for k in range(8):
                        h = hb + k
                        P.do("pe", lambda E, h=h, k=k, hg=hg, csl=csl: E.matmul(
                            pA(hg * 8 + k), kmsk_v[:, h % 2, h // 2, csl],
                            qfm_v[:, h // 2, csl], start=True, stop=True),
                            pe if k == 7 else None, 1)
                    mma_done = pe.v
                    P.wait("dve", pe, mma_done)
                    for k in range(8):
                        h = hb + k
                        if c > int(_o8.environ.get("SKIP_A2_AFTER", "99")):
                            continue
                        P.do("dve", lambda E, h=h, k=k, hg=hg, c=c: E.scalar_tensor_tensor(
                            a2_v[:, h, :], pA(hg * 8 + k), enltm_v[:, c, h:h + 1],
                            maskt[:, :], ALU.mult, ALU.mult), dve, 1)
                    a2_done = dve.v
                    P.wait("pe", dve, a2_done)
                    for k in range(8):
                        h = hb + k
                        P.do("pe", lambda E, h=h, k=k, cl=cl: E.matmul(
                            pO(k), vtm_v[:, cl, (h // 2) * 128:(h // 2) * 128 + 128],
                            a2_v[:, h, :], start=True, stop=True),
                            pe if k == 7 else None, 1)
                    mmoi_done = pe.v
                    for pr in range(hb // 2, hb // 2 + 4):
                        P.do("dve", lambda E, pr=pr, csl=csl: E.tensor_tensor(
                            qs_v[:, pr % 4, :], qfm_v[:, pr, csl],
                            elbbc_v[:, pr, :], ALU.mult), dve, 1)
                    qs_done = dve.v
                    P.wait("pe", dve, qs_done)
                    for k in range(8):
                        h = hb + k
                        P.do("pe", lambda E, h=h, k=k, hg=hg: E.matmul(
                            pA(hg * 8 + k), sbf_v[:, h, :], qs_v[:, (h // 2) % 4, :],
                            start=True, stop=True), pe if k == 7 else None, 1)
                    mmoe_done = pe.v
                    P.wait("dve", pe, mmoe_done)
                    for k in range(8):
                        h = hb + k
                        P.do("dve", lambda E, h=h, k=k, csl=csl: E.tensor_tensor(
                            obf_v[hsl(h), h // 2, csl], pO(k)[hsl(h), :],
                            elbbc_v[hsl(h), h // 2, :], ALU.mult), dve, 1)
                    for k in range(8):
                        h = hb + k
                        P.do("dve", lambda E, h=h, k=k, hg=hg, csl=csl: E.tensor_tensor(
                            obf_v[hsl(h), h // 2, csl], obf_v[hsl(h), h // 2, csl],
                            pA(hg * 8 + k)[hsl(h), :], ALU.add), dve, 1)
                    for k in range(8):
                        h = hb + k
                        if c > int(_o8.environ.get("SKIP_KDS_AFTER", "99")):
                            continue
                        P.do("dve", lambda E, h=h, cl=cl, c=c: E.tensor_scalar(
                            kds_v[:, h // 2 % 4, hsl(h)],
                            ktm_v[:, cl, h * 64:h * 64 + 64],
                            ktab_v[:, c, h:h + 1], None, ALU.mult), dve, 1)
                    kds_done = dve.v
                    P.wait("pe", dve, kds_done)
                    for k in range(8):
                        h = hb + k
                        P.do("pe", lambda E, h=h, k=k, cl=cl: E.matmul(
                            pS(k), kds_v[:, h // 2 % 4, :],
                            vtm_v[:, cl, h * 64:h * 64 + 64],
                            start=True, stop=True), pe if k == 7 else None, 1)
                    mms_done = pe.v
                    P.wait("dve", pe, mms_done)
                    for k in range(8):
                        h = hb + k
                        if c > int(_o8.environ.get("SKIP_SF_AFTER", "99")):
                            continue
                        P.do("dve", lambda E, h=h, k=k, c=c: E.scalar_tensor_tensor(
                            Sf_v[hsl(h), h // 2, :], Sf_v[hsl(h), h // 2, :],
                            etotbc_v[hsl(h), h // 2, c:c + 1], pS(k)[hsl(h), :],
                            ALU.mult, ALU.add), dve, 1)
                    for k in range(8):
                        h = hb + k
                        if c > int(_o8.environ.get("SKIP_SBF_AFTER", "99")):
                            continue
                        P.do("dve", lambda E, h=h: E.tensor_copy(
                            sbf_v[hsl(h), h, hsl(h)], Sf_v[hsl(h), h // 2, :]), dve, 1)
                for i in range(4):
                    ph_use[i]["dve"] = dve.v
                py_use[0]["dve"] = py_use[1]["dve"] = dve.v
                ps1_use["dve"] = dve.v
            if L1A_TRUNC < 5:
                continue
            scan_done = dve.v
            sq_pe = [0, 0]
            pe_claim(ps1_use)
            P.wait("act", dve, scan_done)
            for kd in range(KD):
                par2 = kd % 2
                P.wait("act", pe, sq_pe[par2])
                P.do("act", lambda E, kd=kd, par2=par2: E.activation(
                    sqt[:, par2, :], obf_v[:, kd, :], AF.Square), act, 1)
                sqd = act.v
                P.wait("pe", act, sqd)
                P.do("pe", lambda E, par2=par2, kd=kd: E.matmul(
                    PS1[0:16, :], cones[:, :], sqt[:, par2, :],
                    start=(kd == 0), stop=(kd == KD - 1)), pe, 1)
                sq_pe[par2] = pe.v
            oss_done = pe.v
            P.wait("act", pe, oss_done)
            P.do("act", lambda E: E.activation(st16[0:16, 0, :], PS1[0:16, :], AF.Copy,
                                               scale=1.0 / 1024.0), act, 1)
            ps1_use["act"] = act.v
            P.do("act", lambda E: E.activation(st16[0:16, 0, :], st16[0:16, 0, :],
                                               AF.Ln, bias=fbc[0:16, 1:2]), act, 1)
            P.wait("act", pe, rows_war[0])
            P.do("act", lambda E: E.activation(rows_sm[0:1, 1, :], st16[0:1, 0, :],
                                               AF.Exp, scale=-0.5), act, 1)
            orstd_done = act.v
            P.wait("pe", act, orstd_done)
            pe_claim(py_use[0])
            P.do("pe", lambda E: E.matmul(PY[:, 0, :], conesrow_f[:, :],
                                          rows_sm[0:1, 1, :], start=True, stop=True),
                 pe, 1)
            ob = pe.v
            rows_war[0] = pe.v
            P.wait("dve", pe, ob)
            import os as _o9
            for kd in range(KD):
                if _o9.environ.get("SKIP_RMS") == "1":
                    continue
                P.do("dve", lambda E, kd=kd: E.scalar_tensor_tensor(
                    obf_v[:, kd, :], PY[:, 0, :], nwsb[:, kd:kd + 1], obf_v[:, kd, :],
                    ALU.mult, ALU.mult), dve, 1)
            py_use[0]["dve"] = dve.v
            on_done = dve.v
            par, wv_ = stream_w(wo_dr)
            P.wait("pe", wss[par], wv_)
            P.wait("pe", dve, on_done)
            for ec in range(KD):
                bk = ec % 4
                pe_claim(ph_use[bk])
                for kd in range(KD):
                    P.do("pe", lambda E, ec=ec, kd=kd, bk=bk, par=par: E.matmul(
                        PH[:, bk, :], wstream_v[:, par, kd, ec * 128:(ec + 1) * 128],
                        obf_v[:, kd, :], start=(kd == 0), stop=(kd == KD - 1)),
                        pe if kd == KD - 1 else None, 1)
                od = pe.v
                P.wait("dve", pe, od)
                P.do("dve", lambda E, ec=ec, bk=bk, wts=wts: E.tensor_tensor(
                    xsb[:, ec, wts], xsb[:, ec, wts], PH[:, bk, :], ALU.add), dve, 1)
                ph_use[bk]["dve"] = dve.v
            ws_pe_use[par] = pe.v

    emits = {"l0a": emit_l0a, "l0a_scores": emit_l0a, "l0a_f": emit_l0a_fused,
             "l0f": lambda: emit_ffn(0), "l1f": lambda: emit_ffn(1), "l1a": emit_l1a}
    for b in blocks:
        emits[b]()
        dump("x_" + b, xsb[:, :, :], [128, KD, T], BF16)

    P.wait("pool", pe)
    P.wait("pool", dve)
    P.wait("pool", act)
    P.do("pool", lambda E: E.dma_start(
        out=xout.rearrange("(kd p) t -> p kd t", p=128), in_=xsb[:, :, :]), ode, 16)
    P.wait("pool", ode)
    return P, dumps_made



def _consts():
    c = {}
    c["c_ones"] = np.ones((128, 16), np.float32)
    c["c_onesrow"] = np.ones((1, 128), np.float32)
    sel = np.zeros((16, 8, 128), np.float32)
    for kd in range(8):
        sel[2 * kd, kd, 0:64] = 1.0
        sel[2 * kd + 1, kd, 64:128] = 1.0
    c["c_sel"] = sel
    hs = np.zeros((128, 8, 16), np.float32)
    for kd in range(8):
        for p in range(128):
            hs[p, kd, 2 * kd + (p >= 64)] = 1.0
    c["c_headsel"] = hs
    ss = np.zeros((128, 16), np.float32)
    for p in range(128):
        ss[p, p % 16] = 1.0
    c["c_ss"] = ss
    eo = np.zeros((8, 8, 128), np.float32)
    for e in range(8):
        eo[e, e, :] = 1.0
    c["c_eones"] = eo
    c["c_id"] = np.eye(128, dtype=np.float32)
    m = np.zeros((128, 128), np.float32)
    for j in range(128):
        m[j, j:] = 1.0
    c["c_mask"] = m
    fb = np.zeros((128, 6), np.float32)
    fb[:, 0] = 1e-5
    fb[:, 1] = 1e-6
    fb[:, 2] = 1.0
    fb[0:64, 4] = 1.0
    fb[64:128, 5] = 1.0
    c["c_fb"] = fb
    return c


def _stage_weights(inp):
    st = {}
    A = np.ascontiguousarray

    def w1tile(W1):
        return A(W1.reshape(8, 8, 128, 32, 128).transpose(0, 3, 2, 1, 4))

    def w2tile(W2):
        return A(W2.reshape(8, 4, 8, 128, 8, 128).transpose(0, 1, 4, 3, 2, 5))

    blk = np.zeros((128, 3, 128), np.float32)
    for i, nm in enumerate(("L0_attn_Wq", "L0_attn_Wk", "L0_attn_Wv")):
        Wt = np.asarray(inp[nm], np.float32).T
        blk[0:64, i, 0:64] = Wt
        blk[64:128, i, 64:128] = Wt
    st["w_l0blk"] = blk
    qbd = np.zeros((128, 128), np.float32)
    kbd = np.zeros((128, 128), np.float32)
    for bb in range(2):
        sl = slice(64 * bb, 64 * bb + 64)
        qbd[sl, sl] = np.asarray(inp["L0_attn_Wq"], np.float32)
        kbd[sl, sl] = np.asarray(inp["L0_attn_Wk"], np.float32)
    st["w_l0qbd"] = qbd
    st["w_l0kbd"] = kbd
    st["w_l0wr"] = A(np.asarray(inp["L0_ffn_Wr"], np.float32).reshape(8, 128, 8)
                     .transpose(1, 0, 2))
    st["w_l0w1"] = w1tile(np.asarray(inp["L0_ffn_W1"], np.float32))
    st["w_l0w2"] = w2tile(np.asarray(inp["L0_ffn_W2"], np.float32))
    st["p_br0"] = np.asarray(inp["L0_ffn_br"], np.float32).reshape(8, 1)
    for nm, key in (("Wq", "wq"), ("Wk", "wk"), ("Wv", "wv"), ("Wo", "wo")):
        Wt = np.asarray(inp["L1_attn_" + nm], np.float32).T
        st["w_l1" + key] = A(Wt.reshape(8, 128, 1024).transpose(1, 0, 2))
    st["w_l1wf"] = A(np.asarray(inp["L1_attn_Wf"], np.float32).T
                     .reshape(8, 128, 16).transpose(1, 0, 2))
    st["p_delta"] = np.asarray(inp["L1_attn_delta"], np.float32).reshape(16, 1)
    st["p_nw"] = A(np.asarray(inp["L1_attn_norm_weight"], np.float32).reshape(8, 128).T)
    st["w_l1wr"] = A(np.asarray(inp["L1_ffn_Wr"], np.float32).reshape(8, 128, 8)
                     .transpose(1, 0, 2))
    st["w_l1w1"] = w1tile(np.asarray(inp["L1_ffn_W1"], np.float32))
    st["w_l1w2"] = w2tile(np.asarray(inp["L1_ffn_W2"], np.float32))
    st["p_br1"] = np.asarray(inp["L1_ffn_br"], np.float32).reshape(8, 1)
    st["p_b1"] = A(np.asarray(inp["L1_ffn_b1"], np.float32).reshape(8, 32, 128)
                   .transpose(2, 0, 1))
    st["p_b2"] = A(np.asarray(inp["L1_ffn_b2"], np.float32).reshape(8, 8, 128)
                   .transpose(2, 0, 1))
    st["p_lng"] = A(np.stack(
        [np.asarray(inp["L1_ffn_ln_g"], np.float32).reshape(8, 128).T,
         np.asarray(inp["L1_ffn_ln_b"], np.float32).reshape(8, 128).T], axis=1))
    return st


_PROG_CACHE = {}


def _get_prog(blocks, dump_names=()):
    key = (tuple(blocks), tuple(dump_names), L1A_TRUNC, LN_TRUNC)
    if key not in _PROG_CACHE:
        P, dumps = build_program(blocks, dump_names)
        nc = P.build()
        _PROG_CACHE[key] = (nc, dumps)
    return _PROG_CACHE[key]


def run_blocks(inp, blocks, xcur=None, dump_names=(), s_all=None):
    nc, dumps = _get_prog(tuple(blocks), tuple(dump_names))
    st = _stage_weights(inp)
    st.update(_consts())
    x = np.asarray(xcur if xcur is not None else inp["L0_attn_x"], np.float32)
    xall_fm = (np.ascontiguousarray(x.transpose(0, 2, 1))
               if any(b == "l0a_f" for b in blocks) else None)
    in_maps = []
    for ci in range(8):
        m = dict(st)
        m["xin"] = np.ascontiguousarray(x[ci].T)
        if "l0a" in blocks or "l0a_scores" in blocks or "l0a_f" in blocks:
            sel = np.zeros((128, 16), np.float32)
            for h in range(16):
                sel[16 * ci + h, h] = 1.0
            m["c_mysel"] = sel
        if "l0a_f" in blocks:
            mis = np.zeros((16, 8), np.float32)
            mis[:, ci] = 1.0
            m["c_misel"] = mis
            m["xall"] = xall_fm
        if s_all is not None:
            m["s_all"] = s_all
        in_maps.append(m)
    res = run_bass_kernel_spmd(nc, in_maps, core_ids=list(range(8)))
    keys = set()
    for r in res.results:
        keys |= set(r.keys())
    if "xout" in keys:
        out = np.stack([np.asarray(res.results[ci]["xout"]).T for ci in range(8)])
        out = out.astype(np.float32)
    else:
        out = None
    extra = {nm: [np.asarray(res.results[ci]["dump_" + nm]) for ci in range(8)]
             for nm in dumps}
    if "s_my" in keys:
        extra["s_my"] = [np.asarray(res.results[ci]["s_my"]) for ci in range(8)]
    return out, extra


def kernel(**inp):
    out, _ = run_blocks(inp, ("l0a_f", "l0f", "l1a", "l1f"))
    return out


# version 05
# speedup vs baseline: 1.5158x; 1.0403x over previous
import numpy as np
from contextlib import ExitStack

import concourse.bass as bass
import concourse.mybir as mybir
from concourse.bass_utils import run_bass_kernel_spmd

F32 = mybir.dt.float32
BF16 = mybir.dt.bfloat16
AF = mybir.ActivationFunctionType
ALU = mybir.AluOpType
AX = mybir.AxisListType

T = 2048
KD = 8
NT = 512
NTT = 4
NE = 8
NFC = 32
NFQ = 4
NW = 4
CPW = 4
L1A_TRUNC = 99
LN_TRUNC = 99


class Sem:
    __slots__ = ("h", "v")

    def __init__(self, h):
        self.h = h
        self.v = 0


class PB:
    ENG = {"pe": "tensor", "dve": "vector", "act": "scalar", "pool": "gpsimd", "sp": "sync"}

    def __init__(self):
        self.nc = bass.Bass("TRN2", target_bir_lowering=False)
        self.stk = ExitStack()
        self.q = {k: [] for k in self.ENG}
        self.sem = {}
        for s in ("pe", "dve", "act", "pool", "sp", "wde", "xde", "cc", "ode",
                  "w1a", "w1b", "w2a", "w2b", "wsa", "wsb", "sla", "slb", "wrs"):
            self.sem[s] = Sem(self.stk.enter_context(self.nc.semaphore(s)))

    def sb(self, name, shape, dt):
        return self.stk.enter_context(self.nc.sbuf_tensor(name, list(shape), dt))

    def ps(self, name, shape, dt=F32):
        return self.stk.enter_context(self.nc.psum_tensor(name, list(shape), dt))

    def wait(self, eng, sem, v=None):
        vv = sem.v if v is None else v
        if vv <= 0:
            return
        self.q[eng].append(lambda E, h=sem.h, vv=vv: E.wait_ge(h, vv))

    SELF_SER = ("dve", "act")

    def do(self, eng, fn, sem=None, k=1):
        if eng in self.SELF_SER:
            own = self.sem[eng]
            if own.v > 0:
                self.q[eng].append(lambda E, h=own.h, vv=own.v: E.wait_ge(h, vv))
            if sem is None:
                sem = own
        if sem is None:
            self.q[eng].append(fn)
        else:
            sem.v += k
            self.q[eng].append(lambda E, fn=fn, h=sem.h, k=k: fn(E).then_inc(h, k))

    def build(self):
        with self.nc.Block() as blk:
            for short, attr in self.ENG.items():
                getattr(blk, attr)(lambda E, lst=self.q[short]: [f(E) for f in lst])
        return self.nc


def _fs(ap):
    return ap[tuple([slice(None)] * len(ap.shape))]


def build_program(blocks=("l0a", "l0f", "l1a", "l1f"), dump_names=()):
    P = PB()
    S = P.sem
    pe, dve, act, pool, sp = S["pe"], S["dve"], S["act"], S["pool"], S["sp"]
    wde, xde, ode, ccs = S["wde"], S["xde"], S["ode"], S["cc"]
    w1s = [S["w1a"], S["w1b"]]
    w2s = [S["w2a"], S["w2b"]]
    wss = [S["wsa"], S["wsb"]]
    sls = [S["sla"], S["slb"]]
    need = set(blocks)
    dumps_made = []

    xin = P.nc.dram_tensor("xin", [1024, T], F32, kind="ExternalInput").ap()
    xout = P.nc.dram_tensor("xout", [1024, T], F32, kind="ExternalOutput").ap()
    dr = {}

    def din(nm, shape):
        dr[nm] = P.nc.dram_tensor(nm, list(shape), F32, kind="ExternalInput").ap()

    for nm, shp in [("c_ones", [128, 16]), ("c_onesrow", [1, 128]), ("c_sel", [16, 8, 128]),
                    ("c_headsel", [128, 8, 16]), ("c_ss", [128, 16]), ("c_eones", [8, 8, 128]),
                    ("c_id", [128, 128]), ("c_mask", [128, 128]),
                    ("c_fb", [128, 6])]:
        din(nm, shp)
    l0a_mode = ("scores" if "l0a_scores" in need else
                ("fused" if "l0a_f" in need else ("ext" if "l0a" in need else None)))
    if l0a_mode is not None:
        din("w_l0blk", [128, 3, 128])
        din("c_mysel", [128, 16])
        if l0a_mode == "scores":
            s_my = P.nc.dram_tensor("s_my", [16, T], F32, kind="ExternalOutput").ap()
        elif l0a_mode == "ext":
            din("s_all", [128, T])
            s_all = dr["s_all"]
        else:
            din("xall", [NE, 1024, T])
            din("w_l0qbd", [128, 128])
            din("w_l0kbd", [128, 128])
            din("c_misel", [16, 8])
    for L in (0, 1):
        if ("l0f" if L == 0 else "l1f") in need:
            din(f"w_l{L}wr", [128, 8, 8])
            din(f"w_l{L}w1", [NE, NFC, 128, 8, 128])
            din(f"w_l{L}w2", [NE, NFQ, 8, 128, 8, 128])
            din(f"p_br{L}", [8, 1])
    if "l1f" in need:
        din("p_b1", [128, 8, 32])
        din("p_b2", [128, 8, 8])
        din("p_lng", [128, 2, 8])
    if "l1a" in need:
        import os as _os0
        if _os0.environ.get("SKIP_WQKVO") != "1":
            for w in ("wq", "wk", "wv", "wo"):
                din("w_l1" + w, [128, 8, 1024])
        din("w_l1wf", [128, 8, 16])
        din("p_delta", [16, 1])
        din("p_nw", [128, 8])

    xsb = P.sb("xsb", [128, KD, T], BF16)
    xn = P.sb("xn", [128, KD, T], BF16)
    U = P.sb("U", [128, 40960], BF16)
    W = P.sb("W", [128, 16384], BF16)
    SCRF = P.sb("SCRF", [128, 4096], F32)
    rows_sm = P.sb("rows_sm", [128, 2, NT], F32)
    grow_sm = P.sb("grow_sm", [128, NT], BF16)
    sqt = P.sb("sqt", [128, 2, NT], BF16)
    cones = P.sb("cones", [128, 16], BF16)
    conesrow = P.sb("conesrow", [1, 128], BF16)
    conesrow_f = P.sb("conesrow_f", [1, 128], F32)
    selb = P.sb("selb", [16, 8, 128], BF16)
    headsb = P.sb("headsb", [128, 8, 16], BF16)
    ssb = P.sb("ssb", [128, 16], BF16)
    eones = P.sb("eones", [8, 8, 128], BF16)
    idb = P.sb("idb", [128, 128], BF16)
    idf = P.sb("idf", [128, 128], F32)
    maskt = P.sb("maskt", [128, 128], BF16)
    fbc = P.sb("fbc", [128, 6], F32)
    wrsb = P.sb("wrsb", [128, 8, 8], F32)
    brsb = P.sb("brsb", [8, 2], F32)
    lngb = P.sb("lngb", [128, 2, 8], F32)
    nwsb = P.sb("nwsb", [128, 8], F32)
    dlt = P.sb("dlt", [16, 1], F32)
    b1sb = P.sb("b1sb", [128, 8, 32], F32)
    b2sb = P.sb("b2sb", [128, 8, 8], F32)
    ohtm = P.sb("ohtm", [128, 8], BF16)
    gcolb = P.sb("gcolb", [128, 16], BF16)
    wfTsb = P.sb("wfTsb", [128, 8, 16], BF16)
    miselb = P.sb("miselb", [16, 8], F32)
    wcolf = P.sb("wcolf", [128, 1], F32)
    m11f = P.sb("m11f", [128, 1], F32)

    PH = P.ps("PH", [128, 4, 512])
    PY = P.ps("PY", [128, 2, 512])
    PS1 = P.ps("PS1", [128, 512])
    PSM = P.ps("PSM", [128, 1024], BF16)
    pt_oh = PSM[0:8, 0:128]
    pt_lb = PSM[:, 128:144]
    ptrow = PSM[0:1, 144:272]

    def view(base, off, shape):
        n = int(np.prod(shape))
        ap = base[:, off:off + n]
        if len(shape) > 1:
            names = " ".join(f"a{i}" for i in range(len(shape)))
            ap = ap.rearrange(f"p ({names}) -> p {names}",
                              **{f"a{i}": s for i, s in enumerate(shape)})
        return ap

    hq_v = view(U, 0, (8, T))
    yac_v = view(U, 8 * T, (8, T))
    gateb_v = view(U, 16 * T, (T,))
    gmask_v = view(U, 17 * T, (T,))
    ohfm_v = view(U, 18 * T, (T,))
    ohfm_ct = view(U, 18 * T, (16, 128))
    l0_q = view(U, 0, (NT,))
    l0_qk = view(U, NT, (NT,))
    l0_vs = view(U, 2 * NT, (NT,))
    l0_e = view(U, 3 * NT, (NT,))
    l0_sall = view(U, 4 * NT, (2, NT))
    l0_smine = view(U, 6 * NT, (T,))
    l0_probs = view(U, 6 * NT + T, (T,))
    l0_emine = view(U, 6 * NT + 2 * T, (2, NT))
    l0_inv = view(U, 6 * NT + 3 * T, (2, NT))
    l0_wblk = view(U, 6 * NT + 4 * T, (3, 128))
    f_xb = view(U, 16384, (2, 8, NT))
    f_z = view(U, 24576, (NT,))
    f_xz = view(U, 24576 + NT, (NT,))
    f_sq = view(U, 24576 + 2 * NT, (NT,))
    f_et = view(U, 24576 + 3 * NT, (NT,))
    f_em = view(U, 26624, (T,))
    f_mblk = view(U, 28672, (128,))
    f_mblkT = view(U, 28672 + 128, (128,))
    f_qbd = view(U, 28672 + 256, (128,))
    f_kbd = view(U, 28672 + 384, (128,))
    f_wsel = view(U, 28672 + 512, (8, 16))
    f_m2x = view(U, 28672 + 640, (16,))
    qfm_v = view(U, 0, (8, NT))
    kfm_v = view(U, 4096, (8, NT))
    ktm_v = view(U, 8192, (CPW, 1024))
    vtm_v = view(U, 12288, (CPW, 1024))
    obf_v = view(U, 16384, (8, NT))
    a2_v = view(U, 20480, (16, 128))
    kds_v = view(U, 22528, (4, 128))
    elbbc_v = view(U, 23040, (8, 128))
    lbbf_v = view(U, 24064, (T,))
    elbbf_v = view(U, 24064 + T, (T,))
    sbf_v = view(U, 24064 + 2 * T, (16, 128))
    etot_bf = view(U, 24064 + 2 * T + 2048, (16,))
    sfx_v = view(U, 24064 + 2 * T + 2048 + 16, (T,))
    kmsk_v = view(U, 24064 + 3 * T + 2048 + 16, (2, 8, NT))
    qs_v = view(U, 24064 + 2 * T + 2048 + 16, (4, 128))
    w1c_v = view(W, 0, (2, 8, 128))
    w2c_v = view(W, 2048, (2, 8, 128))
    wstream_v = view(W, 0, (2, 8, 1024))
    stg32 = view(SCRF, 0, (2, NT))
    st16 = view(SCRF, 1024, (2, NT))
    lftt_v = view(SCRF, 2048, (2, NT))
    cols_v = view(SCRF, 3072, (8,))
    ex_v = view(SCRF, 3072 + 8, (8,))
    secol_v = view(SCRF, 3072 + 32, (16,))
    vsump_v = view(SCRF, 3072 + 48, (KD, NTT))
    vsum8_v = view(SCRF, 3072 + 80, (KD,))
    vsc2_v = view(SCRF, 3072 + 88, (KD,))
    ginv_v = view(SCRF, 3072 + 96, (16,))
    enltm_v = view(SCRF, 2048, (16, 16))
    ktab_v = view(SCRF, 2048 + 256, (16, 16))
    Sf_v = view(SCRF, 2048 + 512, (8, 64))
    etotbc_v = view(SCRF, 2048 + 1024, (8, 16))
    totf_v = view(SCRF, 2048 + 1024 + 128, (16,))

    TTS = [slice(tt * NT, (tt + 1) * NT) for tt in range(NTT)]
    TTP = [slice(tp * 2 * NT, (tp + 1) * 2 * NT) for tp in range(2)]

    for nm, dst in [("c_ones", cones), ("c_onesrow", conesrow), ("c_sel", selb),
                    ("c_headsel", headsb), ("c_ss", ssb), ("c_eones", eones),
                    ("c_id", idb), ("c_mask", maskt)]:
        P.do("pool", lambda E, d=dst, s=dr[nm]: E.dma_start(out=_fs(d), in_=_fs(s)), wde, 16)
    P.do("sp", lambda E: E.dma_start(out=_fs(idf), in_=_fs(dr["c_id"])), xde, 16)
    P.do("sp", lambda E: E.dma_start(out=_fs(fbc), in_=_fs(dr["c_fb"])), xde, 16)
    P.do("sp", lambda E: E.dma_start(out=_fs(conesrow_f), in_=_fs(dr["c_onesrow"])), xde, 16)
    P.do("pool", lambda E: E.dma_start(
        out=xsb[:, :, :], in_=xin.rearrange("(kd p) t -> p kd t", p=128)), wde, 16)
    if l0a_mode is not None:
        P.do("pool", lambda E: E.dma_start(out=l0_wblk, in_=_fs(dr["w_l0blk"])), wde, 16)
        myselb = P.sb("myselb", [128, 16], BF16)
        P.do("pool", lambda E: E.dma_start(out=_fs(myselb), in_=_fs(dr["c_mysel"])), wde, 16)
    if l0a_mode == "fused":
        P.do("pool", lambda E: E.dma_start(out=f_qbd, in_=_fs(dr["w_l0qbd"])), wde, 16)
        P.do("pool", lambda E: E.dma_start(out=f_kbd, in_=_fs(dr["w_l0kbd"])), wde, 16)
        P.do("sp", lambda E: E.dma_start(out=_fs(miselb), in_=_fs(dr["c_misel"])), xde, 16)
    if "l1a" in need:
        import os as _os
        if _os.environ.get("SKIP_WF") != "1":
            P.do("pool", lambda E: E.dma_start(out=_fs(wfTsb), in_=_fs(dr["w_l1wf"])), wde, 16)
        if _os.environ.get("SKIP_DLT") != "1":
            P.do("sp", lambda E: E.dma_start(out=_fs(dlt), in_=_fs(dr["p_delta"])), xde, 16)
        if _os.environ.get("SKIP_NW") != "1":
            P.do("sp", lambda E: E.dma_start(out=_fs(nwsb), in_=_fs(dr["p_nw"])), xde, 16)
    if "l1f" in need:
        P.do("sp", lambda E: E.dma_start(out=_fs(b1sb), in_=_fs(dr["p_b1"])), xde, 16)
        P.do("sp", lambda E: E.dma_start(out=_fs(b2sb), in_=_fs(dr["p_b2"])), xde, 16)
        P.do("sp", lambda E: E.dma_start(out=_fs(lngb), in_=_fs(dr["p_lng"])), xde, 16)
    if "l0f" in need:
        P.do("sp", lambda E: E.dma_start(out=brsb[:, 0:1], in_=_fs(dr["p_br0"])), xde, 16)
    if "l1f" in need:
        P.do("sp", lambda E: E.dma_start(out=brsb[:, 1:2], in_=_fs(dr["p_br1"])), xde, 16)
    for e in ("pe", "dve", "act"):
        P.wait(e, wde)
        P.wait(e, xde)
    P.wait("sp", wde)
    P.wait("pool", xde)

    def dump(name, ap, shape, dt=F32):
        if name not in dump_names:
            return
        d = P.nc.dram_tensor("dump_" + name, list(shape), dt, kind="ExternalOutput").ap()
        for e in ("pe", "dve", "act"):
            P.do(e, lambda E: E.nop(), S[e], 1)
            P.wait("sp", S[e])
        P.do("sp", lambda E, d=d, ap=ap: E.dma_start(out=_fs(d), in_=ap), xde, 16)
        P.wait("sp", xde)
        dumps_made.append(name)

    ph_use = [{"dve": 0, "act": 0} for _ in range(4)]
    py_use = [{"dve": 0, "act": 0} for _ in range(2)]
    ps1_use = {"dve": 0, "act": 0}

    def pe_claim(d):
        P.wait("pe", dve, d["dve"])
        P.wait("pe", act, d["act"])

    pt_oh_use = [0]
    ptrow_use = [0]
    rows_war = [0]

    def emit_ln(src, out_bf, router=None, affine=False, eps=1e-5):
        P.wait("act", dve)
        P.wait("act", pe)
        P.wait("pe", dve)
        stg_pe_use = [0, 0]
        sq_pe_use = [0, 0]
        rows_pe_use = [rows_war[0]]
        for tt in range(NTT):
            if LN_TRUNC < 1 + tt:
                break
            ts = TTS[tt]
            for kd in range(KD):
                par = kd % 2
                P.wait("act", pe, sq_pe_use[par])
                P.do("act", lambda E, kd=kd, ts=ts, par=par: E.activation(
                    sqt[:, par, :], src[:, kd, ts], AF.Square), act, 1)
                sq_done = act.v
                if kd == 0:
                    pe_claim(ps1_use)
                    pe_claim(py_use[0])
                P.do("pe", lambda E, kd=kd, ts=ts: E.matmul(
                    PS1[0:16, :], cones[:, :], src[:, kd, ts],
                    start=(kd == 0), stop=(kd == KD - 1)))
                P.wait("pe", act, sq_done)
                P.do("pe", lambda E, kd=kd, par=par: E.matmul(
                    PY[0:16, 0, :], cones[:, :], sqt[:, par, :],
                    start=(kd == 0), stop=(kd == KD - 1)), pe, 1)
                sq_pe_use[par] = pe.v
            stats_done = pe.v
            if LN_TRUNC < 10 + tt:
                continue
            P.wait("act", pe, stats_done)
            P.do("act", lambda E: E.activation(st16[0:16, 0, :], PS1[0:16, :], AF.Copy,
                                               scale=1.0 / 1024.0), act, 1)
            ps1_use["act"] = act.v
            P.do("act", lambda E: E.activation(st16[0:16, 1, :], PY[0:16, 0, :], AF.Copy,
                                               scale=1.0 / 1024.0), act, 1)
            py_use[0]["act"] = act.v
            pe_claim_needed = py_use[1]
            P.wait("act", dve, pe_claim_needed["dve"])
            P.do("act", lambda E: E.activation(PY[0:16, 1, :], st16[0:16, 0, :],
                                               AF.Square), act, 1)
            m2_done = act.v
            P.wait("dve", act, m2_done)
            P.do("dve", lambda E: E.tensor_tensor(st16[0:16, 1, :], st16[0:16, 1, :],
                                                  PY[0:16, 1, :], ALU.subtract), dve, 1)
            py_use[1]["dve"] = dve.v
            var_done = dve.v
            P.wait("act", dve, var_done)
            P.do("act", lambda E: E.activation(st16[0:16, 1, :], st16[0:16, 1, :],
                                                AF.Ln, bias=fbc[0:16, 0:1]), act, 1)
            P.wait("act", pe, rows_pe_use[0])
            P.do("act", lambda E: E.activation(rows_sm[0:1, 1, :], st16[0:1, 1, :],
                                               AF.Exp, scale=-0.5), act, 1)
            P.do("act", lambda E: E.activation(rows_sm[0:1, 0, :], st16[0:1, 0, :],
                                               AF.Copy), act, 1)
            rows_done = act.v
            if LN_TRUNC < 20 + tt:
                continue
            P.wait("pe", act, rows_done)
            pe_claim(py_use[0])
            pe_claim(py_use[1])
            P.do("pe", lambda E: E.matmul(PY[:, 0, :], conesrow_f[:, :],
                                          rows_sm[0:1, 0, :], start=True, stop=True))
            P.do("pe", lambda E: E.matmul(PY[:, 1, :], conesrow_f[:, :],
                                          rows_sm[0:1, 1, :], start=True, stop=True), pe, 1)
            bc_done = pe.v
            rows_pe_use[0] = pe.v
            if LN_TRUNC < 30 + tt:
                continue
            P.wait("dve", pe, bc_done)
            for kd in range(KD):
                spar = kd % 2
                if router is not None:
                    P.wait("dve", pe, stg_pe_use[spar])
                P.do("dve", lambda E, kd=kd, ts=ts, spar=spar: E.scalar_tensor_tensor(
                    stg32[:, spar, :], PY[:, 0, :], -1.0, src[:, kd, ts],
                    ALU.mult, ALU.add), dve, 1)
                if router is not None:
                    P.do("dve", lambda E, spar=spar: E.tensor_tensor(
                        stg32[:, spar, :], stg32[:, spar, :], PY[:, 1, :],
                        ALU.mult), dve, 1)
                    P.do("dve", lambda E, kd=kd, ts=ts, spar=spar: E.tensor_copy(
                        out_bf[:, kd, ts], stg32[:, spar, :]), dve, 1)
                    st_done = dve.v
                    P.wait("pe", dve, st_done)
                    router(kd, tt, stg32[:, spar, :])
                    stg_pe_use[spar] = pe.v
                elif affine:
                    P.do("dve", lambda E, spar=spar: E.tensor_tensor(
                        stg32[:, spar, :], stg32[:, spar, :], PY[:, 1, :],
                        ALU.mult), dve, 1)
                    P.do("dve", lambda E, kd=kd, ts=ts, spar=spar: E.tensor_scalar(
                        out_bf[:, kd, ts], stg32[:, spar, :],
                        lngb[:, 0, kd:kd + 1], lngb[:, 1, kd:kd + 1],
                        ALU.mult, ALU.add), dve, 1)
                else:
                    P.do("dve", lambda E, kd=kd, ts=ts, spar=spar: E.tensor_tensor(
                        out_bf[:, kd, ts], stg32[:, spar, :], PY[:, 1, :],
                        ALU.mult), dve, 1)
            py_use[0]["dve"] = py_use[1]["dve"] = dve.v
        rows_war[0] = rows_pe_use[0]
        P.wait("pe", dve)

    def emit_l0a():
        emit_ln(xsb, xn)
        if l0a_mode == "scores":
            smine_f = view(SCRF, 2048, (T,))
            for tt in range(NTT):
                ts = TTS[tt]
                for kd in range(KD):
                    for b in range(2):
                        pe_claim(ph_use[b])
                    P.do("pe", lambda E, kd=kd, ts=ts: E.matmul(
                        PH[:, 0, :], l0_wblk[:, 0, :], xn[:, kd, ts],
                        start=True, stop=True))
                    P.do("pe", lambda E, kd=kd, ts=ts: E.matmul(
                        PH[:, 1, :], l0_wblk[:, 1, :], xn[:, kd, ts],
                        start=True, stop=True), pe, 1)
                    qk_mm = pe.v
                    P.wait("dve", pe, qk_mm)
                    P.do("dve", lambda E: E.tensor_copy(l0_q, PH[:, 0, :]), dve, 1)
                    P.do("dve", lambda E: E.tensor_tensor(l0_qk, l0_q, PH[:, 1, :],
                                                          ALU.mult), dve, 1)
                    qk_done = dve.v
                    ph_use[0]["dve"] = ph_use[1]["dve"] = dve.v
                    P.wait("pe", dve, qk_done)
                    if kd == 0:
                        pe_claim(ps1_use)
                    P.do("pe", lambda E, kd=kd: E.matmul(
                        PS1[0:16, :], headsb[:, kd, :], l0_qk,
                        start=(kd == 0), stop=(kd == KD - 1)),
                        pe if kd == KD - 1 else None, 1)
                s_done = pe.v
                P.wait("dve", pe, s_done)
                P.do("dve", lambda E, ts=ts: E.tensor_copy(smine_f[0:16, ts],
                                                           PS1[0:16, :]), dve, 1)
                ps1_use["dve"] = dve.v
            P.wait("sp", dve)
            P.do("sp", lambda E, smine_f=smine_f: E.dma_start(
                out=_fs(s_my), in_=smine_f[0:16, :]), xde, 16)
            P.wait("sp", xde)
            return
        for tt in range(NTT):
            ts = TTS[tt]
            for kd in range(KD):
                pe_claim(ph_use[2])
                P.do("pe", lambda E, kd=kd, ts=ts: E.matmul(
                    PH[:, 2, :], l0_wblk[:, 2, :], xn[:, kd, ts],
                    start=True, stop=True), pe, 1)
                v_mm = pe.v
                P.wait("act", pe, v_mm)
                P.do("act", lambda E, kd=kd, tt=tt: E.activation(
                    l0_vs, PH[:, 2, :], AF.Copy,
                    accum_out=vsump_v[:, kd, tt:tt + 1]), act, 1)
                ph_use[2]["act"] = act.v
        P.wait("dve", act)
        P.do("dve", lambda E: E.tensor_tensor(vsum8_v, vsump_v[:, :, 0],
                                              vsump_v[:, :, 1], ALU.add), dve, 1)
        P.do("dve", lambda E: E.tensor_tensor(vsc2_v, vsump_v[:, :, 2],
                                              vsump_v[:, :, 3], ALU.add), dve, 1)
        P.do("dve", lambda E: E.tensor_tensor(vsum8_v, vsum8_v, vsc2_v, ALU.add), dve, 1)
        sall_act_use = [0, 0]
        l0e_pe_use = [0]
        inv_dve_use = [0]
        for tt in range(NTT):
            ts = TTS[tt]
            par = tt % 2
            P.wait("pool", act, sall_act_use[par])
            P.do("pool", lambda E, ts=ts, par=par: E.dma_start(
                out=l0_sall[:, par, :], in_=s_all[:, ts]), sls[par], 16)
            P.wait("act", sls[par])
            P.wait("act", pe, l0e_pe_use[0])
            P.do("act", lambda E, par=par: E.activation(
                l0_e, l0_sall[:, par, :], AF.Exp, scale=1.0 / 32.0), act, 1)
            e_done = act.v
            sall_act_use[par] = act.v
            P.wait("pe", act, e_done)
            pe_claim(ps1_use)
            pe_claim(py_use[0])
            P.do("pe", lambda E: E.matmul(PS1[0:16, :], ssb[:, :], l0_e,
                                          start=True, stop=True))
            P.do("pe", lambda E: E.matmul(PY[0:16, 0, :], myselb[:, :], l0_e,
                                          start=True, stop=True), pe, 1)
            se_done = pe.v
            l0e_pe_use[0] = pe.v
            P.wait("act", pe, se_done)
            P.do("act", lambda E: E.activation(st16[0:16, 0, :], PS1[0:16, :],
                                               AF.Ln), act, 1)
            ps1_use["act"] = act.v
            P.wait("act", dve, inv_dve_use[0])
            P.do("act", lambda E: E.activation(st16[0:16, 1, :], st16[0:16, 0, :],
                                               AF.Exp, scale=-1.0), act, 1)
            inv_done = act.v
            P.wait("dve", act, inv_done)
            P.do("dve", lambda E, ts=ts: E.tensor_tensor(
                l0_probs[0:16, ts], PY[0:16, 0, :], st16[0:16, 1, :], ALU.mult), dve, 1)
            py_use[0]["dve"] = dve.v
            inv_dve_use[0] = dve.v
        probs_done = dve.v
        P.wait("pe", dve, probs_done)
        for tt in range(NTT):
            ts = TTS[tt]
            for kd in range(KD):
                bk = kd % 4
                pe_claim(ph_use[bk])
                P.do("pe", lambda E, kd=kd, ts=ts, bk=bk: E.matmul(
                    PH[:, bk, :], selb[:, kd, :], l0_probs[0:16, ts],
                    start=True, stop=True), pe, 1)
                mm_done = pe.v
                P.wait("dve", pe, mm_done)
                P.do("dve", lambda E, kd=kd, ts=ts, bk=bk: E.scalar_tensor_tensor(
                    xsb[:, kd, ts], PH[:, bk, :], vsum8_v[:, kd:kd + 1], xsb[:, kd, ts],
                    ALU.mult, ALU.add), dve, 1)
                ph_use[bk]["dve"] = dve.v

    def emit_l0a_fused():
        emit_ln(xsb, xn)
        seacc = view(SCRF, 2048, (T,))
        P.wait("pe", wde)
        P.wait("pe", xde)
        pe_claim(ph_use[0])
        pe_claim(ph_use[1])
        P.do("pe", lambda E: E.matmul(PH[:, 0, 0:128], f_qbd, f_kbd,
                                      start=True, stop=True))
        P.do("pe", lambda E: E.matmul(PH[:, 1, 0:128], f_kbd, f_qbd,
                                      start=True, stop=True), pe, 1)
        mb = pe.v
        P.wait("dve", pe, mb)
        P.do("dve", lambda E: E.tensor_copy(f_mblk, PH[:, 0, 0:128]), dve, 1)
        P.do("dve", lambda E: E.tensor_copy(f_mblkT, PH[:, 1, 0:128]), dve, 1)
        ph_use[0]["dve"] = ph_use[1]["dve"] = dve.v
        mev = dve.v
        P.wait("pe", dve, mev)
        pe_claim(py_use[0])
        pe_claim(py_use[1])
        P.do("pe", lambda E: E.matmul(PY[:, 0, 0:1], f_mblk, cones[:, 0:1],
                                      start=True, stop=True))
        P.do("pe", lambda E: E.matmul(PY[:, 1, 0:1], f_mblkT, cones[:, 0:1],
                                      start=True, stop=True), pe, 1)
        wv_ = pe.v
        P.wait("dve", pe, wv_)
        P.do("dve", lambda E: E.tensor_copy(wcolf[:, :], PY[:, 0, 0:1]), dve, 1)
        P.do("dve", lambda E: E.tensor_tensor(wcolf[:, :], wcolf[:, :],
                                              PY[:, 1, 0:1], ALU.add), dve, 1)
        py_use[0]["dve"] = py_use[1]["dve"] = dve.v
        P.do("dve", lambda E: E.tensor_copy(f_m2x[:, 1:2], wcolf[:, :]), dve, 1)
        P.do("dve", lambda E: E.tensor_copy(f_m2x[:, 2:3], fbc[:, 4:5]), dve, 1)
        for kd in range(KD):
            P.do("dve", lambda E, kd=kd: E.tensor_scalar(
                f_wsel[:, kd, :], headsb[:, kd, :], wcolf[:, 0:1], None,
                ALU.mult), dve, 1)
        wsel_done = dve.v
        P.wait("pe", dve, wsel_done)
        pe_claim(ps1_use)
        P.do("pe", lambda E: E.matmul(PS1[0:1, 0:1], f_m2x[:, 1:2], f_m2x[:, 2:3],
                                      start=True, stop=True), pe, 1)
        m2s = pe.v
        P.wait("dve", pe, m2s)
        P.do("dve", lambda E: E.tensor_copy(f_m2x[0:1, 0:1], PS1[0:1, 0:1]), dve, 1)
        ps1_use["dve"] = dve.v
        m2e = dve.v
        P.wait("pe", dve, m2e)
        pe_claim(py_use[0])
        P.do("pe", lambda E: E.matmul(PY[:, 0, 0:1], conesrow[:, :],
                                      f_m2x[0:1, 0:1], start=True, stop=True), pe, 1)
        m2b = pe.v
        P.wait("dve", pe, m2b)
        P.do("dve", lambda E: E.tensor_scalar(m11f[:, :], PY[:, 0, 0:1], 0.5, None,
                                              ALU.mult), dve, 1)
        py_use[0]["dve"] = dve.v
        P.do("dve", lambda E: E.memset(seacc[0:16, :], 0.0), dve, 1)
        P.do("dve", lambda E: E.memset(f_em[0:16, :], 0.0), dve, 1)
        xb_pool_use = [0, 0]
        xb_pe_use = [0, 0]
        for ci in range(NE):
            for tt in range(NTT):
                ts = TTS[tt]
                par = (ci * NTT + tt) % 2
                P.wait("pool", dve, xb_pool_use[par])
                P.wait("pool", pe, xb_pe_use[par])
                P.do("pool", lambda E, ci=ci, ts=ts, par=par: E.dma_start(
                    out=f_xb[:, par, :, :],
                    in_=dr["xall"][ci].rearrange("(kd p) t -> p kd t", p=128)[:, :, ts]),
                    sls[par], 16)
                P.wait("pe", sls[par])
                last_stat = [0]
                for kd in range(KD):
                    zb = kd % 2
                    pe_claim(ph_use[zb])
                    if kd == 0:
                        pe_claim(ps1_use)
                        pe_claim(py_use[0])
                        pe_claim(py_use[1])
                        pe_claim(ph_use[2])
                    P.do("pe", lambda E, kd=kd, zb=zb, par=par: E.matmul(
                        PH[:, zb, :], f_mblkT, f_xb[:, par, kd, :],
                        start=True, stop=True), pe, 1)
                    zmm = pe.v
                    P.wait("dve", pe, zmm)
                    P.wait("dve", pe, last_stat[0])
                    P.do("dve", lambda E, kd=kd, zb=zb, par=par: E.tensor_tensor(
                        f_xz, f_xb[:, par, kd, :], PH[:, zb, :], ALU.mult), dve, 1)
                    ph_use[zb]["dve"] = dve.v
                    P.do("dve", lambda E, kd=kd, par=par: E.tensor_tensor(
                        f_sq, f_xb[:, par, kd, :], f_xb[:, par, kd, :],
                        ALU.mult), dve, 1)
                    sx = dve.v
                    P.wait("pe", dve, sx)
                    P.do("pe", lambda E, kd=kd, par=par: E.matmul(
                        PS1[0:16, :], cones[:, :], f_xb[:, par, kd, :],
                        start=(kd == 0), stop=(kd == KD - 1)))
                    P.do("pe", lambda E, kd=kd: E.matmul(
                        PY[0:16, 0, :], cones[:, :], f_sq,
                        start=(kd == 0), stop=(kd == KD - 1)))
                    P.do("pe", lambda E, kd=kd: E.matmul(
                        PY[0:16, 1, :], headsb[:, kd, :], f_xz,
                        start=(kd == 0), stop=(kd == KD - 1)))
                    P.do("pe", lambda E, kd=kd, par=par: E.matmul(
                        PH[0:16, 2, 0:512], f_wsel[:, kd, :], f_xb[:, par, kd, :],
                        start=(kd == 0), stop=(kd == KD - 1)), pe, 1)
                    last_stat[0] = pe.v
                stats_done = pe.v
                xb_pool_use[par] = dve.v
                xb_pe_use[par] = pe.v
                P.wait("act", pe, stats_done)
                P.do("act", lambda E: E.activation(st16[0:16, 0, :], PS1[0:16, :],
                                                   AF.Copy, scale=1.0 / 1024.0), act, 1)
                P.do("act", lambda E: E.activation(st16[0:16, 1, :], PY[0:16, 0, :],
                                                   AF.Copy, scale=1.0 / 1024.0), act, 1)
                P.wait("act", dve)
                P.do("act", lambda E: E.activation(PH[0:16, 3, 0:512], st16[0:16, 0, :],
                                                   AF.Square), act, 1)
                m2d = act.v
                P.wait("dve", act, m2d)
                P.do("dve", lambda E: E.tensor_tensor(st16[0:16, 1, :], st16[0:16, 1, :],
                                                      PH[0:16, 3, 0:512],
                                                      ALU.subtract), dve, 1)
                vd = dve.v
                P.wait("act", dve, vd)
                P.do("act", lambda E: E.activation(st16[0:16, 1, :], st16[0:16, 1, :],
                                                   AF.Ln, bias=fbc[0:16, 0:1]), act, 1)
                P.do("act", lambda E: E.activation(st16[0:16, 1, :], st16[0:16, 1, :],
                                                   AF.Exp, scale=-1.0), act, 1)
                ivd = act.v
                P.wait("dve", act, ivd)
                P.do("dve", lambda E: E.tensor_tensor(stg32[0:16, 0, :], st16[0:16, 0, :],
                                                      PH[0:16, 2, 0:512],
                                                      ALU.mult), dve, 1)
                P.do("dve", lambda E: E.tensor_tensor(stg32[0:16, 0, :], PY[0:16, 1, :],
                                                      stg32[0:16, 0, :],
                                                      ALU.subtract), dve, 1)
                P.do("dve", lambda E: E.scalar_tensor_tensor(
                    stg32[0:16, 0, :], PH[0:16, 3, 0:512], m11f[0:16, 0:1],
                    stg32[0:16, 0, :], ALU.mult, ALU.add), dve, 1)
                P.do("dve", lambda E: E.tensor_tensor(stg32[0:16, 0, :],
                                                      stg32[0:16, 0, :],
                                                      st16[0:16, 1, :], ALU.mult),
                     dve, 1)
                sdone = dve.v
                ps1_use["dve"] = py_use[0]["dve"] = py_use[1]["dve"] = dve.v
                ph_use[2]["dve"] = ph_use[3]["dve"] = dve.v
                ps1_use["act"] = py_use[0]["act"] = act.v
                P.wait("act", dve, sdone)
                P.do("act", lambda E: E.activation(f_et[0:16, :], stg32[0:16, 0, :],
                                                   AF.Exp, scale=1.0 / 32.0), act, 1)
                etd = act.v
                P.wait("dve", act, etd)
                P.do("dve", lambda E, ts=ts: E.tensor_tensor(
                    seacc[0:16, ts], seacc[0:16, ts], f_et[0:16, :], ALU.add), dve, 1)
                P.do("dve", lambda E, ci=ci, ts=ts: E.scalar_tensor_tensor(
                    f_em[0:16, ts], f_et[0:16, :], miselb[:, ci:ci + 1],
                    f_em[0:16, ts], ALU.mult, ALU.add), dve, 1)
        for tt in range(NTT):
            ts = TTS[tt]
            P.wait("act", dve)
            P.do("act", lambda E, ts=ts: E.activation(st16[0:16, 0, :], seacc[0:16, ts],
                                                      AF.Ln), act, 1)
            P.do("act", lambda E: E.activation(st16[0:16, 0, :], st16[0:16, 0, :],
                                               AF.Exp, scale=-1.0), act, 1)
            ivse = act.v
            P.wait("dve", act, ivse)
            P.do("dve", lambda E, ts=ts: E.tensor_tensor(
                l0_probs[0:16, ts], f_em[0:16, ts], st16[0:16, 0, :], ALU.mult),
                dve, 1)
        probs_done = dve.v
        for tt in range(NTT):
            ts = TTS[tt]
            for kd in range(KD):
                pe_claim(ph_use[2])
                P.do("pe", lambda E, kd=kd, ts=ts: E.matmul(
                    PH[:, 2, :], l0_wblk[:, 2, :], xn[:, kd, ts],
                    start=True, stop=True), pe, 1)
                v_mm = pe.v
                P.wait("act", pe, v_mm)
                P.do("act", lambda E, kd=kd, tt=tt: E.activation(
                    l0_vs, PH[:, 2, :], AF.Copy,
                    accum_out=vsump_v[:, kd, tt:tt + 1]), act, 1)
                ph_use[2]["act"] = act.v
        P.wait("dve", act)
        P.do("dve", lambda E: E.tensor_tensor(vsum8_v, vsump_v[:, :, 0],
                                              vsump_v[:, :, 1], ALU.add), dve, 1)
        P.do("dve", lambda E: E.tensor_tensor(vsc2_v, vsump_v[:, :, 2],
                                              vsump_v[:, :, 3], ALU.add), dve, 1)
        P.do("dve", lambda E: E.tensor_tensor(vsum8_v, vsum8_v, vsc2_v, ALU.add),
             dve, 1)
        vs_done = dve.v
        P.wait("pe", dve, max(probs_done, vs_done))
        for tt in range(NTT):
            ts = TTS[tt]
            for kd in range(KD):
                bk = kd % 4
                pe_claim(ph_use[bk])
                P.do("pe", lambda E, kd=kd, ts=ts, bk=bk: E.matmul(
                    PH[:, bk, :], selb[:, kd, :], l0_probs[0:16, ts],
                    start=True, stop=True), pe, 1)
                mm_done = pe.v
                P.wait("dve", pe, mm_done)
                P.do("dve", lambda E, kd=kd, ts=ts, bk=bk: E.scalar_tensor_tensor(
                    xsb[:, kd, ts], PH[:, bk, :], vsum8_v[:, kd:kd + 1], xsb[:, kd, ts],
                    ALU.mult, ALU.add), dve, 1)
                ph_use[bk]["dve"] = dve.v

    def emit_ffn(L):
        P.wait("sp", pe)
        P.do("sp", lambda E: E.dma_start(out=_fs(wrsb), in_=_fs(dr[f"w_l{L}wr"])),
             S["wrs"], 16)
        P.wait("pe", S["wrs"])

        def router(kd, tt, stage_ap):
            if kd == 0:
                pe_claim(ph_use[tt])
            P.do("pe", lambda E, kd=kd, tt=tt, ap=stage_ap: E.matmul(
                PH[0:8, tt, :], wrsb[:, kd, :], ap,
                start=(kd == 0), stop=(kd == KD - 1)), pe, 1)

        emit_ln(xsb, xn, router=router)
        if L == 1:
            pass
        for tt in range(NTT):
            ts = TTS[tt]
            P.wait("dve", pe)
            P.do("dve", lambda E, tt=tt: E.tensor_scalar(
                lftt_v[0:8, tt % 2, :], PH[0:8, tt, :], brsb[:, L:L + 1], None,
                ALU.add), dve, 1)
            ph_use[tt]["dve"] = dve.v
            lf_done = dve.v
            for ci in range(4):
                ct = tt * 4 + ci
                cs = slice(ci * 128, (ci + 1) * 128)
                slot = ci % 2
                P.wait("pe", dve, lf_done)
                pe_claim(py_use[slot])
                P.do("pe", lambda E, cs=cs, slot=slot, tt=tt: E.transpose(
                    PY[:, slot, 0:8], lftt_v[0:8, tt % 2, cs], idf[0:8, 0:8]), pe, 1)
                tr_done = pe.v
                P.wait("dve", pe, tr_done)
                P.do("dve", lambda E, slot=slot: E.tensor_reduce(
                    cols_v[:, 0:1], PY[:, slot, 0:8], AX.X, ALU.max), dve, 1)
                if L == 1:
                    P.do("dve", lambda E, slot=slot: E.tensor_reduce(
                        cols_v[:, 1:2], PY[:, slot, 0:8], AX.X, ALU.max,
                        negate=True), dve, 1)
                P.do("dve", lambda E, slot=slot: E.tensor_scalar(
                    ohtm[:, :], PY[:, slot, 0:8], cols_v[:, 0:1], None,
                    ALU.is_equal), dve, 1)
                oh_done = dve.v
                py_use[slot]["dve"] = dve.v
                if L == 0:
                    P.wait("act", dve, oh_done)
                    P.do("act", lambda E, ct=ct: E.activation(
                        gcolb[:, ct:ct + 1], cols_v[:, 0:1], AF.Sigmoid), act, 1)
                else:
                    P.wait("act", dve, oh_done)
                    P.do("act", lambda E, slot=slot: E.activation(
                        ex_v[:, :], PY[:, slot, 0:8], AF.Exp,
                        bias=cols_v[:, 1:2]), act, 1)
                    exd = act.v
                    py_use[slot]["act"] = act.v
                    P.wait("dve", act, exd)
                    P.do("dve", lambda E, ct=ct: E.tensor_reduce(
                        secol_v[:, ct:ct + 1], ex_v[:, :], AX.X, ALU.add), dve, 1)
                P.wait("pe", dve, oh_done)
                P.wait("pe", dve, pt_oh_use[0])
                P.do("pe", lambda E: E.transpose(pt_oh[:, :], ohtm[:, :], idb[:, :]), pe, 1)
                trb = pe.v
                P.wait("dve", pe, trb)
                P.do("dve", lambda E, ct=ct: E.tensor_copy(
                    ohfm_ct[0:8, ct, :], pt_oh[0:8, :]), dve, 1)
                pt_oh_use[0] = dve.v
        if L == 1:
            P.do("dve", lambda E: E.reciprocal(ginv_v[:, :], secol_v[:, :]), dve, 1)
            P.do("dve", lambda E: E.tensor_copy(gcolb[:, :], ginv_v[:, :]), dve, 1)
        g_done = dve.v
        P.wait("pe", dve, g_done)
        P.wait("pe", act)
        grow_pe_use = [0]
        for tt in range(NTT):
            for ci in range(4):
                ct = tt * 4 + ci
                P.wait("pe", dve, ptrow_use[0])
                P.do("pe", lambda E, ct=ct: E.transpose(
                    ptrow[:, :], gcolb[:, ct:ct + 1], idb[:, :]), pe, 1)
                trg = pe.v
                P.wait("dve", pe, trg)
                P.wait("dve", pe, grow_pe_use[0])
                P.do("dve", lambda E, ci=ci: E.tensor_copy(
                    grow_sm[0:1, ci * 128:(ci + 1) * 128], ptrow[0:1, :]), dve, 1)
                ptrow_use[0] = dve.v
            gr = dve.v
            P.wait("pe", dve, gr)
            pe_claim(ps1_use)
            P.do("pe", lambda E: E.matmul(PS1[:, :], conesrow[:, :], grow_sm[0:1, :],
                                          start=True, stop=True), pe, 1)
            gb = pe.v
            grow_pe_use[0] = pe.v
            P.wait("dve", pe, gb)
            P.do("dve", lambda E, tt=tt: E.tensor_copy(gateb_v[:, TTS[tt]], PS1[:, :]),
                 dve, 1)
            ps1_use["dve"] = dve.v
        if L == 1:
            emit_ln(xn, xn, affine=True)
        oh_ready = dve.v
        w1_dr, w2_dr = dr[f"w_l{L}w1"], dr[f"w_l{L}w2"]
        w1_pe_use = [pe.v, pe.v]
        w2_pe_use = [pe.v, pe.v]
        w1n, w2n = [0], [0]
        for e in range(NE):
            P.wait("pe", dve, oh_ready)
            for tt in range(NTT):
                pe_claim(ps1_use)
                P.do("pe", lambda E, e=e, tt=tt: E.matmul(
                    PS1[:, :], eones[:, e, :], ohfm_v[0:8, TTS[tt]],
                    start=True, stop=True), pe, 1)
                gm = pe.v
                P.wait("dve", pe, gm)
                P.do("dve", lambda E, tt=tt: E.tensor_tensor(
                    gmask_v[:, TTS[tt]], PS1[:, :], gateb_v[:, TTS[tt]], ALU.mult),
                    dve, 1)
                ps1_use["dve"] = dve.v
            gmask_done = dve.v
            for fq in range(NFQ):
                for fc8 in range(8):
                    fc = fq * 8 + fc8
                    par = w1n[0] % 2
                    P.wait("pool", pe, w1_pe_use[par])
                    P.do("pool", lambda E, e=e, fc=fc, par=par: E.dma_start(
                        out=w1c_v[:, par, :, :], in_=w1_dr[e, fc, :, :, :]),
                        w1s[par], 16)
                    w1n[0] += 1
                    P.wait("pe", w1s[par])
                    for tp in range(2):
                        for tt in (2 * tp, 2 * tp + 1):
                            pe_claim(ph_use[tt])
                            for kd in range(KD):
                                P.do("pe", lambda E, kd=kd, tt=tt, par=par: E.matmul(
                                    PH[:, tt, :], w1c_v[:, par, kd, :], xn[:, kd, TTS[tt]],
                                    start=(kd == 0), stop=(kd == KD - 1)),
                                    pe if kd == KD - 1 else None, 1)
                        mmv = pe.v
                        P.wait("act", pe, mmv)
                        php = PH[:, 2 * tp:2 * tp + 2, :].rearrange("p a b -> p (a b)")
                        if L == 0:
                            P.do("act", lambda E, fc8=fc8, tp=tp, php=php: E.activation(
                                hq_v[:, fc8, TTP[tp]], php, AF.Gelu), act, 1)
                        else:
                            P.do("act", lambda E, fc8=fc8, tp=tp, e=e, fc=fc, php=php:
                                 E.activation(
                                     hq_v[:, fc8, TTP[tp]], php, AF.Relu,
                                     bias=b1sb[:, e, fc:fc + 1]), act, 1)
                        ph_use[2 * tp]["act"] = act.v
                        ph_use[2 * tp + 1]["act"] = act.v
                    w1_pe_use[par] = pe.v
                h_done = act.v
                for dout in range(8):
                    par = w2n[0] % 2
                    P.wait("pool", pe, w2_pe_use[par])
                    P.do("pool", lambda E, e=e, fq=fq, dout=dout, par=par: E.dma_start(
                        out=w2c_v[:, par, :, :], in_=w2_dr[e, fq, dout, :, :, :]),
                        w2s[par], 16)
                    w2n[0] += 1
                    P.wait("pe", w2s[par])
                    P.wait("pe", act, h_done)
                    if dout == 0 and fq == 0:
                        P.wait("pe", dve, gmask_done)
                    for tp in range(2):
                        for tt in (2 * tp, 2 * tp + 1):
                            pe_claim(ph_use[tt])
                            for fc8 in range(8):
                                P.do("pe", lambda E, fc8=fc8, tt=tt, par=par: E.matmul(
                                    PH[:, tt, :], w2c_v[:, par, fc8, :],
                                    hq_v[:, fc8, TTS[tt]],
                                    start=(fc8 == 0), stop=(fc8 == 7)),
                                    pe if fc8 == 7 else None, 1)
                        mmv = pe.v
                        P.wait("dve", pe, mmv)
                        php = PH[:, 2 * tp:2 * tp + 2, :].rearrange("p a b -> p (a b)")
                        if fq == 0:
                            P.do("dve", lambda E, dout=dout, tp=tp, php=php:
                                 E.tensor_copy(yac_v[:, dout, TTP[tp]], php), dve, 1)
                        elif fq < 3 or L == 0:
                            P.do("dve", lambda E, dout=dout, tp=tp, php=php:
                                 E.tensor_tensor(
                                     yac_v[:, dout, TTP[tp]], yac_v[:, dout, TTP[tp]],
                                     php, ALU.add), dve, 1)
                        else:
                            P.do("dve", lambda E, dout=dout, tp=tp, e=e, php=php:
                                 E.scalar_tensor_tensor(
                                     yac_v[:, dout, TTP[tp]], php,
                                     b2sb[:, e, dout:dout + 1], yac_v[:, dout, TTP[tp]],
                                     ALU.add, ALU.add), dve, 1)
                        ph_use[2 * tp]["dve"] = dve.v
                        ph_use[2 * tp + 1]["dve"] = dve.v
                    w2_pe_use[par] = pe.v
            for dout in range(8):
                sl = dout % 2
                P.do("dve", lambda E, dout=dout, sl=sl: E.tensor_tensor(
                    hq_v[:, sl, :], yac_v[:, dout, :], gmask_v[:, :],
                    ALU.mult), dve, 1)
                P.do("dve", lambda E, dout=dout, sl=sl: E.tensor_tensor(
                    xsb[:, dout, :], xsb[:, dout, :], hq_v[:, sl, :],
                    ALU.add), dve, 1)

    def emit_l1a():
        if L1A_TRUNC < 0:
            return
        emit_ln(xsb, xn)
        if L1A_TRUNC < 1:
            return
        P.do("dve", lambda E: E.memset(Sf_v, 0.0), dve, 1)
        P.do("dve", lambda E: E.memset(sbf_v, 0.0), dve, 1)
        P.do("dve", lambda E: E.memset(kds_v, 0.0), dve, 1)
        for tt in range(NTT):
            ts = TTS[tt]
            pe_claim(ps1_use)
            for kd in range(KD):
                P.do("pe", lambda E, kd=kd, ts=ts: E.matmul(
                    PS1[0:16, :], wfTsb[:, kd, :], xn[:, kd, ts],
                    start=(kd == 0), stop=(kd == KD - 1)),
                    pe if kd == KD - 1 else None, 1)
            f_done = pe.v
            P.wait("dve", pe, f_done)
            P.do("dve", lambda E: E.tensor_scalar(st16[0:16, 1, :], PS1[0:16, :],
                                                  dlt[:, 0:1], 0.0, ALU.add, ALU.min),
                 dve, 1)
            min_done = dve.v
            ps1_use["dve"] = dve.v
            P.wait("act", pe, f_done)
            P.wait("act", dve, min_done)
            P.do("act", lambda E: E.activation(st16[0:16, 0, :], PS1[0:16, :], AF.Abs,
                                               bias=dlt[:, 0:1]), act, 1)
            P.do("act", lambda E: E.activation(st16[0:16, 0, :], st16[0:16, 0, :],
                                               AF.Exp, scale=-1.0), act, 1)
            P.do("act", lambda E: E.activation(st16[0:16, 0, :], st16[0:16, 0, :],
                                               AF.Ln, bias=fbc[0:16, 2:3]), act, 1)
            t3_done = act.v
            ps1_use["act"] = t3_done
            P.wait("dve", act, t3_done)
            P.do("dve", lambda E: E.tensor_tensor(st16[0:16, 1, :], st16[0:16, 1, :],
                                                  st16[0:16, 0, :], ALU.subtract), dve, 1)
            for ci in range(4):
                c0 = ci * 128
                P.do("dve", lambda E, c0=c0, ts=ts: E.tensor_tensor_scan(
                    lbbf_v[0:16, ts.start + c0:ts.start + c0 + 128],
                    st16[0:16, 1, c0:c0 + 128], st16[0:16, 1, c0:c0 + 128],
                    0.0, ALU.add, ALU.bypass), dve, 1)
        lb_done = dve.v
        if L1A_TRUNC < 2:
            return
        P.wait("act", dve, lb_done)
        P.do("act", lambda E: E.activation(elbbf_v[0:16, :], lbbf_v[0:16, :], AF.Exp),
             act, 1)
        P.do("dve", lambda E: E.tensor_copy(totf_v[0:16, :], lbbf_v[0:16, 127::128]),
             dve, 1)
        tot_done = dve.v
        P.wait("act", dve, tot_done)
        P.do("act", lambda E: E.activation(etot_bf[0:16, :], totf_v[0:16, :], AF.Exp),
             act, 1)
        etot_done = act.v
        for c in range(16):
            P.do("dve", lambda E, c=c: E.tensor_scalar(
                sfx_v[0:16, c * 128:(c + 1) * 128], lbbf_v[0:16, c * 128:(c + 1) * 128],
                totf_v[0:16, c:c + 1], -1.0, ALU.subtract, ALU.mult), dve, 1)
        sfx_done = dve.v
        P.wait("pe", act, etot_done)
        for kd in range(KD):
            bk = kd % 2
            pe_claim(py_use[bk])
            P.do("pe", lambda E, kd=kd, bk=bk: E.matmul(
                PY[:, bk, 0:16], selb[:, kd, :], etot_bf[0:16, :],
                start=True, stop=True), pe, 1)
            bc = pe.v
            P.wait("dve", pe, bc)
            P.do("dve", lambda E, kd=kd, bk=bk: E.tensor_copy(
                etotbc_v[:, kd, :], PY[:, bk, 0:16]), dve, 1)
            py_use[bk]["dve"] = dve.v
        P.wait("pe", dve, sfx_done)
        for c in range(16):
            P.wait("pe", act)
            P.do("pe", lambda E, c=c: E.transpose(
                pt_lb[:, :], lbbf_v[0:16, c * 128:(c + 1) * 128], idb[0:16, 0:16]), pe, 1)
            t1 = pe.v
            P.wait("act", pe, t1)
            P.do("act", lambda E, c=c: E.activation(enltm_v[:, c, :], pt_lb[:, :],
                                                    AF.Exp, scale=-1.0), act, 1)
            en_done = act.v
            P.wait("pe", act, en_done)
            P.do("pe", lambda E, c=c: E.transpose(
                pt_lb[:, :], sfx_v[0:16, c * 128:(c + 1) * 128], idb[0:16, 0:16]), pe, 1)
            t2 = pe.v
            P.wait("act", pe, t2)
            P.do("act", lambda E, c=c: E.activation(ktab_v[:, c, :], pt_lb[:, :],
                                                    AF.Exp), act, 1)
        tables_act = act.v
        if L1A_TRUNC < 3:
            return

        wq_dr, wk_dr, wv_dr, wo_dr = (dr["w_l1wq"], dr["w_l1wk"], dr["w_l1wv"],
                                      dr["w_l1wo"])
        ws_pe_use = [pe.v, pe.v]
        wsn = [0]

        def stream_w(src):
            par = wsn[0] % 2
            P.wait("pool", pe, ws_pe_use[par])
            P.do("pool", lambda E, src=src, par=par: E.dma_start(
                out=wstream_v[:, par, :, :], in_=_fs(src)), wss[par], 16)
            wsn[0] += 1
            return par, wss[par].v

        def pA(h):
            return PH[:, h // 4, (h % 4) * 128:(h % 4) * 128 + 128]

        def pO(h):
            half, r = 64 * (h % 2), h // 2
            return PY[half:half + 64, r // 4, (r % 4) * 128:(r % 4) * 128 + 128]

        def pS(h):
            half, r = 64 * (h % 2), h // 2
            return PS1[half:half + 64, r * 64:r * 64 + 64]

        def hsl(h):
            return slice(64 * (h % 2), 64 * (h % 2) + 64)

        first_wave = True
        for w in range(NW):
            wts = slice(w * NT, (w + 1) * NT)
            par, wv_ = stream_w(wq_dr)
            P.wait("pe", wss[par], wv_)
            for ec in range(KD):
                bk = ec % 4
                pe_claim(ph_use[bk])
                for kd in range(KD):
                    P.do("pe", lambda E, ec=ec, kd=kd, bk=bk, par=par, wts=wts: E.matmul(
                        PH[:, bk, :], wstream_v[:, par, kd, ec * 128:(ec + 1) * 128],
                        xn[:, kd, wts], start=(kd == 0), stop=(kd == KD - 1)),
                        pe if kd == KD - 1 else None, 1)
                qd = pe.v
                P.wait("act", pe, qd)
                P.do("act", lambda E, ec=ec, bk=bk: E.activation(
                    qfm_v[:, ec, :], PH[:, bk, :], AF.Silu), act, 1)
                ph_use[bk]["act"] = act.v
            ws_pe_use[par] = pe.v
            par, wv_ = stream_w(wk_dr)
            P.wait("pe", wss[par], wv_)
            for ec in range(KD):
                bk = ec % 4
                pe_claim(ph_use[bk])
                for kd in range(KD):
                    P.do("pe", lambda E, ec=ec, kd=kd, bk=bk, par=par, wts=wts: E.matmul(
                        PH[:, bk, :], wstream_v[:, par, kd, ec * 128:(ec + 1) * 128],
                        xn[:, kd, wts], start=(kd == 0), stop=(kd == KD - 1)),
                        pe if kd == KD - 1 else None, 1)
                kdn = pe.v
                P.wait("act", pe, kdn)
                P.do("act", lambda E, ec=ec, bk=bk: E.activation(
                    kfm_v[:, ec, :], PH[:, bk, :], AF.Silu), act, 1)
                ph_use[bk]["act"] = act.v
                kf_done = act.v
                P.wait("dve", act, kf_done)
                P.do("dve", lambda E, ec=ec: E.tensor_scalar(
                    kmsk_v[:, 0, ec, :], kfm_v[:, ec, :], fbc[:, 4:5], None,
                    ALU.mult), dve, 1)
                P.do("dve", lambda E, ec=ec: E.tensor_scalar(
                    kmsk_v[:, 1, ec, :], kfm_v[:, ec, :], fbc[:, 5:6], None,
                    ALU.mult), dve, 1)
            for ct in range(CPW):
                c0 = wts.start + ct * 128
                for eh in range(2):
                    bk = (ct * 2 + eh) % 4
                    pe_claim(ph_use[bk])
                    for kd in range(KD):
                        P.do("pe", lambda E, kd=kd, bk=bk, par=par, c0=c0, eh=eh:
                             E.matmul(PH[:, bk, :], xn[:, kd, c0:c0 + 128],
                                      wstream_v[:, par, kd, eh * 512:(eh + 1) * 512],
                                      start=(kd == 0), stop=(kd == KD - 1)),
                             pe if kd == KD - 1 else None, 1)
                    ktd = pe.v
                    P.wait("act", pe, ktd)
                    P.do("act", lambda E, ct=ct, eh=eh, bk=bk: E.activation(
                        ktm_v[:, ct, eh * 512:(eh + 1) * 512], PH[:, bk, :], AF.Silu),
                        act, 1)
                    ph_use[bk]["act"] = act.v
            ws_pe_use[par] = pe.v
            par, wv_ = stream_w(wv_dr)
            P.wait("pe", wss[par], wv_)
            for ct in range(CPW):
                c0 = wts.start + ct * 128
                for eh in range(2):
                    bk = (ct * 2 + eh) % 4
                    pe_claim(ph_use[bk])
                    for kd in range(KD):
                        P.do("pe", lambda E, kd=kd, bk=bk, par=par, c0=c0, eh=eh:
                             E.matmul(PH[:, bk, :], xn[:, kd, c0:c0 + 128],
                                      wstream_v[:, par, kd, eh * 512:(eh + 1) * 512],
                                      start=(kd == 0), stop=(kd == KD - 1)),
                             pe if kd == KD - 1 else None, 1)
                    vtd = pe.v
                    P.wait("dve", pe, vtd)
                    P.do("dve", lambda E, ct=ct, eh=eh, bk=bk: E.tensor_copy(
                        vtm_v[:, ct, eh * 512:(eh + 1) * 512], PH[:, bk, :]), dve, 1)
                    ph_use[bk]["dve"] = dve.v
            ws_pe_use[par] = pe.v
            proj_act = act.v
            proj_dve = dve.v
            if L1A_TRUNC < 4:
                continue
            def pA(k):
                return PH[:, k // 4, (k % 4) * 128:(k % 4) * 128 + 128]

            def pO(k):
                return PY[:, k // 4, (k % 4) * 128:(k % 4) * 128 + 128]

            def pS(k):
                return PS1[:, k * 64:k * 64 + 64]

            def hsl(h):
                return slice(64 * (h % 2), 64 * (h % 2) + 64)

            if first_wave:
                P.wait("dve", act, tables_act)
                first_wave = False
            import os as _o8
            _scan_stop = int(_o8.environ.get("SCAN_STOP", "99"))
            for cl in range(CPW):
                c = w * CPW + cl
                if c > _scan_stop:
                    continue
                csl = slice(cl * 128, (cl + 1) * 128)
                P.wait("pe", act, proj_act)
                P.wait("pe", dve, dve.v)
                for kd in range(KD):
                    if c > int(_o8.environ.get("SKIP_ELB_AFTER", "99")):
                        continue
                    bk = kd % 2
                    P.wait("pe", dve)
                    P.do("pe", lambda E, kd=kd, bk=bk, csl=csl, w=w: E.matmul(
                        PH[:, bk, 0:128], selb[:, kd, :],
                        elbbf_v[0:16, w * NT + cl * 128:w * NT + cl * 128 + 128]
                        if False else elbbf_v[0:16, csl.start + w * NT:csl.stop + w * NT],
                        start=True, stop=True), pe, 1)
                    eb = pe.v
                    P.wait("dve", pe, eb)
                    P.do("dve", lambda E, kd=kd, bk=bk: E.tensor_copy(
                        elbbc_v[:, kd, :], PH[:, bk, 0:128]), dve, 1)
                for hg in range(2):
                    hb = hg * 8
                    P.wait("pe", dve, dve.v)
                    for k in range(8):
                        h = hb + k
                        P.do("pe", lambda E, h=h, k=k, hg=hg, csl=csl: E.matmul(
                            pA(hg * 8 + k), kmsk_v[:, h % 2, h // 2, csl],
                            qfm_v[:, h // 2, csl], start=True, stop=True),
                            pe if k == 7 else None, 1)
                    mma_done = pe.v
                    P.wait("dve", pe, mma_done)
                    for k in range(8):
                        h = hb + k
                        if c > int(_o8.environ.get("SKIP_A2_AFTER", "99")):
                            continue
                        P.do("dve", lambda E, h=h, k=k, hg=hg, c=c: E.scalar_tensor_tensor(
                            a2_v[:, h, :], pA(hg * 8 + k), enltm_v[:, c, h:h + 1],
                            maskt[:, :], ALU.mult, ALU.mult), dve, 1)
                    a2_done = dve.v
                    P.wait("pe", dve, a2_done)
                    for k in range(8):
                        h = hb + k
                        P.do("pe", lambda E, h=h, k=k, cl=cl: E.matmul(
                            pO(k), vtm_v[:, cl, (h // 2) * 128:(h // 2) * 128 + 128],
                            a2_v[:, h, :], start=True, stop=True),
                            pe if k == 7 else None, 1)
                    mmoi_done = pe.v
                    for pr in range(hb // 2, hb // 2 + 4):
                        P.do("dve", lambda E, pr=pr, csl=csl: E.tensor_tensor(
                            qs_v[:, pr % 4, :], qfm_v[:, pr, csl],
                            elbbc_v[:, pr, :], ALU.mult), dve, 1)
                    qs_done = dve.v
                    P.wait("pe", dve, qs_done)
                    for k in range(8):
                        h = hb + k
                        P.do("pe", lambda E, h=h, k=k, hg=hg: E.matmul(
                            pA(hg * 8 + k), sbf_v[:, h, :], qs_v[:, (h // 2) % 4, :],
                            start=True, stop=True), pe if k == 7 else None, 1)
                    mmoe_done = pe.v
                    P.wait("dve", pe, mmoe_done)
                    for k in range(8):
                        h = hb + k
                        P.do("dve", lambda E, h=h, k=k, csl=csl: E.tensor_tensor(
                            obf_v[hsl(h), h // 2, csl], pO(k)[hsl(h), :],
                            elbbc_v[hsl(h), h // 2, :], ALU.mult), dve, 1)
                    for k in range(8):
                        h = hb + k
                        P.do("dve", lambda E, h=h, k=k, hg=hg, csl=csl: E.tensor_tensor(
                            obf_v[hsl(h), h // 2, csl], obf_v[hsl(h), h // 2, csl],
                            pA(hg * 8 + k)[hsl(h), :], ALU.add), dve, 1)
                    for k in range(8):
                        h = hb + k
                        if c > int(_o8.environ.get("SKIP_KDS_AFTER", "99")):
                            continue
                        P.do("dve", lambda E, h=h, cl=cl, c=c: E.tensor_scalar(
                            kds_v[:, h // 2 % 4, hsl(h)],
                            ktm_v[:, cl, h * 64:h * 64 + 64],
                            ktab_v[:, c, h:h + 1], None, ALU.mult), dve, 1)
                    kds_done = dve.v
                    P.wait("pe", dve, kds_done)
                    for k in range(8):
                        h = hb + k
                        P.do("pe", lambda E, h=h, k=k, cl=cl: E.matmul(
                            pS(k), kds_v[:, h // 2 % 4, :],
                            vtm_v[:, cl, h * 64:h * 64 + 64],
                            start=True, stop=True), pe if k == 7 else None, 1)
                    mms_done = pe.v
                    P.wait("dve", pe, mms_done)
                    for k in range(8):
                        h = hb + k
                        if c > int(_o8.environ.get("SKIP_SF_AFTER", "99")):
                            continue
                        P.do("dve", lambda E, h=h, k=k, c=c: E.scalar_tensor_tensor(
                            Sf_v[hsl(h), h // 2, :], Sf_v[hsl(h), h // 2, :],
                            etotbc_v[hsl(h), h // 2, c:c + 1], pS(k)[hsl(h), :],
                            ALU.mult, ALU.add), dve, 1)
                    for k in range(8):
                        h = hb + k
                        if c > int(_o8.environ.get("SKIP_SBF_AFTER", "99")):
                            continue
                        P.do("dve", lambda E, h=h: E.tensor_copy(
                            sbf_v[hsl(h), h, hsl(h)], Sf_v[hsl(h), h // 2, :]), dve, 1)
                for i in range(4):
                    ph_use[i]["dve"] = dve.v
                py_use[0]["dve"] = py_use[1]["dve"] = dve.v
                ps1_use["dve"] = dve.v
            if L1A_TRUNC < 5:
                continue
            scan_done = dve.v
            sq_pe = [0, 0]
            pe_claim(ps1_use)
            P.wait("act", dve, scan_done)
            for kd in range(KD):
                par2 = kd % 2
                P.wait("act", pe, sq_pe[par2])
                P.do("act", lambda E, kd=kd, par2=par2: E.activation(
                    sqt[:, par2, :], obf_v[:, kd, :], AF.Square), act, 1)
                sqd = act.v
                P.wait("pe", act, sqd)
                P.do("pe", lambda E, par2=par2, kd=kd: E.matmul(
                    PS1[0:16, :], cones[:, :], sqt[:, par2, :],
                    start=(kd == 0), stop=(kd == KD - 1)), pe, 1)
                sq_pe[par2] = pe.v
            oss_done = pe.v
            P.wait("act", pe, oss_done)
            P.do("act", lambda E: E.activation(st16[0:16, 0, :], PS1[0:16, :], AF.Copy,
                                               scale=1.0 / 1024.0), act, 1)
            ps1_use["act"] = act.v
            P.do("act", lambda E: E.activation(st16[0:16, 0, :], st16[0:16, 0, :],
                                               AF.Ln, bias=fbc[0:16, 1:2]), act, 1)
            P.wait("act", pe, rows_war[0])
            P.do("act", lambda E: E.activation(rows_sm[0:1, 1, :], st16[0:1, 0, :],
                                               AF.Exp, scale=-0.5), act, 1)
            orstd_done = act.v
            P.wait("pe", act, orstd_done)
            pe_claim(py_use[0])
            P.do("pe", lambda E: E.matmul(PY[:, 0, :], conesrow_f[:, :],
                                          rows_sm[0:1, 1, :], start=True, stop=True),
                 pe, 1)
            ob = pe.v
            rows_war[0] = pe.v
            P.wait("dve", pe, ob)
            import os as _o9
            for kd in range(KD):
                if _o9.environ.get("SKIP_RMS") == "1":
                    continue
                P.do("dve", lambda E, kd=kd: E.scalar_tensor_tensor(
                    obf_v[:, kd, :], PY[:, 0, :], nwsb[:, kd:kd + 1], obf_v[:, kd, :],
                    ALU.mult, ALU.mult), dve, 1)
            py_use[0]["dve"] = dve.v
            on_done = dve.v
            par, wv_ = stream_w(wo_dr)
            P.wait("pe", wss[par], wv_)
            P.wait("pe", dve, on_done)
            for ec in range(KD):
                bk = ec % 4
                pe_claim(ph_use[bk])
                for kd in range(KD):
                    P.do("pe", lambda E, ec=ec, kd=kd, bk=bk, par=par: E.matmul(
                        PH[:, bk, :], wstream_v[:, par, kd, ec * 128:(ec + 1) * 128],
                        obf_v[:, kd, :], start=(kd == 0), stop=(kd == KD - 1)),
                        pe if kd == KD - 1 else None, 1)
                od = pe.v
                P.wait("dve", pe, od)
                P.do("dve", lambda E, ec=ec, bk=bk, wts=wts: E.tensor_tensor(
                    xsb[:, ec, wts], xsb[:, ec, wts], PH[:, bk, :], ALU.add), dve, 1)
                ph_use[bk]["dve"] = dve.v
            ws_pe_use[par] = pe.v

    emits = {"l0a": emit_l0a, "l0a_scores": emit_l0a, "l0a_f": emit_l0a_fused,
             "l0f": lambda: emit_ffn(0), "l1f": lambda: emit_ffn(1), "l1a": emit_l1a}
    for b in blocks:
        emits[b]()
        dump("x_" + b, xsb[:, :, :], [128, KD, T], BF16)

    P.wait("pool", pe)
    P.wait("pool", dve)
    P.wait("pool", act)
    P.do("pool", lambda E: E.dma_start(
        out=xout.rearrange("(kd p) t -> p kd t", p=128), in_=xsb[:, :, :]), ode, 16)
    P.wait("pool", ode)
    return P, dumps_made



def _consts():
    c = {}
    c["c_ones"] = np.ones((128, 16), np.float32)
    c["c_onesrow"] = np.ones((1, 128), np.float32)
    sel = np.zeros((16, 8, 128), np.float32)
    for kd in range(8):
        sel[2 * kd, kd, 0:64] = 1.0
        sel[2 * kd + 1, kd, 64:128] = 1.0
    c["c_sel"] = sel
    hs = np.zeros((128, 8, 16), np.float32)
    for kd in range(8):
        for p in range(128):
            hs[p, kd, 2 * kd + (p >= 64)] = 1.0
    c["c_headsel"] = hs
    ss = np.zeros((128, 16), np.float32)
    for p in range(128):
        ss[p, p % 16] = 1.0
    c["c_ss"] = ss
    eo = np.zeros((8, 8, 128), np.float32)
    for e in range(8):
        eo[e, e, :] = 1.0
    c["c_eones"] = eo
    c["c_id"] = np.eye(128, dtype=np.float32)
    m = np.zeros((128, 128), np.float32)
    for j in range(128):
        m[j, j:] = 1.0
    c["c_mask"] = m
    fb = np.zeros((128, 6), np.float32)
    fb[:, 0] = 1e-5
    fb[:, 1] = 1e-6
    fb[:, 2] = 1.0
    fb[0:64, 4] = 1.0
    fb[64:128, 5] = 1.0
    c["c_fb"] = fb
    return c


def _stage_weights(inp):
    st = {}
    A = np.ascontiguousarray

    def w1tile(W1):
        return A(W1.reshape(8, 8, 128, 32, 128).transpose(0, 3, 2, 1, 4))

    def w2tile(W2):
        return A(W2.reshape(8, 4, 8, 128, 8, 128).transpose(0, 1, 4, 3, 2, 5))

    blk = np.zeros((128, 3, 128), np.float32)
    for i, nm in enumerate(("L0_attn_Wq", "L0_attn_Wk", "L0_attn_Wv")):
        Wt = np.asarray(inp[nm], np.float32).T
        blk[0:64, i, 0:64] = Wt
        blk[64:128, i, 64:128] = Wt
    st["w_l0blk"] = blk
    qbd = np.zeros((128, 128), np.float32)
    kbd = np.zeros((128, 128), np.float32)
    for bb in range(2):
        sl = slice(64 * bb, 64 * bb + 64)
        qbd[sl, sl] = np.asarray(inp["L0_attn_Wq"], np.float32)
        kbd[sl, sl] = np.asarray(inp["L0_attn_Wk"], np.float32)
    st["w_l0qbd"] = qbd
    st["w_l0kbd"] = kbd
    st["w_l0wr"] = A(np.asarray(inp["L0_ffn_Wr"], np.float32).reshape(8, 128, 8)
                     .transpose(1, 0, 2))
    st["w_l0w1"] = w1tile(np.asarray(inp["L0_ffn_W1"], np.float32))
    st["w_l0w2"] = w2tile(np.asarray(inp["L0_ffn_W2"], np.float32))
    st["p_br0"] = np.asarray(inp["L0_ffn_br"], np.float32).reshape(8, 1)
    for nm, key in (("Wq", "wq"), ("Wk", "wk"), ("Wv", "wv"), ("Wo", "wo")):
        Wt = np.asarray(inp["L1_attn_" + nm], np.float32).T
        st["w_l1" + key] = A(Wt.reshape(8, 128, 1024).transpose(1, 0, 2))
    st["w_l1wf"] = A(np.asarray(inp["L1_attn_Wf"], np.float32).T
                     .reshape(8, 128, 16).transpose(1, 0, 2))
    st["p_delta"] = np.asarray(inp["L1_attn_delta"], np.float32).reshape(16, 1)
    st["p_nw"] = A(np.asarray(inp["L1_attn_norm_weight"], np.float32).reshape(8, 128).T)
    st["w_l1wr"] = A(np.asarray(inp["L1_ffn_Wr"], np.float32).reshape(8, 128, 8)
                     .transpose(1, 0, 2))
    st["w_l1w1"] = w1tile(np.asarray(inp["L1_ffn_W1"], np.float32))
    st["w_l1w2"] = w2tile(np.asarray(inp["L1_ffn_W2"], np.float32))
    st["p_br1"] = np.asarray(inp["L1_ffn_br"], np.float32).reshape(8, 1)
    st["p_b1"] = A(np.asarray(inp["L1_ffn_b1"], np.float32).reshape(8, 32, 128)
                   .transpose(2, 0, 1))
    st["p_b2"] = A(np.asarray(inp["L1_ffn_b2"], np.float32).reshape(8, 8, 128)
                   .transpose(2, 0, 1))
    st["p_lng"] = A(np.stack(
        [np.asarray(inp["L1_ffn_ln_g"], np.float32).reshape(8, 128).T,
         np.asarray(inp["L1_ffn_ln_b"], np.float32).reshape(8, 128).T], axis=1))
    return st


_PROG_CACHE = {}


def _get_prog(blocks, dump_names=()):
    key = (tuple(blocks), tuple(dump_names), L1A_TRUNC, LN_TRUNC)
    if key not in _PROG_CACHE:
        P, dumps = build_program(blocks, dump_names)
        nc = P.build()
        _PROG_CACHE[key] = (nc, dumps)
    return _PROG_CACHE[key]


def run_blocks(inp, blocks, xcur=None, dump_names=(), s_all=None):
    nc, dumps = _get_prog(tuple(blocks), tuple(dump_names))
    st = _stage_weights(inp)
    st.update(_consts())
    x = np.asarray(xcur if xcur is not None else inp["L0_attn_x"], np.float32)
    xall_fm = (np.ascontiguousarray(x.transpose(0, 2, 1))
               if any(b == "l0a_f" for b in blocks) else None)
    in_maps = []
    for ci in range(8):
        m = dict(st)
        m["xin"] = np.ascontiguousarray(x[ci].T)
        if "l0a" in blocks or "l0a_scores" in blocks or "l0a_f" in blocks:
            sel = np.zeros((128, 16), np.float32)
            for h in range(16):
                sel[16 * ci + h, h] = 1.0
            m["c_mysel"] = sel
        if "l0a_f" in blocks:
            mis = np.zeros((16, 8), np.float32)
            mis[:, ci] = 1.0
            m["c_misel"] = mis
            m["xall"] = xall_fm
        if s_all is not None:
            m["s_all"] = s_all
        in_maps.append(m)
    res = run_bass_kernel_spmd(nc, in_maps, core_ids=list(range(8)))
    keys = set()
    for r in res.results:
        keys |= set(r.keys())
    if "xout" in keys:
        out = np.stack([np.asarray(res.results[ci]["xout"]).T for ci in range(8)])
        out = out.astype(np.float32)
    else:
        out = None
    extra = {nm: [np.asarray(res.results[ci]["dump_" + nm]) for ci in range(8)]
             for nm in dumps}
    if "s_my" in keys:
        extra["s_my"] = [np.asarray(res.results[ci]["s_my"]) for ci in range(8)]
    return out, extra


def kernel(**inp):
    out, _ = run_blocks(inp, ("l0a_f", "l0f", "l1a", "l1f"))
    return out


# version 06
# speedup vs baseline: 1.5218x; 1.0040x over previous
import numpy as np
from contextlib import ExitStack

import concourse.bass as bass
import concourse.mybir as mybir
from concourse.bass_utils import run_bass_kernel_spmd

F32 = mybir.dt.float32
BF16 = mybir.dt.bfloat16
AF = mybir.ActivationFunctionType
ALU = mybir.AluOpType
AX = mybir.AxisListType

T = 2048
KD = 8
NT = 512
NTT = 4
NE = 8
NFC = 32
NFQ = 4
NW = 4
CPW = 4
L1A_TRUNC = 99
LN_TRUNC = 99


class Sem:
    __slots__ = ("h", "v")

    def __init__(self, h):
        self.h = h
        self.v = 0


class PB:
    ENG = {"pe": "tensor", "dve": "vector", "act": "scalar", "pool": "gpsimd", "sp": "sync"}

    def __init__(self):
        self.nc = bass.Bass("TRN2", target_bir_lowering=False)
        self.stk = ExitStack()
        self.q = {k: [] for k in self.ENG}
        self.sem = {}
        for s in ("pe", "dve", "act", "pool", "sp", "wde", "xde", "cc", "ode",
                  "w1a", "w1b", "w2a", "w2b", "wsa", "wsb", "sla", "slb", "wrs"):
            self.sem[s] = Sem(self.stk.enter_context(self.nc.semaphore(s)))

    def sb(self, name, shape, dt):
        return self.stk.enter_context(self.nc.sbuf_tensor(name, list(shape), dt))

    def ps(self, name, shape, dt=F32):
        return self.stk.enter_context(self.nc.psum_tensor(name, list(shape), dt))

    def wait(self, eng, sem, v=None):
        vv = sem.v if v is None else v
        if vv <= 0:
            return
        self.q[eng].append(lambda E, h=sem.h, vv=vv: E.wait_ge(h, vv))

    SELF_SER = ("dve", "act")

    def do(self, eng, fn, sem=None, k=1):
        if eng in self.SELF_SER:
            own = self.sem[eng]
            if own.v > 0:
                self.q[eng].append(lambda E, h=own.h, vv=own.v: E.wait_ge(h, vv))
            if sem is None:
                sem = own
        if sem is None:
            self.q[eng].append(fn)
        else:
            sem.v += k
            self.q[eng].append(lambda E, fn=fn, h=sem.h, k=k: fn(E).then_inc(h, k))

    def build(self):
        with self.nc.Block() as blk:
            for short, attr in self.ENG.items():
                getattr(blk, attr)(lambda E, lst=self.q[short]: [f(E) for f in lst])
        return self.nc


def _fs(ap):
    return ap[tuple([slice(None)] * len(ap.shape))]


def build_program(blocks=("l0a", "l0f", "l1a", "l1f"), dump_names=()):
    P = PB()
    S = P.sem
    pe, dve, act, pool, sp = S["pe"], S["dve"], S["act"], S["pool"], S["sp"]
    wde, xde, ode, ccs = S["wde"], S["xde"], S["ode"], S["cc"]
    w1s = [S["w1a"], S["w1b"]]
    w2s = [S["w2a"], S["w2b"]]
    wss = [S["wsa"], S["wsb"]]
    sls = [S["sla"], S["slb"]]
    need = set(blocks)
    dumps_made = []

    xin = P.nc.dram_tensor("xin", [1024, T], F32, kind="ExternalInput").ap()
    xout = P.nc.dram_tensor("xout", [1024, T], F32, kind="ExternalOutput").ap()
    dr = {}

    def din(nm, shape):
        dr[nm] = P.nc.dram_tensor(nm, list(shape), F32, kind="ExternalInput").ap()

    for nm, shp in [("c_ones", [128, 16]), ("c_onesrow", [1, 128]), ("c_sel", [16, 8, 128]),
                    ("c_headsel", [128, 8, 16]), ("c_ss", [128, 16]), ("c_eones", [8, 8, 128]),
                    ("c_id", [128, 128]), ("c_mask", [128, 128]),
                    ("c_fb", [128, 6])]:
        din(nm, shp)
    l0a_mode = ("scores" if "l0a_scores" in need else
                ("fused" if "l0a_f" in need else ("ext" if "l0a" in need else None)))
    if l0a_mode is not None:
        din("w_l0blk", [128, 3, 128])
        din("c_mysel", [128, 16])
        if l0a_mode == "scores":
            s_my = P.nc.dram_tensor("s_my", [16, T], F32, kind="ExternalOutput").ap()
        elif l0a_mode == "ext":
            din("s_all", [128, T])
            s_all = dr["s_all"]
        else:
            din("xall", [NE, 1024, T])
            din("w_l0qbd", [128, 128])
            din("w_l0kbd", [128, 128])
            din("c_misel", [16, 8])
    for L in (0, 1):
        if ("l0f" if L == 0 else "l1f") in need:
            din(f"w_l{L}wr", [128, 8, 8])
            din(f"w_l{L}w1", [NE, NFC, 128, 8, 128])
            din(f"w_l{L}w2", [NE, NFQ, 8, 128, 8, 128])
            din(f"p_br{L}", [8, 1])
    if "l1f" in need:
        din("p_b1", [128, 8, 32])
        din("p_b2", [128, 8, 8])
        din("p_lng", [128, 2, 8])
    if "l1a" in need:
        import os as _os0
        if _os0.environ.get("SKIP_WQKVO") != "1":
            for w in ("wq", "wk", "wv", "wo"):
                din("w_l1" + w, [128, 8, 1024])
        din("w_l1wf", [128, 8, 16])
        din("p_delta", [16, 1])
        din("p_nw", [128, 8])

    xsb = P.sb("xsb", [128, KD, T], BF16)
    xn = P.sb("xn", [128, KD, T], BF16)
    U = P.sb("U", [128, 40960], BF16)
    W = P.sb("W", [128, 16384], BF16)
    SCRF = P.sb("SCRF", [128, 4096], F32)
    rows_sm = P.sb("rows_sm", [128, 2, NT], F32)
    grow_sm = P.sb("grow_sm", [128, NT], BF16)
    sqt = P.sb("sqt", [128, 2, NT], BF16)
    cones = P.sb("cones", [128, 16], BF16)
    conesrow = P.sb("conesrow", [1, 128], BF16)
    conesrow_f = P.sb("conesrow_f", [1, 128], F32)
    selb = P.sb("selb", [16, 8, 128], BF16)
    headsb = P.sb("headsb", [128, 8, 16], BF16)
    ssb = P.sb("ssb", [128, 16], BF16)
    eones = P.sb("eones", [8, 8, 128], BF16)
    idb = P.sb("idb", [128, 128], BF16)
    idf = P.sb("idf", [128, 128], F32)
    maskt = P.sb("maskt", [128, 128], BF16)
    fbc = P.sb("fbc", [128, 6], F32)
    wrsb = P.sb("wrsb", [128, 8, 8], F32)
    brsb = P.sb("brsb", [8, 2], F32)
    lngb = P.sb("lngb", [128, 2, 8], F32)
    nwsb = P.sb("nwsb", [128, 8], F32)
    dlt = P.sb("dlt", [16, 1], F32)
    b1sb = P.sb("b1sb", [128, 8, 32], F32)
    b2sb = P.sb("b2sb", [128, 8, 8], F32)
    ohtm = P.sb("ohtm", [128, 8], BF16)
    gcolb = P.sb("gcolb", [128, 16], BF16)
    wfTsb = P.sb("wfTsb", [128, 8, 16], BF16)
    miselb = P.sb("miselb", [16, 8], F32)
    wcolf = P.sb("wcolf", [128, 1], F32)
    m11f = P.sb("m11f", [128, 1], F32)

    PH = P.ps("PH", [128, 4, 512])
    PY = P.ps("PY", [128, 2, 512])
    PS1 = P.ps("PS1", [128, 512])
    PSM = P.ps("PSM", [128, 1024], BF16)
    pt_oh = PSM[0:8, 0:128]
    pt_lb = PSM[:, 128:144]
    ptrow = PSM[0:1, 144:272]

    def view(base, off, shape):
        n = int(np.prod(shape))
        ap = base[:, off:off + n]
        if len(shape) > 1:
            names = " ".join(f"a{i}" for i in range(len(shape)))
            ap = ap.rearrange(f"p ({names}) -> p {names}",
                              **{f"a{i}": s for i, s in enumerate(shape)})
        return ap

    hq_v = view(U, 0, (8, T))
    yac_v = view(U, 8 * T, (8, T))
    gateb_v = view(U, 16 * T, (T,))
    gmask_v = view(U, 17 * T, (T,))
    gmask2_v = view(U, 19 * T, (T,))
    ohfm_v = view(U, 18 * T, (T,))
    ohfm_ct = view(U, 18 * T, (16, 128))
    l0_q = view(U, 0, (NT,))
    l0_qk = view(U, NT, (NT,))
    l0_vs = view(U, 2 * NT, (NT,))
    l0_e = view(U, 3 * NT, (NT,))
    l0_sall = view(U, 4 * NT, (2, NT))
    l0_smine = view(U, 6 * NT, (T,))
    l0_probs = view(U, 6 * NT + T, (T,))
    l0_emine = view(U, 6 * NT + 2 * T, (2, NT))
    l0_inv = view(U, 6 * NT + 3 * T, (2, NT))
    l0_wblk = view(U, 6 * NT + 4 * T, (3, 128))
    f_xb = view(U, 16384, (2, 8, NT))
    f_z = view(U, 24576, (NT,))
    f_xz = view(U, 24576 + NT, (NT,))
    f_sq = view(U, 24576 + 2 * NT, (NT,))
    f_et = view(U, 24576 + 3 * NT, (NT,))
    f_em = view(U, 26624, (T,))
    f_mblk = view(U, 28672, (128,))
    f_mblkT = view(U, 28672 + 128, (128,))
    f_qbd = view(U, 28672 + 256, (128,))
    f_kbd = view(U, 28672 + 384, (128,))
    f_wsel = view(U, 28672 + 512, (8, 16))
    f_m2x = view(U, 28672 + 640, (16,))
    qfm_v = view(U, 0, (8, NT))
    kfm_v = view(U, 4096, (8, NT))
    ktm_v = view(U, 8192, (CPW, 1024))
    vtm_v = view(U, 12288, (CPW, 1024))
    obf_v = view(U, 16384, (8, NT))
    a2_v = view(U, 20480, (16, 128))
    kds_v = view(U, 22528, (4, 128))
    elbbc_v = view(U, 23040, (8, 128))
    lbbf_v = view(U, 24064, (T,))
    elbbf_v = view(U, 24064 + T, (T,))
    sbf_v = view(U, 24064 + 2 * T, (16, 128))
    etot_bf = view(U, 24064 + 2 * T + 2048, (16,))
    sfx_v = view(U, 24064 + 2 * T + 2048 + 16, (T,))
    kmsk_v = view(U, 24064 + 3 * T + 2048 + 16, (2, 8, NT))
    qs_v = view(U, 24064 + 2 * T + 2048 + 16, (4, 128))
    w1c_v = view(W, 0, (2, 8, 128))
    w2c_v = view(W, 2048, (2, 8, 128))
    wstream_v = view(W, 0, (2, 8, 1024))
    stg32 = view(SCRF, 0, (2, NT))
    st16 = view(SCRF, 1024, (2, NT))
    lftt_v = view(SCRF, 2048, (2, NT))
    cols_v = view(SCRF, 3072, (8,))
    ex_v = view(SCRF, 3072 + 8, (8,))
    secol_v = view(SCRF, 3072 + 32, (16,))
    vsump_v = view(SCRF, 3072 + 48, (KD, NTT))
    vsum8_v = view(SCRF, 3072 + 80, (KD,))
    vsc2_v = view(SCRF, 3072 + 88, (KD,))
    ginv_v = view(SCRF, 3072 + 96, (16,))
    enltm_v = view(SCRF, 2048, (16, 16))
    ktab_v = view(SCRF, 2048 + 256, (16, 16))
    Sf_v = view(SCRF, 2048 + 512, (8, 64))
    etotbc_v = view(SCRF, 2048 + 1024, (8, 16))
    totf_v = view(SCRF, 2048 + 1024 + 128, (16,))

    TTS = [slice(tt * NT, (tt + 1) * NT) for tt in range(NTT)]
    TTP = [slice(tp * 2 * NT, (tp + 1) * 2 * NT) for tp in range(2)]

    for nm, dst in [("c_ones", cones), ("c_onesrow", conesrow), ("c_sel", selb),
                    ("c_headsel", headsb), ("c_ss", ssb), ("c_eones", eones),
                    ("c_id", idb), ("c_mask", maskt)]:
        P.do("pool", lambda E, d=dst, s=dr[nm]: E.dma_start(out=_fs(d), in_=_fs(s)), wde, 16)
    P.do("sp", lambda E: E.dma_start(out=_fs(idf), in_=_fs(dr["c_id"])), xde, 16)
    P.do("sp", lambda E: E.dma_start(out=_fs(fbc), in_=_fs(dr["c_fb"])), xde, 16)
    P.do("sp", lambda E: E.dma_start(out=_fs(conesrow_f), in_=_fs(dr["c_onesrow"])), xde, 16)
    P.do("pool", lambda E: E.dma_start(
        out=xsb[:, :, :], in_=xin.rearrange("(kd p) t -> p kd t", p=128)), wde, 16)
    if l0a_mode is not None:
        P.do("pool", lambda E: E.dma_start(out=l0_wblk, in_=_fs(dr["w_l0blk"])), wde, 16)
        myselb = P.sb("myselb", [128, 16], BF16)
        P.do("pool", lambda E: E.dma_start(out=_fs(myselb), in_=_fs(dr["c_mysel"])), wde, 16)
    if l0a_mode == "fused":
        P.do("pool", lambda E: E.dma_start(out=f_qbd, in_=_fs(dr["w_l0qbd"])), wde, 16)
        P.do("pool", lambda E: E.dma_start(out=f_kbd, in_=_fs(dr["w_l0kbd"])), wde, 16)
        P.do("sp", lambda E: E.dma_start(out=_fs(miselb), in_=_fs(dr["c_misel"])), xde, 16)
    if "l1a" in need:
        import os as _os
        if _os.environ.get("SKIP_WF") != "1":
            P.do("pool", lambda E: E.dma_start(out=_fs(wfTsb), in_=_fs(dr["w_l1wf"])), wde, 16)
        if _os.environ.get("SKIP_DLT") != "1":
            P.do("sp", lambda E: E.dma_start(out=_fs(dlt), in_=_fs(dr["p_delta"])), xde, 16)
        if _os.environ.get("SKIP_NW") != "1":
            P.do("sp", lambda E: E.dma_start(out=_fs(nwsb), in_=_fs(dr["p_nw"])), xde, 16)
    if "l1f" in need:
        P.do("sp", lambda E: E.dma_start(out=_fs(b1sb), in_=_fs(dr["p_b1"])), xde, 16)
        P.do("sp", lambda E: E.dma_start(out=_fs(b2sb), in_=_fs(dr["p_b2"])), xde, 16)
        P.do("sp", lambda E: E.dma_start(out=_fs(lngb), in_=_fs(dr["p_lng"])), xde, 16)
    if "l0f" in need:
        P.do("sp", lambda E: E.dma_start(out=brsb[:, 0:1], in_=_fs(dr["p_br0"])), xde, 16)
    if "l1f" in need:
        P.do("sp", lambda E: E.dma_start(out=brsb[:, 1:2], in_=_fs(dr["p_br1"])), xde, 16)
    for e in ("pe", "dve", "act"):
        P.wait(e, wde)
        P.wait(e, xde)
    P.wait("sp", wde)
    P.wait("pool", xde)

    def dump(name, ap, shape, dt=F32):
        if name not in dump_names:
            return
        d = P.nc.dram_tensor("dump_" + name, list(shape), dt, kind="ExternalOutput").ap()
        for e in ("pe", "dve", "act"):
            P.do(e, lambda E: E.nop(), S[e], 1)
            P.wait("sp", S[e])
        P.do("sp", lambda E, d=d, ap=ap: E.dma_start(out=_fs(d), in_=ap), xde, 16)
        P.wait("sp", xde)
        dumps_made.append(name)

    ph_use = [{"dve": 0, "act": 0} for _ in range(4)]
    py_use = [{"dve": 0, "act": 0} for _ in range(2)]
    ps1_use = {"dve": 0, "act": 0}

    def pe_claim(d):
        P.wait("pe", dve, d["dve"])
        P.wait("pe", act, d["act"])

    pt_oh_use = [0]
    ptrow_use = [0]
    rows_war = [0]

    def emit_ln(src, out_bf, router=None, affine=False, eps=1e-5):
        P.wait("act", dve)
        P.wait("act", pe)
        P.wait("pe", dve)
        stg_pe_use = [0, 0]
        sq_pe_use = [0, 0]
        rows_pe_use = [rows_war[0]]
        for tt in range(NTT):
            if LN_TRUNC < 1 + tt:
                break
            ts = TTS[tt]
            for kd in range(KD):
                par = kd % 2
                P.wait("act", pe, sq_pe_use[par])
                P.do("act", lambda E, kd=kd, ts=ts, par=par: E.activation(
                    sqt[:, par, :], src[:, kd, ts], AF.Square), act, 1)
                sq_done = act.v
                if kd == 0:
                    pe_claim(ps1_use)
                    pe_claim(py_use[0])
                P.do("pe", lambda E, kd=kd, ts=ts: E.matmul(
                    PS1[0:16, :], cones[:, :], src[:, kd, ts],
                    start=(kd == 0), stop=(kd == KD - 1)))
                P.wait("pe", act, sq_done)
                P.do("pe", lambda E, kd=kd, par=par: E.matmul(
                    PY[0:16, 0, :], cones[:, :], sqt[:, par, :],
                    start=(kd == 0), stop=(kd == KD - 1)), pe, 1)
                sq_pe_use[par] = pe.v
            stats_done = pe.v
            if LN_TRUNC < 10 + tt:
                continue
            P.wait("act", pe, stats_done)
            P.do("act", lambda E: E.activation(st16[0:16, 0, :], PS1[0:16, :], AF.Copy,
                                               scale=1.0 / 1024.0), act, 1)
            ps1_use["act"] = act.v
            P.do("act", lambda E: E.activation(st16[0:16, 1, :], PY[0:16, 0, :], AF.Copy,
                                               scale=1.0 / 1024.0), act, 1)
            py_use[0]["act"] = act.v
            pe_claim_needed = py_use[1]
            P.wait("act", dve, pe_claim_needed["dve"])
            P.do("act", lambda E: E.activation(PY[0:16, 1, :], st16[0:16, 0, :],
                                               AF.Square), act, 1)
            m2_done = act.v
            P.wait("dve", act, m2_done)
            P.do("dve", lambda E: E.tensor_tensor(st16[0:16, 1, :], st16[0:16, 1, :],
                                                  PY[0:16, 1, :], ALU.subtract), dve, 1)
            py_use[1]["dve"] = dve.v
            var_done = dve.v
            P.wait("act", dve, var_done)
            P.do("act", lambda E: E.activation(st16[0:16, 1, :], st16[0:16, 1, :],
                                                AF.Ln, bias=fbc[0:16, 0:1]), act, 1)
            P.wait("act", pe, rows_pe_use[0])
            P.do("act", lambda E: E.activation(rows_sm[0:1, 1, :], st16[0:1, 1, :],
                                               AF.Exp, scale=-0.5), act, 1)
            P.do("act", lambda E: E.activation(rows_sm[0:1, 0, :], st16[0:1, 0, :],
                                               AF.Copy), act, 1)
            rows_done = act.v
            if LN_TRUNC < 20 + tt:
                continue
            P.wait("pe", act, rows_done)
            pe_claim(py_use[0])
            pe_claim(py_use[1])
            P.do("pe", lambda E: E.matmul(PY[:, 0, :], conesrow_f[:, :],
                                          rows_sm[0:1, 0, :], start=True, stop=True))
            P.do("pe", lambda E: E.matmul(PY[:, 1, :], conesrow_f[:, :],
                                          rows_sm[0:1, 1, :], start=True, stop=True), pe, 1)
            bc_done = pe.v
            rows_pe_use[0] = pe.v
            if LN_TRUNC < 30 + tt:
                continue
            P.wait("dve", pe, bc_done)
            for kd in range(KD):
                spar = kd % 2
                if router is not None:
                    P.wait("dve", pe, stg_pe_use[spar])
                P.do("dve", lambda E, kd=kd, ts=ts, spar=spar: E.scalar_tensor_tensor(
                    stg32[:, spar, :], PY[:, 0, :], -1.0, src[:, kd, ts],
                    ALU.mult, ALU.add), dve, 1)
                if router is not None:
                    P.do("dve", lambda E, spar=spar: E.tensor_tensor(
                        stg32[:, spar, :], stg32[:, spar, :], PY[:, 1, :],
                        ALU.mult), dve, 1)
                    P.do("dve", lambda E, kd=kd, ts=ts, spar=spar: E.tensor_copy(
                        out_bf[:, kd, ts], stg32[:, spar, :]), dve, 1)
                    st_done = dve.v
                    P.wait("pe", dve, st_done)
                    router(kd, tt, stg32[:, spar, :])
                    stg_pe_use[spar] = pe.v
                elif affine:
                    P.do("dve", lambda E, spar=spar: E.tensor_tensor(
                        stg32[:, spar, :], stg32[:, spar, :], PY[:, 1, :],
                        ALU.mult), dve, 1)
                    P.do("dve", lambda E, kd=kd, ts=ts, spar=spar: E.tensor_scalar(
                        out_bf[:, kd, ts], stg32[:, spar, :],
                        lngb[:, 0, kd:kd + 1], lngb[:, 1, kd:kd + 1],
                        ALU.mult, ALU.add), dve, 1)
                else:
                    P.do("dve", lambda E, kd=kd, ts=ts, spar=spar: E.tensor_tensor(
                        out_bf[:, kd, ts], stg32[:, spar, :], PY[:, 1, :],
                        ALU.mult), dve, 1)
            py_use[0]["dve"] = py_use[1]["dve"] = dve.v
        rows_war[0] = rows_pe_use[0]
        P.wait("pe", dve)

    def emit_l0a():
        emit_ln(xsb, xn)
        if l0a_mode == "scores":
            smine_f = view(SCRF, 2048, (T,))
            for tt in range(NTT):
                ts = TTS[tt]
                for kd in range(KD):
                    for b in range(2):
                        pe_claim(ph_use[b])
                    P.do("pe", lambda E, kd=kd, ts=ts: E.matmul(
                        PH[:, 0, :], l0_wblk[:, 0, :], xn[:, kd, ts],
                        start=True, stop=True))
                    P.do("pe", lambda E, kd=kd, ts=ts: E.matmul(
                        PH[:, 1, :], l0_wblk[:, 1, :], xn[:, kd, ts],
                        start=True, stop=True), pe, 1)
                    qk_mm = pe.v
                    P.wait("dve", pe, qk_mm)
                    P.do("dve", lambda E: E.tensor_copy(l0_q, PH[:, 0, :]), dve, 1)
                    P.do("dve", lambda E: E.tensor_tensor(l0_qk, l0_q, PH[:, 1, :],
                                                          ALU.mult), dve, 1)
                    qk_done = dve.v
                    ph_use[0]["dve"] = ph_use[1]["dve"] = dve.v
                    P.wait("pe", dve, qk_done)
                    if kd == 0:
                        pe_claim(ps1_use)
                    P.do("pe", lambda E, kd=kd: E.matmul(
                        PS1[0:16, :], headsb[:, kd, :], l0_qk,
                        start=(kd == 0), stop=(kd == KD - 1)),
                        pe if kd == KD - 1 else None, 1)
                s_done = pe.v
                P.wait("dve", pe, s_done)
                P.do("dve", lambda E, ts=ts: E.tensor_copy(smine_f[0:16, ts],
                                                           PS1[0:16, :]), dve, 1)
                ps1_use["dve"] = dve.v
            P.wait("sp", dve)
            P.do("sp", lambda E, smine_f=smine_f: E.dma_start(
                out=_fs(s_my), in_=smine_f[0:16, :]), xde, 16)
            P.wait("sp", xde)
            return
        for tt in range(NTT):
            ts = TTS[tt]
            for kd in range(KD):
                pe_claim(ph_use[2])
                P.do("pe", lambda E, kd=kd, ts=ts: E.matmul(
                    PH[:, 2, :], l0_wblk[:, 2, :], xn[:, kd, ts],
                    start=True, stop=True), pe, 1)
                v_mm = pe.v
                P.wait("act", pe, v_mm)
                P.do("act", lambda E, kd=kd, tt=tt: E.activation(
                    l0_vs, PH[:, 2, :], AF.Copy,
                    accum_out=vsump_v[:, kd, tt:tt + 1]), act, 1)
                ph_use[2]["act"] = act.v
        P.wait("dve", act)
        P.do("dve", lambda E: E.tensor_tensor(vsum8_v, vsump_v[:, :, 0],
                                              vsump_v[:, :, 1], ALU.add), dve, 1)
        P.do("dve", lambda E: E.tensor_tensor(vsc2_v, vsump_v[:, :, 2],
                                              vsump_v[:, :, 3], ALU.add), dve, 1)
        P.do("dve", lambda E: E.tensor_tensor(vsum8_v, vsum8_v, vsc2_v, ALU.add), dve, 1)
        sall_act_use = [0, 0]
        l0e_pe_use = [0]
        inv_dve_use = [0]
        for tt in range(NTT):
            ts = TTS[tt]
            par = tt % 2
            P.wait("pool", act, sall_act_use[par])
            P.do("pool", lambda E, ts=ts, par=par: E.dma_start(
                out=l0_sall[:, par, :], in_=s_all[:, ts]), sls[par], 16)
            P.wait("act", sls[par])
            P.wait("act", pe, l0e_pe_use[0])
            P.do("act", lambda E, par=par: E.activation(
                l0_e, l0_sall[:, par, :], AF.Exp, scale=1.0 / 32.0), act, 1)
            e_done = act.v
            sall_act_use[par] = act.v
            P.wait("pe", act, e_done)
            pe_claim(ps1_use)
            pe_claim(py_use[0])
            P.do("pe", lambda E: E.matmul(PS1[0:16, :], ssb[:, :], l0_e,
                                          start=True, stop=True))
            P.do("pe", lambda E: E.matmul(PY[0:16, 0, :], myselb[:, :], l0_e,
                                          start=True, stop=True), pe, 1)
            se_done = pe.v
            l0e_pe_use[0] = pe.v
            P.wait("act", pe, se_done)
            P.do("act", lambda E: E.activation(st16[0:16, 0, :], PS1[0:16, :],
                                               AF.Ln), act, 1)
            ps1_use["act"] = act.v
            P.wait("act", dve, inv_dve_use[0])
            P.do("act", lambda E: E.activation(st16[0:16, 1, :], st16[0:16, 0, :],
                                               AF.Exp, scale=-1.0), act, 1)
            inv_done = act.v
            P.wait("dve", act, inv_done)
            P.do("dve", lambda E, ts=ts: E.tensor_tensor(
                l0_probs[0:16, ts], PY[0:16, 0, :], st16[0:16, 1, :], ALU.mult), dve, 1)
            py_use[0]["dve"] = dve.v
            inv_dve_use[0] = dve.v
        probs_done = dve.v
        P.wait("pe", dve, probs_done)
        for tt in range(NTT):
            ts = TTS[tt]
            for kd in range(KD):
                bk = kd % 4
                pe_claim(ph_use[bk])
                P.do("pe", lambda E, kd=kd, ts=ts, bk=bk: E.matmul(
                    PH[:, bk, :], selb[:, kd, :], l0_probs[0:16, ts],
                    start=True, stop=True), pe, 1)
                mm_done = pe.v
                P.wait("dve", pe, mm_done)
                P.do("dve", lambda E, kd=kd, ts=ts, bk=bk: E.scalar_tensor_tensor(
                    xsb[:, kd, ts], PH[:, bk, :], vsum8_v[:, kd:kd + 1], xsb[:, kd, ts],
                    ALU.mult, ALU.add), dve, 1)
                ph_use[bk]["dve"] = dve.v

    def emit_l0a_fused():
        emit_ln(xsb, xn)
        seacc = view(SCRF, 2048, (T,))
        P.wait("pe", wde)
        P.wait("pe", xde)
        pe_claim(ph_use[0])
        pe_claim(ph_use[1])
        P.do("pe", lambda E: E.matmul(PH[:, 0, 0:128], f_qbd, f_kbd,
                                      start=True, stop=True))
        P.do("pe", lambda E: E.matmul(PH[:, 1, 0:128], f_kbd, f_qbd,
                                      start=True, stop=True), pe, 1)
        mb = pe.v
        P.wait("dve", pe, mb)
        P.do("dve", lambda E: E.tensor_copy(f_mblk, PH[:, 0, 0:128]), dve, 1)
        P.do("dve", lambda E: E.tensor_copy(f_mblkT, PH[:, 1, 0:128]), dve, 1)
        ph_use[0]["dve"] = ph_use[1]["dve"] = dve.v
        mev = dve.v
        P.wait("pe", dve, mev)
        pe_claim(py_use[0])
        pe_claim(py_use[1])
        P.do("pe", lambda E: E.matmul(PY[:, 0, 0:1], f_mblk, cones[:, 0:1],
                                      start=True, stop=True))
        P.do("pe", lambda E: E.matmul(PY[:, 1, 0:1], f_mblkT, cones[:, 0:1],
                                      start=True, stop=True), pe, 1)
        wv_ = pe.v
        P.wait("dve", pe, wv_)
        P.do("dve", lambda E: E.tensor_copy(wcolf[:, :], PY[:, 0, 0:1]), dve, 1)
        P.do("dve", lambda E: E.tensor_tensor(wcolf[:, :], wcolf[:, :],
                                              PY[:, 1, 0:1], ALU.add), dve, 1)
        py_use[0]["dve"] = py_use[1]["dve"] = dve.v
        P.do("dve", lambda E: E.tensor_copy(f_m2x[:, 1:2], wcolf[:, :]), dve, 1)
        P.do("dve", lambda E: E.tensor_copy(f_m2x[:, 2:3], fbc[:, 4:5]), dve, 1)
        for kd in range(KD):
            P.do("dve", lambda E, kd=kd: E.tensor_scalar(
                f_wsel[:, kd, :], headsb[:, kd, :], wcolf[:, 0:1], None,
                ALU.mult), dve, 1)
        wsel_done = dve.v
        P.wait("pe", dve, wsel_done)
        pe_claim(ps1_use)
        P.do("pe", lambda E: E.matmul(PS1[0:1, 0:1], f_m2x[:, 1:2], f_m2x[:, 2:3],
                                      start=True, stop=True), pe, 1)
        m2s = pe.v
        P.wait("dve", pe, m2s)
        P.do("dve", lambda E: E.tensor_copy(f_m2x[0:1, 0:1], PS1[0:1, 0:1]), dve, 1)
        ps1_use["dve"] = dve.v
        m2e = dve.v
        P.wait("pe", dve, m2e)
        pe_claim(py_use[0])
        P.do("pe", lambda E: E.matmul(PY[:, 0, 0:1], conesrow[:, :],
                                      f_m2x[0:1, 0:1], start=True, stop=True), pe, 1)
        m2b = pe.v
        P.wait("dve", pe, m2b)
        P.do("dve", lambda E: E.tensor_scalar(m11f[:, :], PY[:, 0, 0:1], 0.5, None,
                                              ALU.mult), dve, 1)
        py_use[0]["dve"] = dve.v
        P.do("dve", lambda E: E.memset(seacc[0:16, :], 0.0), dve, 1)
        P.do("dve", lambda E: E.memset(f_em[0:16, :], 0.0), dve, 1)
        xb_pool_use = [0, 0]
        xb_pe_use = [0, 0]
        for ci in range(NE):
            for tt in range(NTT):
                ts = TTS[tt]
                par = (ci * NTT + tt) % 2
                P.wait("pool", dve, xb_pool_use[par])
                P.wait("pool", pe, xb_pe_use[par])
                P.do("pool", lambda E, ci=ci, ts=ts, par=par: E.dma_start(
                    out=f_xb[:, par, :, :],
                    in_=dr["xall"][ci].rearrange("(kd p) t -> p kd t", p=128)[:, :, ts]),
                    sls[par], 16)
                P.wait("pe", sls[par])
                last_stat = [0]
                for kd in range(KD):
                    zb = kd % 2
                    pe_claim(ph_use[zb])
                    if kd == 0:
                        pe_claim(ps1_use)
                        pe_claim(py_use[0])
                        pe_claim(py_use[1])
                        pe_claim(ph_use[2])
                    P.do("pe", lambda E, kd=kd, zb=zb, par=par: E.matmul(
                        PH[:, zb, :], f_mblkT, f_xb[:, par, kd, :],
                        start=True, stop=True), pe, 1)
                    zmm = pe.v
                    P.wait("dve", pe, zmm)
                    P.wait("dve", pe, last_stat[0])
                    P.do("dve", lambda E, kd=kd, zb=zb, par=par: E.tensor_tensor(
                        f_xz, f_xb[:, par, kd, :], PH[:, zb, :], ALU.mult), dve, 1)
                    ph_use[zb]["dve"] = dve.v
                    P.do("dve", lambda E, kd=kd, par=par: E.tensor_tensor(
                        f_sq, f_xb[:, par, kd, :], f_xb[:, par, kd, :],
                        ALU.mult), dve, 1)
                    sx = dve.v
                    P.wait("pe", dve, sx)
                    P.do("pe", lambda E, kd=kd, par=par: E.matmul(
                        PS1[0:16, :], cones[:, :], f_xb[:, par, kd, :],
                        start=(kd == 0), stop=(kd == KD - 1)))
                    P.do("pe", lambda E, kd=kd: E.matmul(
                        PY[0:16, 0, :], cones[:, :], f_sq,
                        start=(kd == 0), stop=(kd == KD - 1)))
                    P.do("pe", lambda E, kd=kd: E.matmul(
                        PY[0:16, 1, :], headsb[:, kd, :], f_xz,
                        start=(kd == 0), stop=(kd == KD - 1)))
                    P.do("pe", lambda E, kd=kd, par=par: E.matmul(
                        PH[0:16, 2, 0:512], f_wsel[:, kd, :], f_xb[:, par, kd, :],
                        start=(kd == 0), stop=(kd == KD - 1)), pe, 1)
                    last_stat[0] = pe.v
                stats_done = pe.v
                xb_pool_use[par] = dve.v
                xb_pe_use[par] = pe.v
                P.wait("act", pe, stats_done)
                P.do("act", lambda E: E.activation(st16[0:16, 0, :], PS1[0:16, :],
                                                   AF.Copy, scale=1.0 / 1024.0), act, 1)
                P.do("act", lambda E: E.activation(st16[0:16, 1, :], PY[0:16, 0, :],
                                                   AF.Copy, scale=1.0 / 1024.0), act, 1)
                P.wait("act", dve)
                P.do("act", lambda E: E.activation(PH[0:16, 3, 0:512], st16[0:16, 0, :],
                                                   AF.Square), act, 1)
                m2d = act.v
                P.wait("dve", act, m2d)
                P.do("dve", lambda E: E.tensor_tensor(st16[0:16, 1, :], st16[0:16, 1, :],
                                                      PH[0:16, 3, 0:512],
                                                      ALU.subtract), dve, 1)
                vd = dve.v
                P.wait("act", dve, vd)
                P.do("act", lambda E: E.activation(st16[0:16, 1, :], st16[0:16, 1, :],
                                                   AF.Ln, bias=fbc[0:16, 0:1]), act, 1)
                P.do("act", lambda E: E.activation(st16[0:16, 1, :], st16[0:16, 1, :],
                                                   AF.Exp, scale=-1.0), act, 1)
                ivd = act.v
                P.wait("dve", act, ivd)
                P.do("dve", lambda E: E.tensor_tensor(stg32[0:16, 0, :], st16[0:16, 0, :],
                                                      PH[0:16, 2, 0:512],
                                                      ALU.mult), dve, 1)
                P.do("dve", lambda E: E.tensor_tensor(stg32[0:16, 0, :], PY[0:16, 1, :],
                                                      stg32[0:16, 0, :],
                                                      ALU.subtract), dve, 1)
                P.do("dve", lambda E: E.scalar_tensor_tensor(
                    stg32[0:16, 0, :], PH[0:16, 3, 0:512], m11f[0:16, 0:1],
                    stg32[0:16, 0, :], ALU.mult, ALU.add), dve, 1)
                P.do("dve", lambda E: E.tensor_tensor(stg32[0:16, 0, :],
                                                      stg32[0:16, 0, :],
                                                      st16[0:16, 1, :], ALU.mult),
                     dve, 1)
                sdone = dve.v
                ps1_use["dve"] = py_use[0]["dve"] = py_use[1]["dve"] = dve.v
                ph_use[2]["dve"] = ph_use[3]["dve"] = dve.v
                ps1_use["act"] = py_use[0]["act"] = act.v
                P.wait("act", dve, sdone)
                P.do("act", lambda E: E.activation(f_et[0:16, :], stg32[0:16, 0, :],
                                                   AF.Exp, scale=1.0 / 32.0), act, 1)
                etd = act.v
                P.wait("dve", act, etd)
                P.do("dve", lambda E, ts=ts: E.tensor_tensor(
                    seacc[0:16, ts], seacc[0:16, ts], f_et[0:16, :], ALU.add), dve, 1)
                P.do("dve", lambda E, ci=ci, ts=ts: E.scalar_tensor_tensor(
                    f_em[0:16, ts], f_et[0:16, :], miselb[:, ci:ci + 1],
                    f_em[0:16, ts], ALU.mult, ALU.add), dve, 1)
        for tt in range(NTT):
            ts = TTS[tt]
            P.wait("act", dve)
            P.do("act", lambda E, ts=ts: E.activation(st16[0:16, 0, :], seacc[0:16, ts],
                                                      AF.Ln), act, 1)
            P.do("act", lambda E: E.activation(st16[0:16, 0, :], st16[0:16, 0, :],
                                               AF.Exp, scale=-1.0), act, 1)
            ivse = act.v
            P.wait("dve", act, ivse)
            P.do("dve", lambda E, ts=ts: E.tensor_tensor(
                l0_probs[0:16, ts], f_em[0:16, ts], st16[0:16, 0, :], ALU.mult),
                dve, 1)
        probs_done = dve.v
        for tt in range(NTT):
            ts = TTS[tt]
            for kd in range(KD):
                pe_claim(ph_use[2])
                P.do("pe", lambda E, kd=kd, ts=ts: E.matmul(
                    PH[:, 2, :], l0_wblk[:, 2, :], xn[:, kd, ts],
                    start=True, stop=True), pe, 1)
                v_mm = pe.v
                P.wait("act", pe, v_mm)
                P.do("act", lambda E, kd=kd, tt=tt: E.activation(
                    l0_vs, PH[:, 2, :], AF.Copy,
                    accum_out=vsump_v[:, kd, tt:tt + 1]), act, 1)
                ph_use[2]["act"] = act.v
        P.wait("dve", act)
        P.do("dve", lambda E: E.tensor_tensor(vsum8_v, vsump_v[:, :, 0],
                                              vsump_v[:, :, 1], ALU.add), dve, 1)
        P.do("dve", lambda E: E.tensor_tensor(vsc2_v, vsump_v[:, :, 2],
                                              vsump_v[:, :, 3], ALU.add), dve, 1)
        P.do("dve", lambda E: E.tensor_tensor(vsum8_v, vsum8_v, vsc2_v, ALU.add),
             dve, 1)
        vs_done = dve.v
        P.wait("pe", dve, max(probs_done, vs_done))
        for tt in range(NTT):
            ts = TTS[tt]
            for kd in range(KD):
                bk = kd % 4
                pe_claim(ph_use[bk])
                P.do("pe", lambda E, kd=kd, ts=ts, bk=bk: E.matmul(
                    PH[:, bk, :], selb[:, kd, :], l0_probs[0:16, ts],
                    start=True, stop=True), pe, 1)
                mm_done = pe.v
                P.wait("dve", pe, mm_done)
                P.do("dve", lambda E, kd=kd, ts=ts, bk=bk: E.scalar_tensor_tensor(
                    xsb[:, kd, ts], PH[:, bk, :], vsum8_v[:, kd:kd + 1], xsb[:, kd, ts],
                    ALU.mult, ALU.add), dve, 1)
                ph_use[bk]["dve"] = dve.v

    def emit_ffn(L):
        P.wait("sp", pe)
        P.do("sp", lambda E: E.dma_start(out=_fs(wrsb), in_=_fs(dr[f"w_l{L}wr"])),
             S["wrs"], 16)
        P.wait("pe", S["wrs"])

        def router(kd, tt, stage_ap):
            if kd == 0:
                pe_claim(ph_use[tt])
            P.do("pe", lambda E, kd=kd, tt=tt, ap=stage_ap: E.matmul(
                PH[0:8, tt, :], wrsb[:, kd, :], ap,
                start=(kd == 0), stop=(kd == KD - 1)), pe, 1)

        emit_ln(xsb, xn, router=router)
        if L == 1:
            pass
        for tt in range(NTT):
            ts = TTS[tt]
            P.wait("dve", pe)
            P.do("dve", lambda E, tt=tt: E.tensor_scalar(
                lftt_v[0:8, tt % 2, :], PH[0:8, tt, :], brsb[:, L:L + 1], None,
                ALU.add), dve, 1)
            ph_use[tt]["dve"] = dve.v
            lf_done = dve.v
            for ci in range(4):
                ct = tt * 4 + ci
                cs = slice(ci * 128, (ci + 1) * 128)
                slot = ci % 2
                P.wait("pe", dve, lf_done)
                pe_claim(py_use[slot])
                P.do("pe", lambda E, cs=cs, slot=slot, tt=tt: E.transpose(
                    PY[:, slot, 0:8], lftt_v[0:8, tt % 2, cs], idf[0:8, 0:8]), pe, 1)
                tr_done = pe.v
                P.wait("dve", pe, tr_done)
                P.do("dve", lambda E, slot=slot: E.tensor_reduce(
                    cols_v[:, 0:1], PY[:, slot, 0:8], AX.X, ALU.max), dve, 1)
                if L == 1:
                    P.do("dve", lambda E, slot=slot: E.tensor_reduce(
                        cols_v[:, 1:2], PY[:, slot, 0:8], AX.X, ALU.max,
                        negate=True), dve, 1)
                P.do("dve", lambda E, slot=slot: E.tensor_scalar(
                    ohtm[:, :], PY[:, slot, 0:8], cols_v[:, 0:1], None,
                    ALU.is_equal), dve, 1)
                oh_done = dve.v
                py_use[slot]["dve"] = dve.v
                if L == 0:
                    P.wait("act", dve, oh_done)
                    P.do("act", lambda E, ct=ct: E.activation(
                        gcolb[:, ct:ct + 1], cols_v[:, 0:1], AF.Sigmoid), act, 1)
                else:
                    P.wait("act", dve, oh_done)
                    P.do("act", lambda E, slot=slot: E.activation(
                        ex_v[:, :], PY[:, slot, 0:8], AF.Exp,
                        bias=cols_v[:, 1:2]), act, 1)
                    exd = act.v
                    py_use[slot]["act"] = act.v
                    P.wait("dve", act, exd)
                    P.do("dve", lambda E, ct=ct: E.tensor_reduce(
                        secol_v[:, ct:ct + 1], ex_v[:, :], AX.X, ALU.add), dve, 1)
                P.wait("pe", dve, oh_done)
                P.wait("pe", dve, pt_oh_use[0])
                P.do("pe", lambda E: E.transpose(pt_oh[:, :], ohtm[:, :], idb[:, :]), pe, 1)
                trb = pe.v
                P.wait("dve", pe, trb)
                P.do("dve", lambda E, ct=ct: E.tensor_copy(
                    ohfm_ct[0:8, ct, :], pt_oh[0:8, :]), dve, 1)
                pt_oh_use[0] = dve.v
        if L == 1:
            P.do("dve", lambda E: E.reciprocal(ginv_v[:, :], secol_v[:, :]), dve, 1)
            P.do("dve", lambda E: E.tensor_copy(gcolb[:, :], ginv_v[:, :]), dve, 1)
        g_done = dve.v
        P.wait("pe", dve, g_done)
        P.wait("pe", act)
        grow_pe_use = [0]
        for tt in range(NTT):
            for ci in range(4):
                ct = tt * 4 + ci
                P.wait("pe", dve, ptrow_use[0])
                P.do("pe", lambda E, ct=ct: E.transpose(
                    ptrow[:, :], gcolb[:, ct:ct + 1], idb[:, :]), pe, 1)
                trg = pe.v
                P.wait("dve", pe, trg)
                P.wait("dve", pe, grow_pe_use[0])
                P.do("dve", lambda E, ci=ci: E.tensor_copy(
                    grow_sm[0:1, ci * 128:(ci + 1) * 128], ptrow[0:1, :]), dve, 1)
                ptrow_use[0] = dve.v
            gr = dve.v
            P.wait("pe", dve, gr)
            pe_claim(ps1_use)
            P.do("pe", lambda E: E.matmul(PS1[:, :], conesrow[:, :], grow_sm[0:1, :],
                                          start=True, stop=True), pe, 1)
            gb = pe.v
            grow_pe_use[0] = pe.v
            P.wait("dve", pe, gb)
            P.do("dve", lambda E, tt=tt: E.tensor_copy(gateb_v[:, TTS[tt]], PS1[:, :]),
                 dve, 1)
            ps1_use["dve"] = dve.v
        if L == 1:
            emit_ln(xn, xn, affine=True)
        oh_ready = dve.v
        w1_dr, w2_dr = dr[f"w_l{L}w1"], dr[f"w_l{L}w2"]
        w1_pe_use = [pe.v, pe.v]
        w2_pe_use = [pe.v, pe.v]
        w1n, w2n = [0], [0]
        def emit_gmask(e):
            gv = gmask_v if e % 2 == 0 else gmask2_v
            P.wait("pe", dve, oh_ready)
            for tt in range(NTT):
                pe_claim(ps1_use)
                P.do("pe", lambda E, e=e, tt=tt: E.matmul(
                    PS1[:, :], eones[:, e, :], ohfm_v[0:8, TTS[tt]],
                    start=True, stop=True), pe, 1)
                gm = pe.v
                P.wait("dve", pe, gm)
                P.do("dve", lambda E, tt=tt, gv=gv: E.tensor_tensor(
                    gv[:, TTS[tt]], PS1[:, :], gateb_v[:, TTS[tt]], ALU.mult),
                    dve, 1)
                ps1_use["dve"] = dve.v
            return dve.v

        gmask_done = emit_gmask(0)
        for e in range(NE):
            for fq in range(NFQ):
                for fc8 in range(8):
                    fc = fq * 8 + fc8
                    par = w1n[0] % 2
                    P.wait("pool", pe, w1_pe_use[par])
                    P.do("pool", lambda E, e=e, fc=fc, par=par: E.dma_start(
                        out=w1c_v[:, par, :, :], in_=w1_dr[e, fc, :, :, :]),
                        w1s[par], 16)
                    w1n[0] += 1
                    P.wait("pe", w1s[par])
                    for tp in range(2):
                        for tt in (2 * tp, 2 * tp + 1):
                            pe_claim(ph_use[tt])
                            for kd in range(KD):
                                P.do("pe", lambda E, kd=kd, tt=tt, par=par: E.matmul(
                                    PH[:, tt, :], w1c_v[:, par, kd, :], xn[:, kd, TTS[tt]],
                                    start=(kd == 0), stop=(kd == KD - 1)),
                                    pe if kd == KD - 1 else None, 1)
                        mmv = pe.v
                        P.wait("act", pe, mmv)
                        php = PH[:, 2 * tp:2 * tp + 2, :].rearrange("p a b -> p (a b)")
                        if L == 0:
                            P.do("act", lambda E, fc8=fc8, tp=tp, php=php: E.activation(
                                hq_v[:, fc8, TTP[tp]], php, AF.Gelu), act, 1)
                        else:
                            P.do("act", lambda E, fc8=fc8, tp=tp, e=e, fc=fc, php=php:
                                 E.activation(
                                     hq_v[:, fc8, TTP[tp]], php, AF.Relu,
                                     bias=b1sb[:, e, fc:fc + 1]), act, 1)
                        ph_use[2 * tp]["act"] = act.v
                        ph_use[2 * tp + 1]["act"] = act.v
                    w1_pe_use[par] = pe.v
                h_done = act.v
                for dout in range(8):
                    par = w2n[0] % 2
                    P.wait("pool", pe, w2_pe_use[par])
                    P.do("pool", lambda E, e=e, fq=fq, dout=dout, par=par: E.dma_start(
                        out=w2c_v[:, par, :, :], in_=w2_dr[e, fq, dout, :, :, :]),
                        w2s[par], 16)
                    w2n[0] += 1
                    P.wait("pe", w2s[par])
                    P.wait("pe", act, h_done)
                    if dout == 0 and fq == 0:
                        P.wait("pe", dve, gmask_done)
                    for tp in range(2):
                        for tt in (2 * tp, 2 * tp + 1):
                            pe_claim(ph_use[tt])
                            for fc8 in range(8):
                                P.do("pe", lambda E, fc8=fc8, tt=tt, par=par: E.matmul(
                                    PH[:, tt, :], w2c_v[:, par, fc8, :],
                                    hq_v[:, fc8, TTS[tt]],
                                    start=(fc8 == 0), stop=(fc8 == 7)),
                                    pe if fc8 == 7 else None, 1)
                        mmv = pe.v
                        P.wait("dve", pe, mmv)
                        php = PH[:, 2 * tp:2 * tp + 2, :].rearrange("p a b -> p (a b)")
                        if fq == 0:
                            P.do("dve", lambda E, dout=dout, tp=tp, php=php:
                                 E.tensor_copy(yac_v[:, dout, TTP[tp]], php), dve, 1)
                        elif fq < 3 or L == 0:
                            P.do("dve", lambda E, dout=dout, tp=tp, php=php:
                                 E.tensor_tensor(
                                     yac_v[:, dout, TTP[tp]], yac_v[:, dout, TTP[tp]],
                                     php, ALU.add), dve, 1)
                        else:
                            P.do("dve", lambda E, dout=dout, tp=tp, e=e, php=php:
                                 E.scalar_tensor_tensor(
                                     yac_v[:, dout, TTP[tp]], php,
                                     b2sb[:, e, dout:dout + 1], yac_v[:, dout, TTP[tp]],
                                     ALU.add, ALU.add), dve, 1)
                        ph_use[2 * tp]["dve"] = dve.v
                        ph_use[2 * tp + 1]["dve"] = dve.v
                    w2_pe_use[par] = pe.v
            if e + 1 < NE:
                gmask_done = emit_gmask(e + 1)
            gv = gmask_v if e % 2 == 0 else gmask2_v
            for dout in range(8):
                P.do("dve", lambda E, dout=dout, gv=gv: E.tensor_tensor(
                    yac_v[:, dout, :], yac_v[:, dout, :], gv[:, :],
                    ALU.mult), dve, 1)
                P.do("dve", lambda E, dout=dout: E.tensor_tensor(
                    xsb[:, dout, :], xsb[:, dout, :], yac_v[:, dout, :],
                    ALU.add), dve, 1)

    def emit_l1a():
        if L1A_TRUNC < 0:
            return
        emit_ln(xsb, xn)
        if L1A_TRUNC < 1:
            return
        P.do("dve", lambda E: E.memset(Sf_v, 0.0), dve, 1)
        P.do("dve", lambda E: E.memset(sbf_v, 0.0), dve, 1)
        P.do("dve", lambda E: E.memset(kds_v, 0.0), dve, 1)
        for tt in range(NTT):
            ts = TTS[tt]
            pe_claim(ps1_use)
            for kd in range(KD):
                P.do("pe", lambda E, kd=kd, ts=ts: E.matmul(
                    PS1[0:16, :], wfTsb[:, kd, :], xn[:, kd, ts],
                    start=(kd == 0), stop=(kd == KD - 1)),
                    pe if kd == KD - 1 else None, 1)
            f_done = pe.v
            P.wait("dve", pe, f_done)
            P.do("dve", lambda E: E.tensor_scalar(st16[0:16, 1, :], PS1[0:16, :],
                                                  dlt[:, 0:1], 0.0, ALU.add, ALU.min),
                 dve, 1)
            min_done = dve.v
            ps1_use["dve"] = dve.v
            P.wait("act", pe, f_done)
            P.wait("act", dve, min_done)
            P.do("act", lambda E: E.activation(st16[0:16, 0, :], PS1[0:16, :], AF.Abs,
                                               bias=dlt[:, 0:1]), act, 1)
            P.do("act", lambda E: E.activation(st16[0:16, 0, :], st16[0:16, 0, :],
                                               AF.Exp, scale=-1.0), act, 1)
            P.do("act", lambda E: E.activation(st16[0:16, 0, :], st16[0:16, 0, :],
                                               AF.Ln, bias=fbc[0:16, 2:3]), act, 1)
            t3_done = act.v
            ps1_use["act"] = t3_done
            P.wait("dve", act, t3_done)
            P.do("dve", lambda E: E.tensor_tensor(st16[0:16, 1, :], st16[0:16, 1, :],
                                                  st16[0:16, 0, :], ALU.subtract), dve, 1)
            for ci in range(4):
                c0 = ci * 128
                P.do("dve", lambda E, c0=c0, ts=ts: E.tensor_tensor_scan(
                    lbbf_v[0:16, ts.start + c0:ts.start + c0 + 128],
                    st16[0:16, 1, c0:c0 + 128], st16[0:16, 1, c0:c0 + 128],
                    0.0, ALU.add, ALU.bypass), dve, 1)
        lb_done = dve.v
        if L1A_TRUNC < 2:
            return
        P.wait("act", dve, lb_done)
        P.do("act", lambda E: E.activation(elbbf_v[0:16, :], lbbf_v[0:16, :], AF.Exp),
             act, 1)
        P.do("dve", lambda E: E.tensor_copy(totf_v[0:16, :], lbbf_v[0:16, 127::128]),
             dve, 1)
        tot_done = dve.v
        P.wait("act", dve, tot_done)
        P.do("act", lambda E: E.activation(etot_bf[0:16, :], totf_v[0:16, :], AF.Exp),
             act, 1)
        etot_done = act.v
        for c in range(16):
            P.do("dve", lambda E, c=c: E.tensor_scalar(
                sfx_v[0:16, c * 128:(c + 1) * 128], lbbf_v[0:16, c * 128:(c + 1) * 128],
                totf_v[0:16, c:c + 1], -1.0, ALU.subtract, ALU.mult), dve, 1)
        sfx_done = dve.v
        P.wait("pe", act, etot_done)
        for kd in range(KD):
            bk = kd % 2
            pe_claim(py_use[bk])
            P.do("pe", lambda E, kd=kd, bk=bk: E.matmul(
                PY[:, bk, 0:16], selb[:, kd, :], etot_bf[0:16, :],
                start=True, stop=True), pe, 1)
            bc = pe.v
            P.wait("dve", pe, bc)
            P.do("dve", lambda E, kd=kd, bk=bk: E.tensor_copy(
                etotbc_v[:, kd, :], PY[:, bk, 0:16]), dve, 1)
            py_use[bk]["dve"] = dve.v
        P.wait("pe", dve, sfx_done)
        for c in range(16):
            P.wait("pe", act)
            P.do("pe", lambda E, c=c: E.transpose(
                pt_lb[:, :], lbbf_v[0:16, c * 128:(c + 1) * 128], idb[0:16, 0:16]), pe, 1)
            t1 = pe.v
            P.wait("act", pe, t1)
            P.do("act", lambda E, c=c: E.activation(enltm_v[:, c, :], pt_lb[:, :],
                                                    AF.Exp, scale=-1.0), act, 1)
            en_done = act.v
            P.wait("pe", act, en_done)
            P.do("pe", lambda E, c=c: E.transpose(
                pt_lb[:, :], sfx_v[0:16, c * 128:(c + 1) * 128], idb[0:16, 0:16]), pe, 1)
            t2 = pe.v
            P.wait("act", pe, t2)
            P.do("act", lambda E, c=c: E.activation(ktab_v[:, c, :], pt_lb[:, :],
                                                    AF.Exp), act, 1)
        tables_act = act.v
        if L1A_TRUNC < 3:
            return

        wq_dr, wk_dr, wv_dr, wo_dr = (dr["w_l1wq"], dr["w_l1wk"], dr["w_l1wv"],
                                      dr["w_l1wo"])
        ws_pe_use = [pe.v, pe.v]
        wsn = [0]

        def stream_w(src):
            par = wsn[0] % 2
            P.wait("pool", pe, ws_pe_use[par])
            P.do("pool", lambda E, src=src, par=par: E.dma_start(
                out=wstream_v[:, par, :, :], in_=_fs(src)), wss[par], 16)
            wsn[0] += 1
            return par, wss[par].v

        def pA(h):
            return PH[:, h // 4, (h % 4) * 128:(h % 4) * 128 + 128]

        def pO(h):
            half, r = 64 * (h % 2), h // 2
            return PY[half:half + 64, r // 4, (r % 4) * 128:(r % 4) * 128 + 128]

        def pS(h):
            half, r = 64 * (h % 2), h // 2
            return PS1[half:half + 64, r * 64:r * 64 + 64]

        def hsl(h):
            return slice(64 * (h % 2), 64 * (h % 2) + 64)

        first_wave = True
        for w in range(NW):
            wts = slice(w * NT, (w + 1) * NT)
            par, wv_ = stream_w(wq_dr)
            P.wait("pe", wss[par], wv_)
            for ec in range(KD):
                bk = ec % 4
                pe_claim(ph_use[bk])
                for kd in range(KD):
                    P.do("pe", lambda E, ec=ec, kd=kd, bk=bk, par=par, wts=wts: E.matmul(
                        PH[:, bk, :], wstream_v[:, par, kd, ec * 128:(ec + 1) * 128],
                        xn[:, kd, wts], start=(kd == 0), stop=(kd == KD - 1)),
                        pe if kd == KD - 1 else None, 1)
                qd = pe.v
                P.wait("act", pe, qd)
                P.do("act", lambda E, ec=ec, bk=bk: E.activation(
                    qfm_v[:, ec, :], PH[:, bk, :], AF.Silu), act, 1)
                ph_use[bk]["act"] = act.v
            ws_pe_use[par] = pe.v
            par, wv_ = stream_w(wk_dr)
            P.wait("pe", wss[par], wv_)
            for ec in range(KD):
                bk = ec % 4
                pe_claim(ph_use[bk])
                for kd in range(KD):
                    P.do("pe", lambda E, ec=ec, kd=kd, bk=bk, par=par, wts=wts: E.matmul(
                        PH[:, bk, :], wstream_v[:, par, kd, ec * 128:(ec + 1) * 128],
                        xn[:, kd, wts], start=(kd == 0), stop=(kd == KD - 1)),
                        pe if kd == KD - 1 else None, 1)
                kdn = pe.v
                P.wait("act", pe, kdn)
                P.do("act", lambda E, ec=ec, bk=bk: E.activation(
                    kfm_v[:, ec, :], PH[:, bk, :], AF.Silu), act, 1)
                ph_use[bk]["act"] = act.v
                kf_done = act.v
                P.wait("dve", act, kf_done)
                P.do("dve", lambda E, ec=ec: E.tensor_scalar(
                    kmsk_v[:, 0, ec, :], kfm_v[:, ec, :], fbc[:, 4:5], None,
                    ALU.mult), dve, 1)
                P.do("dve", lambda E, ec=ec: E.tensor_scalar(
                    kmsk_v[:, 1, ec, :], kfm_v[:, ec, :], fbc[:, 5:6], None,
                    ALU.mult), dve, 1)
            for ct in range(CPW):
                c0 = wts.start + ct * 128
                for eh in range(2):
                    bk = (ct * 2 + eh) % 4
                    pe_claim(ph_use[bk])
                    for kd in range(KD):
                        P.do("pe", lambda E, kd=kd, bk=bk, par=par, c0=c0, eh=eh:
                             E.matmul(PH[:, bk, :], xn[:, kd, c0:c0 + 128],
                                      wstream_v[:, par, kd, eh * 512:(eh + 1) * 512],
                                      start=(kd == 0), stop=(kd == KD - 1)),
                             pe if kd == KD - 1 else None, 1)
                    ktd = pe.v
                    P.wait("act", pe, ktd)
                    P.do("act", lambda E, ct=ct, eh=eh, bk=bk: E.activation(
                        ktm_v[:, ct, eh * 512:(eh + 1) * 512], PH[:, bk, :], AF.Silu),
                        act, 1)
                    ph_use[bk]["act"] = act.v
            ws_pe_use[par] = pe.v
            par, wv_ = stream_w(wv_dr)
            P.wait("pe", wss[par], wv_)
            for ct in range(CPW):
                c0 = wts.start + ct * 128
                for eh in range(2):
                    bk = (ct * 2 + eh) % 4
                    pe_claim(ph_use[bk])
                    for kd in range(KD):
                        P.do("pe", lambda E, kd=kd, bk=bk, par=par, c0=c0, eh=eh:
                             E.matmul(PH[:, bk, :], xn[:, kd, c0:c0 + 128],
                                      wstream_v[:, par, kd, eh * 512:(eh + 1) * 512],
                                      start=(kd == 0), stop=(kd == KD - 1)),
                             pe if kd == KD - 1 else None, 1)
                    vtd = pe.v
                    P.wait("dve", pe, vtd)
                    P.do("dve", lambda E, ct=ct, eh=eh, bk=bk: E.tensor_copy(
                        vtm_v[:, ct, eh * 512:(eh + 1) * 512], PH[:, bk, :]), dve, 1)
                    ph_use[bk]["dve"] = dve.v
            ws_pe_use[par] = pe.v
            proj_act = act.v
            proj_dve = dve.v
            if L1A_TRUNC < 4:
                continue
            def pA(k):
                return PH[:, k // 4, (k % 4) * 128:(k % 4) * 128 + 128]

            def pO(k):
                return PY[:, k // 4, (k % 4) * 128:(k % 4) * 128 + 128]

            def pS(k):
                return PS1[:, k * 64:k * 64 + 64]

            def hsl(h):
                return slice(64 * (h % 2), 64 * (h % 2) + 64)

            if first_wave:
                P.wait("dve", act, tables_act)
                first_wave = False
            import os as _o8
            _scan_stop = int(_o8.environ.get("SCAN_STOP", "99"))
            for cl in range(CPW):
                c = w * CPW + cl
                if c > _scan_stop:
                    continue
                csl = slice(cl * 128, (cl + 1) * 128)
                P.wait("pe", act, proj_act)
                P.wait("pe", dve, dve.v)
                for kd in range(KD):
                    if c > int(_o8.environ.get("SKIP_ELB_AFTER", "99")):
                        continue
                    bk = kd % 2
                    P.wait("pe", dve)
                    P.do("pe", lambda E, kd=kd, bk=bk, csl=csl, w=w: E.matmul(
                        PH[:, bk, 0:128], selb[:, kd, :],
                        elbbf_v[0:16, w * NT + cl * 128:w * NT + cl * 128 + 128]
                        if False else elbbf_v[0:16, csl.start + w * NT:csl.stop + w * NT],
                        start=True, stop=True), pe, 1)
                    eb = pe.v
                    P.wait("dve", pe, eb)
                    P.do("dve", lambda E, kd=kd, bk=bk: E.tensor_copy(
                        elbbc_v[:, kd, :], PH[:, bk, 0:128]), dve, 1)
                for hg in range(2):
                    hb = hg * 8
                    P.wait("pe", dve, dve.v)
                    for k in range(8):
                        h = hb + k
                        P.do("pe", lambda E, h=h, k=k, hg=hg, csl=csl: E.matmul(
                            pA(hg * 8 + k), kmsk_v[:, h % 2, h // 2, csl],
                            qfm_v[:, h // 2, csl], start=True, stop=True),
                            pe if k == 7 else None, 1)
                    mma_done = pe.v
                    P.wait("dve", pe, mma_done)
                    for k in range(8):
                        h = hb + k
                        if c > int(_o8.environ.get("SKIP_A2_AFTER", "99")):
                            continue
                        P.do("dve", lambda E, h=h, k=k, hg=hg, c=c: E.scalar_tensor_tensor(
                            a2_v[:, h, :], pA(hg * 8 + k), enltm_v[:, c, h:h + 1],
                            maskt[:, :], ALU.mult, ALU.mult), dve, 1)
                    a2_done = dve.v
                    P.wait("pe", dve, a2_done)
                    for k in range(8):
                        h = hb + k
                        P.do("pe", lambda E, h=h, k=k, cl=cl: E.matmul(
                            pO(k), vtm_v[:, cl, (h // 2) * 128:(h // 2) * 128 + 128],
                            a2_v[:, h, :], start=True, stop=True),
                            pe if k == 7 else None, 1)
                    mmoi_done = pe.v
                    for pr in range(hb // 2, hb // 2 + 4):
                        P.do("dve", lambda E, pr=pr, csl=csl: E.tensor_tensor(
                            qs_v[:, pr % 4, :], qfm_v[:, pr, csl],
                            elbbc_v[:, pr, :], ALU.mult), dve, 1)
                    qs_done = dve.v
                    P.wait("pe", dve, qs_done)
                    for k in range(8):
                        h = hb + k
                        P.do("pe", lambda E, h=h, k=k, hg=hg: E.matmul(
                            pA(hg * 8 + k), sbf_v[:, h, :], qs_v[:, (h // 2) % 4, :],
                            start=True, stop=True), pe if k == 7 else None, 1)
                    mmoe_done = pe.v
                    P.wait("dve", pe, mmoe_done)
                    for k in range(8):
                        h = hb + k
                        P.do("dve", lambda E, h=h, k=k, csl=csl: E.tensor_tensor(
                            obf_v[hsl(h), h // 2, csl], pO(k)[hsl(h), :],
                            elbbc_v[hsl(h), h // 2, :], ALU.mult), dve, 1)
                    for k in range(8):
                        h = hb + k
                        P.do("dve", lambda E, h=h, k=k, hg=hg, csl=csl: E.tensor_tensor(
                            obf_v[hsl(h), h // 2, csl], obf_v[hsl(h), h // 2, csl],
                            pA(hg * 8 + k)[hsl(h), :], ALU.add), dve, 1)
                    for k in range(8):
                        h = hb + k
                        if c > int(_o8.environ.get("SKIP_KDS_AFTER", "99")):
                            continue
                        P.do("dve", lambda E, h=h, cl=cl, c=c: E.tensor_scalar(
                            kds_v[:, h // 2 % 4, hsl(h)],
                            ktm_v[:, cl, h * 64:h * 64 + 64],
                            ktab_v[:, c, h:h + 1], None, ALU.mult), dve, 1)
                    kds_done = dve.v
                    P.wait("pe", dve, kds_done)
                    for k in range(8):
                        h = hb + k
                        P.do("pe", lambda E, h=h, k=k, cl=cl: E.matmul(
                            pS(k), kds_v[:, h // 2 % 4, :],
                            vtm_v[:, cl, h * 64:h * 64 + 64],
                            start=True, stop=True), pe if k == 7 else None, 1)
                    mms_done = pe.v
                    P.wait("dve", pe, mms_done)
                    for k in range(8):
                        h = hb + k
                        if c > int(_o8.environ.get("SKIP_SF_AFTER", "99")):
                            continue
                        P.do("dve", lambda E, h=h, k=k, c=c: E.scalar_tensor_tensor(
                            Sf_v[hsl(h), h // 2, :], Sf_v[hsl(h), h // 2, :],
                            etotbc_v[hsl(h), h // 2, c:c + 1], pS(k)[hsl(h), :],
                            ALU.mult, ALU.add), dve, 1)
                    for k in range(8):
                        h = hb + k
                        if c > int(_o8.environ.get("SKIP_SBF_AFTER", "99")):
                            continue
                        P.do("dve", lambda E, h=h: E.tensor_copy(
                            sbf_v[hsl(h), h, hsl(h)], Sf_v[hsl(h), h // 2, :]), dve, 1)
                for i in range(4):
                    ph_use[i]["dve"] = dve.v
                py_use[0]["dve"] = py_use[1]["dve"] = dve.v
                ps1_use["dve"] = dve.v
            if L1A_TRUNC < 5:
                continue
            scan_done = dve.v
            sq_pe = [0, 0]
            pe_claim(ps1_use)
            P.wait("act", dve, scan_done)
            for kd in range(KD):
                par2 = kd % 2
                P.wait("act", pe, sq_pe[par2])
                P.do("act", lambda E, kd=kd, par2=par2: E.activation(
                    sqt[:, par2, :], obf_v[:, kd, :], AF.Square), act, 1)
                sqd = act.v
                P.wait("pe", act, sqd)
                P.do("pe", lambda E, par2=par2, kd=kd: E.matmul(
                    PS1[0:16, :], cones[:, :], sqt[:, par2, :],
                    start=(kd == 0), stop=(kd == KD - 1)), pe, 1)
                sq_pe[par2] = pe.v
            oss_done = pe.v
            P.wait("act", pe, oss_done)
            P.do("act", lambda E: E.activation(st16[0:16, 0, :], PS1[0:16, :], AF.Copy,
                                               scale=1.0 / 1024.0), act, 1)
            ps1_use["act"] = act.v
            P.do("act", lambda E: E.activation(st16[0:16, 0, :], st16[0:16, 0, :],
                                               AF.Ln, bias=fbc[0:16, 1:2]), act, 1)
            P.wait("act", pe, rows_war[0])
            P.do("act", lambda E: E.activation(rows_sm[0:1, 1, :], st16[0:1, 0, :],
                                               AF.Exp, scale=-0.5), act, 1)
            orstd_done = act.v
            P.wait("pe", act, orstd_done)
            pe_claim(py_use[0])
            P.do("pe", lambda E: E.matmul(PY[:, 0, :], conesrow_f[:, :],
                                          rows_sm[0:1, 1, :], start=True, stop=True),
                 pe, 1)
            ob = pe.v
            rows_war[0] = pe.v
            P.wait("dve", pe, ob)
            import os as _o9
            for kd in range(KD):
                if _o9.environ.get("SKIP_RMS") == "1":
                    continue
                P.do("dve", lambda E, kd=kd: E.scalar_tensor_tensor(
                    obf_v[:, kd, :], PY[:, 0, :], nwsb[:, kd:kd + 1], obf_v[:, kd, :],
                    ALU.mult, ALU.mult), dve, 1)
            py_use[0]["dve"] = dve.v
            on_done = dve.v
            par, wv_ = stream_w(wo_dr)
            P.wait("pe", wss[par], wv_)
            P.wait("pe", dve, on_done)
            for ec in range(KD):
                bk = ec % 4
                pe_claim(ph_use[bk])
                for kd in range(KD):
                    P.do("pe", lambda E, ec=ec, kd=kd, bk=bk, par=par: E.matmul(
                        PH[:, bk, :], wstream_v[:, par, kd, ec * 128:(ec + 1) * 128],
                        obf_v[:, kd, :], start=(kd == 0), stop=(kd == KD - 1)),
                        pe if kd == KD - 1 else None, 1)
                od = pe.v
                P.wait("dve", pe, od)
                P.do("dve", lambda E, ec=ec, bk=bk, wts=wts: E.tensor_tensor(
                    xsb[:, ec, wts], xsb[:, ec, wts], PH[:, bk, :], ALU.add), dve, 1)
                ph_use[bk]["dve"] = dve.v
            ws_pe_use[par] = pe.v

    emits = {"l0a": emit_l0a, "l0a_scores": emit_l0a, "l0a_f": emit_l0a_fused,
             "l0f": lambda: emit_ffn(0), "l1f": lambda: emit_ffn(1), "l1a": emit_l1a}
    for b in blocks:
        emits[b]()
        dump("x_" + b, xsb[:, :, :], [128, KD, T], BF16)

    P.wait("pool", pe)
    P.wait("pool", dve)
    P.wait("pool", act)
    P.do("pool", lambda E: E.dma_start(
        out=xout.rearrange("(kd p) t -> p kd t", p=128), in_=xsb[:, :, :]), ode, 16)
    P.wait("pool", ode)
    return P, dumps_made



def _consts():
    c = {}
    c["c_ones"] = np.ones((128, 16), np.float32)
    c["c_onesrow"] = np.ones((1, 128), np.float32)
    sel = np.zeros((16, 8, 128), np.float32)
    for kd in range(8):
        sel[2 * kd, kd, 0:64] = 1.0
        sel[2 * kd + 1, kd, 64:128] = 1.0
    c["c_sel"] = sel
    hs = np.zeros((128, 8, 16), np.float32)
    for kd in range(8):
        for p in range(128):
            hs[p, kd, 2 * kd + (p >= 64)] = 1.0
    c["c_headsel"] = hs
    ss = np.zeros((128, 16), np.float32)
    for p in range(128):
        ss[p, p % 16] = 1.0
    c["c_ss"] = ss
    eo = np.zeros((8, 8, 128), np.float32)
    for e in range(8):
        eo[e, e, :] = 1.0
    c["c_eones"] = eo
    c["c_id"] = np.eye(128, dtype=np.float32)
    m = np.zeros((128, 128), np.float32)
    for j in range(128):
        m[j, j:] = 1.0
    c["c_mask"] = m
    fb = np.zeros((128, 6), np.float32)
    fb[:, 0] = 1e-5
    fb[:, 1] = 1e-6
    fb[:, 2] = 1.0
    fb[0:64, 4] = 1.0
    fb[64:128, 5] = 1.0
    c["c_fb"] = fb
    return c


def _stage_weights(inp):
    st = {}
    A = np.ascontiguousarray

    def w1tile(W1):
        return A(W1.reshape(8, 8, 128, 32, 128).transpose(0, 3, 2, 1, 4))

    def w2tile(W2):
        return A(W2.reshape(8, 4, 8, 128, 8, 128).transpose(0, 1, 4, 3, 2, 5))

    blk = np.zeros((128, 3, 128), np.float32)
    for i, nm in enumerate(("L0_attn_Wq", "L0_attn_Wk", "L0_attn_Wv")):
        Wt = np.asarray(inp[nm], np.float32).T
        blk[0:64, i, 0:64] = Wt
        blk[64:128, i, 64:128] = Wt
    st["w_l0blk"] = blk
    qbd = np.zeros((128, 128), np.float32)
    kbd = np.zeros((128, 128), np.float32)
    for bb in range(2):
        sl = slice(64 * bb, 64 * bb + 64)
        qbd[sl, sl] = np.asarray(inp["L0_attn_Wq"], np.float32)
        kbd[sl, sl] = np.asarray(inp["L0_attn_Wk"], np.float32)
    st["w_l0qbd"] = qbd
    st["w_l0kbd"] = kbd
    st["w_l0wr"] = A(np.asarray(inp["L0_ffn_Wr"], np.float32).reshape(8, 128, 8)
                     .transpose(1, 0, 2))
    st["w_l0w1"] = w1tile(np.asarray(inp["L0_ffn_W1"], np.float32))
    st["w_l0w2"] = w2tile(np.asarray(inp["L0_ffn_W2"], np.float32))
    st["p_br0"] = np.asarray(inp["L0_ffn_br"], np.float32).reshape(8, 1)
    for nm, key in (("Wq", "wq"), ("Wk", "wk"), ("Wv", "wv"), ("Wo", "wo")):
        Wt = np.asarray(inp["L1_attn_" + nm], np.float32).T
        st["w_l1" + key] = A(Wt.reshape(8, 128, 1024).transpose(1, 0, 2))
    st["w_l1wf"] = A(np.asarray(inp["L1_attn_Wf"], np.float32).T
                     .reshape(8, 128, 16).transpose(1, 0, 2))
    st["p_delta"] = np.asarray(inp["L1_attn_delta"], np.float32).reshape(16, 1)
    st["p_nw"] = A(np.asarray(inp["L1_attn_norm_weight"], np.float32).reshape(8, 128).T)
    st["w_l1wr"] = A(np.asarray(inp["L1_ffn_Wr"], np.float32).reshape(8, 128, 8)
                     .transpose(1, 0, 2))
    st["w_l1w1"] = w1tile(np.asarray(inp["L1_ffn_W1"], np.float32))
    st["w_l1w2"] = w2tile(np.asarray(inp["L1_ffn_W2"], np.float32))
    st["p_br1"] = np.asarray(inp["L1_ffn_br"], np.float32).reshape(8, 1)
    st["p_b1"] = A(np.asarray(inp["L1_ffn_b1"], np.float32).reshape(8, 32, 128)
                   .transpose(2, 0, 1))
    st["p_b2"] = A(np.asarray(inp["L1_ffn_b2"], np.float32).reshape(8, 8, 128)
                   .transpose(2, 0, 1))
    st["p_lng"] = A(np.stack(
        [np.asarray(inp["L1_ffn_ln_g"], np.float32).reshape(8, 128).T,
         np.asarray(inp["L1_ffn_ln_b"], np.float32).reshape(8, 128).T], axis=1))
    return st


_PROG_CACHE = {}


def _get_prog(blocks, dump_names=()):
    key = (tuple(blocks), tuple(dump_names), L1A_TRUNC, LN_TRUNC)
    if key not in _PROG_CACHE:
        P, dumps = build_program(blocks, dump_names)
        nc = P.build()
        _PROG_CACHE[key] = (nc, dumps)
    return _PROG_CACHE[key]


def run_blocks(inp, blocks, xcur=None, dump_names=(), s_all=None):
    nc, dumps = _get_prog(tuple(blocks), tuple(dump_names))
    st = _stage_weights(inp)
    st.update(_consts())
    x = np.asarray(xcur if xcur is not None else inp["L0_attn_x"], np.float32)
    xall_fm = (np.ascontiguousarray(x.transpose(0, 2, 1))
               if any(b == "l0a_f" for b in blocks) else None)
    in_maps = []
    for ci in range(8):
        m = dict(st)
        m["xin"] = np.ascontiguousarray(x[ci].T)
        if "l0a" in blocks or "l0a_scores" in blocks or "l0a_f" in blocks:
            sel = np.zeros((128, 16), np.float32)
            for h in range(16):
                sel[16 * ci + h, h] = 1.0
            m["c_mysel"] = sel
        if "l0a_f" in blocks:
            mis = np.zeros((16, 8), np.float32)
            mis[:, ci] = 1.0
            m["c_misel"] = mis
            m["xall"] = xall_fm
        if s_all is not None:
            m["s_all"] = s_all
        in_maps.append(m)
    res = run_bass_kernel_spmd(nc, in_maps, core_ids=list(range(8)))
    keys = set()
    for r in res.results:
        keys |= set(r.keys())
    if "xout" in keys:
        out = np.stack([np.asarray(res.results[ci]["xout"]).T for ci in range(8)])
        out = out.astype(np.float32)
    else:
        out = None
    extra = {nm: [np.asarray(res.results[ci]["dump_" + nm]) for ci in range(8)]
             for nm in dumps}
    if "s_my" in keys:
        extra["s_my"] = [np.asarray(res.results[ci]["s_my"]) for ci in range(8)]
    return out, extra


def kernel(**inp):
    out, _ = run_blocks(inp, ("l0a_f", "l0f", "l1a", "l1f"))
    return out


# version 07 (final)
# speedup vs baseline: 1.5326x; 1.0071x over previous
import numpy as np
from contextlib import ExitStack

import concourse.bass as bass
import concourse.mybir as mybir
from concourse.bass_utils import run_bass_kernel_spmd

F32 = mybir.dt.float32
BF16 = mybir.dt.bfloat16
AF = mybir.ActivationFunctionType
ALU = mybir.AluOpType
AX = mybir.AxisListType

T = 2048
KD = 8
NT = 512
NTT = 4
NE = 8
NFC = 32
NFQ = 4
NW = 4
CPW = 4
L1A_TRUNC = 99
LN_TRUNC = 99


class Sem:
    __slots__ = ("h", "v")

    def __init__(self, h):
        self.h = h
        self.v = 0


class PB:
    ENG = {"pe": "tensor", "dve": "vector", "act": "scalar", "pool": "gpsimd", "sp": "sync"}

    def __init__(self):
        self.nc = bass.Bass("TRN2", target_bir_lowering=False)
        self.stk = ExitStack()
        self.q = {k: [] for k in self.ENG}
        self.sem = {}
        for s in ("pe", "dve", "act", "pool", "sp", "wde", "xde", "cc", "ode",
                  "w1a", "w1b", "w2a", "w2b", "wsa", "wsb", "sla", "slb", "wrs"):
            self.sem[s] = Sem(self.stk.enter_context(self.nc.semaphore(s)))

    def sb(self, name, shape, dt):
        return self.stk.enter_context(self.nc.sbuf_tensor(name, list(shape), dt))

    def ps(self, name, shape, dt=F32):
        return self.stk.enter_context(self.nc.psum_tensor(name, list(shape), dt))

    def wait(self, eng, sem, v=None):
        vv = sem.v if v is None else v
        if vv <= 0:
            return
        self.q[eng].append(lambda E, h=sem.h, vv=vv: E.wait_ge(h, vv))

    SELF_SER = ("dve", "act")

    def do(self, eng, fn, sem=None, k=1):
        if eng in self.SELF_SER:
            own = self.sem[eng]
            if own.v > 0:
                self.q[eng].append(lambda E, h=own.h, vv=own.v: E.wait_ge(h, vv))
            if sem is None:
                sem = own
        if sem is None:
            self.q[eng].append(fn)
        else:
            sem.v += k
            self.q[eng].append(lambda E, fn=fn, h=sem.h, k=k: fn(E).then_inc(h, k))

    def build(self):
        with self.nc.Block() as blk:
            for short, attr in self.ENG.items():
                getattr(blk, attr)(lambda E, lst=self.q[short]: [f(E) for f in lst])
        return self.nc


def _fs(ap):
    return ap[tuple([slice(None)] * len(ap.shape))]


def build_program(blocks=("l0a", "l0f", "l1a", "l1f"), dump_names=()):
    P = PB()
    S = P.sem
    pe, dve, act, pool, sp = S["pe"], S["dve"], S["act"], S["pool"], S["sp"]
    wde, xde, ode, ccs = S["wde"], S["xde"], S["ode"], S["cc"]
    w1s = [S["w1a"], S["w1b"]]
    w2s = [S["w2a"], S["w2b"]]
    wss = [S["wsa"], S["wsb"]]
    sls = [S["sla"], S["slb"]]
    need = set(blocks)
    dumps_made = []

    xin = P.nc.dram_tensor("xin", [1024, T], F32, kind="ExternalInput").ap()
    xout = P.nc.dram_tensor("xout", [1024, T], F32, kind="ExternalOutput").ap()
    dr = {}

    def din(nm, shape):
        dr[nm] = P.nc.dram_tensor(nm, list(shape), F32, kind="ExternalInput").ap()

    for nm, shp in [("c_ones", [128, 16]), ("c_onesrow", [1, 128]), ("c_sel", [16, 8, 128]),
                    ("c_headsel", [128, 8, 16]), ("c_ss", [128, 16]), ("c_eones", [8, 8, 128]),
                    ("c_id", [128, 128]), ("c_mask", [128, 128]),
                    ("c_fb", [128, 6])]:
        din(nm, shp)
    l0a_mode = ("scores" if "l0a_scores" in need else
                ("fused" if "l0a_f" in need else ("ext" if "l0a" in need else None)))
    if l0a_mode is not None:
        din("w_l0blk", [128, 3, 128])
        din("c_mysel", [128, 16])
        if l0a_mode == "scores":
            s_my = P.nc.dram_tensor("s_my", [16, T], F32, kind="ExternalOutput").ap()
        elif l0a_mode == "ext":
            din("s_all", [128, T])
            s_all = dr["s_all"]
        else:
            din("xall", [NE, 1024, T])
            din("w_l0qbd", [128, 128])
            din("w_l0kbd", [128, 128])
            din("c_misel", [16, 8])
    for L in (0, 1):
        if ("l0f" if L == 0 else "l1f") in need:
            din(f"w_l{L}wr", [128, 8, 8])
            din(f"w_l{L}w1", [NE, NFC, 128, 8, 128])
            din(f"w_l{L}w2", [NE, NFQ, 8, 128, 8, 128])
            din(f"p_br{L}", [8, 1])
    if "l1f" in need:
        din("p_b1", [128, 8, 32])
        din("p_b2", [128, 8, 8])
        din("p_lng", [128, 2, 8])
    if "l1a" in need:
        import os as _os0
        if _os0.environ.get("SKIP_WQKVO") != "1":
            for w in ("wq", "wk", "wv", "wo"):
                din("w_l1" + w, [128, 8, 1024])
        din("w_l1wf", [128, 8, 16])
        din("p_delta", [16, 1])
        din("p_nw", [128, 8])

    xsb = P.sb("xsb", [128, KD, T], BF16)
    xn = P.sb("xn", [128, KD, T], BF16)
    U = P.sb("U", [128, 40960], BF16)
    W = P.sb("W", [128, 16384], BF16)
    SCRF = P.sb("SCRF", [128, 4096], F32)
    rows_sm = P.sb("rows_sm", [128, 2, NT], F32)
    grow_sm = P.sb("grow_sm", [128, NT], BF16)
    sqt = P.sb("sqt", [128, 2, NT], BF16)
    cones = P.sb("cones", [128, 16], BF16)
    conesrow = P.sb("conesrow", [1, 128], BF16)
    conesrow_f = P.sb("conesrow_f", [1, 128], F32)
    selb = P.sb("selb", [16, 8, 128], BF16)
    headsb = P.sb("headsb", [128, 8, 16], BF16)
    ssb = P.sb("ssb", [128, 16], BF16)
    eones = P.sb("eones", [8, 8, 128], BF16)
    idb = P.sb("idb", [128, 128], BF16)
    idf = P.sb("idf", [128, 128], F32)
    maskt = P.sb("maskt", [128, 128], BF16)
    fbc = P.sb("fbc", [128, 6], F32)
    wrsb = P.sb("wrsb", [128, 8, 8], F32)
    brsb = P.sb("brsb", [8, 2], F32)
    lngb = P.sb("lngb", [128, 2, 8], F32)
    nwsb = P.sb("nwsb", [128, 8], F32)
    dlt = P.sb("dlt", [16, 1], F32)
    b1sb = P.sb("b1sb", [128, 8, 32], F32)
    b2sb = P.sb("b2sb", [128, 8, 8], F32)
    ohtm = P.sb("ohtm", [128, 8], BF16)
    gcolb = P.sb("gcolb", [128, 16], BF16)
    wfTsb = P.sb("wfTsb", [128, 8, 16], BF16)
    miselb = P.sb("miselb", [16, 8], F32)
    wcolf = P.sb("wcolf", [128, 1], F32)
    m11f = P.sb("m11f", [128, 1], F32)

    PH = P.ps("PH", [128, 4, 512])
    PY = P.ps("PY", [128, 2, 512])
    PS1 = P.ps("PS1", [128, 512])
    PSM = P.ps("PSM", [128, 1024], BF16)
    pt_oh = PSM[0:8, 0:128]
    pt_lb = PSM[:, 128:144]
    ptrow = PSM[0:1, 144:272]

    def view(base, off, shape):
        n = int(np.prod(shape))
        ap = base[:, off:off + n]
        if len(shape) > 1:
            names = " ".join(f"a{i}" for i in range(len(shape)))
            ap = ap.rearrange(f"p ({names}) -> p {names}",
                              **{f"a{i}": s for i, s in enumerate(shape)})
        return ap

    hq_v = view(U, 0, (8, T))
    yac_v = view(U, 8 * T, (8, T))
    gateb_v = view(U, 16 * T, (T,))
    gmask_v = view(U, 17 * T, (T,))
    gmask2_v = view(U, 19 * T, (T,))
    ohfm_v = view(U, 18 * T, (T,))
    ohfm_ct = view(U, 18 * T, (16, 128))
    l0_q = view(U, 0, (NT,))
    l0_qk = view(U, NT, (NT,))
    l0_vs = view(U, 2 * NT, (NT,))
    l0_e = view(U, 3 * NT, (NT,))
    l0_sall = view(U, 4 * NT, (2, NT))
    l0_smine = view(U, 6 * NT, (T,))
    l0_probs = view(U, 6 * NT + T, (T,))
    l0_emine = view(U, 6 * NT + 2 * T, (2, NT))
    l0_inv = view(U, 6 * NT + 3 * T, (2, NT))
    l0_wblk = view(U, 6 * NT + 4 * T, (3, 128))
    f_xb = view(U, 16384, (2, 8, NT))
    f_z = view(U, 24576, (NT,))
    f_xz = view(U, 24576 + NT, (NT,))
    f_sq = view(U, 24576 + 2 * NT, (NT,))
    f_et = view(U, 24576 + 3 * NT, (NT,))
    f_em = view(U, 26624, (T,))
    f_mblk = view(U, 28672, (128,))
    f_mblkT = view(U, 28672 + 128, (128,))
    f_qbd = view(U, 28672 + 256, (128,))
    f_kbd = view(U, 28672 + 384, (128,))
    f_wsel = view(U, 28672 + 512, (8, 16))
    f_m2x = view(U, 28672 + 640, (16,))
    qfm_v = view(U, 0, (8, NT))
    kfm_v = view(U, 4096, (8, NT))
    ktm_v = view(U, 8192, (CPW, 1024))
    vtm_v = view(U, 12288, (CPW, 1024))
    obf_v = view(U, 16384, (8, NT))
    a2_v = view(U, 20480, (16, 128))
    kds_v = view(U, 22528, (4, 128))
    elbbc_v = view(U, 23040, (8, 128))
    lbbf_v = view(U, 24064, (T,))
    elbbf_v = view(U, 24064 + T, (T,))
    sbf_v = view(U, 24064 + 2 * T, (16, 128))
    etot_bf = view(U, 24064 + 2 * T + 2048, (16,))
    sfx_v = view(U, 24064 + 2 * T + 2048 + 16, (T,))
    kmsk_v = view(U, 24064 + 3 * T + 2048 + 16, (2, 8, NT))
    qs_v = view(U, 24064 + 2 * T + 2048 + 16, (4, 128))
    w1c_v = view(W, 0, (2, 8, 128))
    w2c_v = view(W, 2048, (2, 8, 128))
    wstream_v = view(W, 0, (2, 8, 1024))
    stg32 = view(SCRF, 0, (2, NT))
    st16 = view(SCRF, 1024, (2, NT))
    lftt_v = view(SCRF, 2048, (2, NT))
    cols_v = view(SCRF, 3072, (8,))
    ex_v = view(SCRF, 3072 + 8, (8,))
    secol_v = view(SCRF, 3072 + 32, (16,))
    vsump_v = view(SCRF, 3072 + 48, (KD, NTT))
    vsum8_v = view(SCRF, 3072 + 80, (KD,))
    vsc2_v = view(SCRF, 3072 + 88, (KD,))
    ginv_v = view(SCRF, 3072 + 96, (16,))
    enltm_v = view(SCRF, 2048, (16, 16))
    ktab_v = view(SCRF, 2048 + 256, (16, 16))
    Sf_v = view(SCRF, 2048 + 512, (8, 64))
    etotbc_v = view(SCRF, 2048 + 1024, (8, 16))
    totf_v = view(SCRF, 2048 + 1024 + 128, (16,))

    TTS = [slice(tt * NT, (tt + 1) * NT) for tt in range(NTT)]
    TTP = [slice(tp * 2 * NT, (tp + 1) * 2 * NT) for tp in range(2)]

    for nm, dst in [("c_ones", cones), ("c_onesrow", conesrow), ("c_sel", selb),
                    ("c_headsel", headsb), ("c_ss", ssb), ("c_eones", eones),
                    ("c_id", idb), ("c_mask", maskt)]:
        P.do("pool", lambda E, d=dst, s=dr[nm]: E.dma_start(out=_fs(d), in_=_fs(s)), wde, 16)
    P.do("sp", lambda E: E.dma_start(out=_fs(idf), in_=_fs(dr["c_id"])), xde, 16)
    P.do("sp", lambda E: E.dma_start(out=_fs(fbc), in_=_fs(dr["c_fb"])), xde, 16)
    P.do("sp", lambda E: E.dma_start(out=_fs(conesrow_f), in_=_fs(dr["c_onesrow"])), xde, 16)
    P.do("pool", lambda E: E.dma_start(
        out=xsb[:, :, :], in_=xin.rearrange("(kd p) t -> p kd t", p=128)), wde, 16)
    if l0a_mode is not None:
        P.do("pool", lambda E: E.dma_start(out=l0_wblk, in_=_fs(dr["w_l0blk"])), wde, 16)
        myselb = P.sb("myselb", [128, 16], BF16)
        P.do("pool", lambda E: E.dma_start(out=_fs(myselb), in_=_fs(dr["c_mysel"])), wde, 16)
    if l0a_mode == "fused":
        P.do("pool", lambda E: E.dma_start(out=f_qbd, in_=_fs(dr["w_l0qbd"])), wde, 16)
        P.do("pool", lambda E: E.dma_start(out=f_kbd, in_=_fs(dr["w_l0kbd"])), wde, 16)
        P.do("sp", lambda E: E.dma_start(out=_fs(miselb), in_=_fs(dr["c_misel"])), xde, 16)
    if "l1a" in need:
        import os as _os
        if _os.environ.get("SKIP_WF") != "1":
            P.do("pool", lambda E: E.dma_start(out=_fs(wfTsb), in_=_fs(dr["w_l1wf"])), wde, 16)
        if _os.environ.get("SKIP_DLT") != "1":
            P.do("sp", lambda E: E.dma_start(out=_fs(dlt), in_=_fs(dr["p_delta"])), xde, 16)
        if _os.environ.get("SKIP_NW") != "1":
            P.do("sp", lambda E: E.dma_start(out=_fs(nwsb), in_=_fs(dr["p_nw"])), xde, 16)
    if "l1f" in need:
        P.do("sp", lambda E: E.dma_start(out=_fs(b1sb), in_=_fs(dr["p_b1"])), xde, 16)
        P.do("sp", lambda E: E.dma_start(out=_fs(b2sb), in_=_fs(dr["p_b2"])), xde, 16)
        P.do("sp", lambda E: E.dma_start(out=_fs(lngb), in_=_fs(dr["p_lng"])), xde, 16)
    if "l0f" in need:
        P.do("sp", lambda E: E.dma_start(out=brsb[:, 0:1], in_=_fs(dr["p_br0"])), xde, 16)
    if "l1f" in need:
        P.do("sp", lambda E: E.dma_start(out=brsb[:, 1:2], in_=_fs(dr["p_br1"])), xde, 16)
    for e in ("pe", "dve", "act"):
        P.wait(e, wde)
        P.wait(e, xde)
    P.wait("sp", wde)
    P.wait("pool", xde)

    def dump(name, ap, shape, dt=F32):
        if name not in dump_names:
            return
        d = P.nc.dram_tensor("dump_" + name, list(shape), dt, kind="ExternalOutput").ap()
        for e in ("pe", "dve", "act"):
            P.do(e, lambda E: E.nop(), S[e], 1)
            P.wait("sp", S[e])
        P.do("sp", lambda E, d=d, ap=ap: E.dma_start(out=_fs(d), in_=ap), xde, 16)
        P.wait("sp", xde)
        dumps_made.append(name)

    ph_use = [{"dve": 0, "act": 0} for _ in range(4)]
    py_use = [{"dve": 0, "act": 0} for _ in range(2)]
    ps1_use = {"dve": 0, "act": 0}

    def pe_claim(d):
        P.wait("pe", dve, d["dve"])
        P.wait("pe", act, d["act"])

    pt_oh_use = [0]
    ptrow_use = [0]
    rows_war = [0]

    def emit_ln(src, out_bf, router=None, affine=False, eps=1e-5):
        P.wait("act", dve)
        P.wait("act", pe)
        P.wait("pe", dve)
        stg_pe_use = [0, 0]
        sq_pe_use = [0, 0]
        rows_pe_use = [rows_war[0]]
        for tt in range(NTT):
            if LN_TRUNC < 1 + tt:
                break
            ts = TTS[tt]
            for kd in range(KD):
                par = kd % 2
                P.wait("act", pe, sq_pe_use[par])
                P.do("act", lambda E, kd=kd, ts=ts, par=par: E.activation(
                    sqt[:, par, :], src[:, kd, ts], AF.Square), act, 1)
                sq_done = act.v
                if kd == 0:
                    pe_claim(ps1_use)
                    pe_claim(py_use[0])
                P.do("pe", lambda E, kd=kd, ts=ts: E.matmul(
                    PS1[0:16, :], cones[:, :], src[:, kd, ts],
                    start=(kd == 0), stop=(kd == KD - 1)))
                P.wait("pe", act, sq_done)
                P.do("pe", lambda E, kd=kd, par=par: E.matmul(
                    PY[0:16, 0, :], cones[:, :], sqt[:, par, :],
                    start=(kd == 0), stop=(kd == KD - 1)), pe, 1)
                sq_pe_use[par] = pe.v
            stats_done = pe.v
            if LN_TRUNC < 10 + tt:
                continue
            P.wait("act", pe, stats_done)
            P.do("act", lambda E: E.activation(st16[0:16, 0, :], PS1[0:16, :], AF.Copy,
                                               scale=1.0 / 1024.0), act, 1)
            ps1_use["act"] = act.v
            P.do("act", lambda E: E.activation(st16[0:16, 1, :], PY[0:16, 0, :], AF.Copy,
                                               scale=1.0 / 1024.0), act, 1)
            py_use[0]["act"] = act.v
            pe_claim_needed = py_use[1]
            P.wait("act", dve, pe_claim_needed["dve"])
            P.do("act", lambda E: E.activation(PY[0:16, 1, :], st16[0:16, 0, :],
                                               AF.Square), act, 1)
            m2_done = act.v
            P.wait("dve", act, m2_done)
            P.do("dve", lambda E: E.tensor_tensor(st16[0:16, 1, :], st16[0:16, 1, :],
                                                  PY[0:16, 1, :], ALU.subtract), dve, 1)
            py_use[1]["dve"] = dve.v
            var_done = dve.v
            P.wait("act", dve, var_done)
            P.do("act", lambda E: E.activation(st16[0:16, 1, :], st16[0:16, 1, :],
                                                AF.Ln, bias=fbc[0:16, 0:1]), act, 1)
            P.wait("act", pe, rows_pe_use[0])
            P.do("act", lambda E: E.activation(rows_sm[0:1, 1, :], st16[0:1, 1, :],
                                               AF.Exp, scale=-0.5), act, 1)
            P.do("act", lambda E: E.activation(rows_sm[0:1, 0, :], st16[0:1, 0, :],
                                               AF.Copy), act, 1)
            rows_done = act.v
            if LN_TRUNC < 20 + tt:
                continue
            P.wait("pe", act, rows_done)
            pe_claim(py_use[0])
            pe_claim(py_use[1])
            P.do("pe", lambda E: E.matmul(PY[:, 0, :], conesrow_f[:, :],
                                          rows_sm[0:1, 0, :], start=True, stop=True))
            P.do("pe", lambda E: E.matmul(PY[:, 1, :], conesrow_f[:, :],
                                          rows_sm[0:1, 1, :], start=True, stop=True), pe, 1)
            bc_done = pe.v
            rows_pe_use[0] = pe.v
            if LN_TRUNC < 30 + tt:
                continue
            P.wait("dve", pe, bc_done)
            for kd in range(KD):
                spar = kd % 2
                if router is not None:
                    P.wait("dve", pe, stg_pe_use[spar])
                P.do("dve", lambda E, kd=kd, ts=ts, spar=spar: E.scalar_tensor_tensor(
                    stg32[:, spar, :], PY[:, 0, :], -1.0, src[:, kd, ts],
                    ALU.mult, ALU.add), dve, 1)
                if router is not None:
                    P.do("dve", lambda E, spar=spar: E.tensor_tensor(
                        stg32[:, spar, :], stg32[:, spar, :], PY[:, 1, :],
                        ALU.mult), dve, 1)
                    P.do("dve", lambda E, kd=kd, ts=ts, spar=spar: E.tensor_copy(
                        out_bf[:, kd, ts], stg32[:, spar, :]), dve, 1)
                    st_done = dve.v
                    P.wait("pe", dve, st_done)
                    router(kd, tt, stg32[:, spar, :])
                    stg_pe_use[spar] = pe.v
                elif affine:
                    P.do("dve", lambda E, spar=spar: E.tensor_tensor(
                        stg32[:, spar, :], stg32[:, spar, :], PY[:, 1, :],
                        ALU.mult), dve, 1)
                    P.do("dve", lambda E, kd=kd, ts=ts, spar=spar: E.tensor_scalar(
                        out_bf[:, kd, ts], stg32[:, spar, :],
                        lngb[:, 0, kd:kd + 1], lngb[:, 1, kd:kd + 1],
                        ALU.mult, ALU.add), dve, 1)
                else:
                    P.do("dve", lambda E, kd=kd, ts=ts, spar=spar: E.tensor_tensor(
                        out_bf[:, kd, ts], stg32[:, spar, :], PY[:, 1, :],
                        ALU.mult), dve, 1)
            py_use[0]["dve"] = py_use[1]["dve"] = dve.v
        rows_war[0] = rows_pe_use[0]
        P.wait("pe", dve)

    def emit_l0a():
        emit_ln(xsb, xn)
        if l0a_mode == "scores":
            smine_f = view(SCRF, 2048, (T,))
            for tt in range(NTT):
                ts = TTS[tt]
                for kd in range(KD):
                    for b in range(2):
                        pe_claim(ph_use[b])
                    P.do("pe", lambda E, kd=kd, ts=ts: E.matmul(
                        PH[:, 0, :], l0_wblk[:, 0, :], xn[:, kd, ts],
                        start=True, stop=True))
                    P.do("pe", lambda E, kd=kd, ts=ts: E.matmul(
                        PH[:, 1, :], l0_wblk[:, 1, :], xn[:, kd, ts],
                        start=True, stop=True), pe, 1)
                    qk_mm = pe.v
                    P.wait("dve", pe, qk_mm)
                    P.do("dve", lambda E: E.tensor_copy(l0_q, PH[:, 0, :]), dve, 1)
                    P.do("dve", lambda E: E.tensor_tensor(l0_qk, l0_q, PH[:, 1, :],
                                                          ALU.mult), dve, 1)
                    qk_done = dve.v
                    ph_use[0]["dve"] = ph_use[1]["dve"] = dve.v
                    P.wait("pe", dve, qk_done)
                    if kd == 0:
                        pe_claim(ps1_use)
                    P.do("pe", lambda E, kd=kd: E.matmul(
                        PS1[0:16, :], headsb[:, kd, :], l0_qk,
                        start=(kd == 0), stop=(kd == KD - 1)),
                        pe if kd == KD - 1 else None, 1)
                s_done = pe.v
                P.wait("dve", pe, s_done)
                P.do("dve", lambda E, ts=ts: E.tensor_copy(smine_f[0:16, ts],
                                                           PS1[0:16, :]), dve, 1)
                ps1_use["dve"] = dve.v
            P.wait("sp", dve)
            P.do("sp", lambda E, smine_f=smine_f: E.dma_start(
                out=_fs(s_my), in_=smine_f[0:16, :]), xde, 16)
            P.wait("sp", xde)
            return
        for tt in range(NTT):
            ts = TTS[tt]
            for kd in range(KD):
                pe_claim(ph_use[2])
                P.do("pe", lambda E, kd=kd, ts=ts: E.matmul(
                    PH[:, 2, :], l0_wblk[:, 2, :], xn[:, kd, ts],
                    start=True, stop=True), pe, 1)
                v_mm = pe.v
                P.wait("act", pe, v_mm)
                P.do("act", lambda E, kd=kd, tt=tt: E.activation(
                    l0_vs, PH[:, 2, :], AF.Copy,
                    accum_out=vsump_v[:, kd, tt:tt + 1]), act, 1)
                ph_use[2]["act"] = act.v
        P.wait("dve", act)
        P.do("dve", lambda E: E.tensor_tensor(vsum8_v, vsump_v[:, :, 0],
                                              vsump_v[:, :, 1], ALU.add), dve, 1)
        P.do("dve", lambda E: E.tensor_tensor(vsc2_v, vsump_v[:, :, 2],
                                              vsump_v[:, :, 3], ALU.add), dve, 1)
        P.do("dve", lambda E: E.tensor_tensor(vsum8_v, vsum8_v, vsc2_v, ALU.add), dve, 1)
        sall_act_use = [0, 0]
        l0e_pe_use = [0]
        inv_dve_use = [0]
        for tt in range(NTT):
            ts = TTS[tt]
            par = tt % 2
            P.wait("pool", act, sall_act_use[par])
            P.do("pool", lambda E, ts=ts, par=par: E.dma_start(
                out=l0_sall[:, par, :], in_=s_all[:, ts]), sls[par], 16)
            P.wait("act", sls[par])
            P.wait("act", pe, l0e_pe_use[0])
            P.do("act", lambda E, par=par: E.activation(
                l0_e, l0_sall[:, par, :], AF.Exp, scale=1.0 / 32.0), act, 1)
            e_done = act.v
            sall_act_use[par] = act.v
            P.wait("pe", act, e_done)
            pe_claim(ps1_use)
            pe_claim(py_use[0])
            P.do("pe", lambda E: E.matmul(PS1[0:16, :], ssb[:, :], l0_e,
                                          start=True, stop=True))
            P.do("pe", lambda E: E.matmul(PY[0:16, 0, :], myselb[:, :], l0_e,
                                          start=True, stop=True), pe, 1)
            se_done = pe.v
            l0e_pe_use[0] = pe.v
            P.wait("act", pe, se_done)
            P.do("act", lambda E: E.activation(st16[0:16, 0, :], PS1[0:16, :],
                                               AF.Ln), act, 1)
            ps1_use["act"] = act.v
            P.wait("act", dve, inv_dve_use[0])
            P.do("act", lambda E: E.activation(st16[0:16, 1, :], st16[0:16, 0, :],
                                               AF.Exp, scale=-1.0), act, 1)
            inv_done = act.v
            P.wait("dve", act, inv_done)
            P.do("dve", lambda E, ts=ts: E.tensor_tensor(
                l0_probs[0:16, ts], PY[0:16, 0, :], st16[0:16, 1, :], ALU.mult), dve, 1)
            py_use[0]["dve"] = dve.v
            inv_dve_use[0] = dve.v
        probs_done = dve.v
        P.wait("pe", dve, probs_done)
        for tt in range(NTT):
            ts = TTS[tt]
            for kd in range(KD):
                bk = kd % 4
                pe_claim(ph_use[bk])
                P.do("pe", lambda E, kd=kd, ts=ts, bk=bk: E.matmul(
                    PH[:, bk, :], selb[:, kd, :], l0_probs[0:16, ts],
                    start=True, stop=True), pe, 1)
                mm_done = pe.v
                P.wait("dve", pe, mm_done)
                P.do("dve", lambda E, kd=kd, ts=ts, bk=bk: E.scalar_tensor_tensor(
                    xsb[:, kd, ts], PH[:, bk, :], vsum8_v[:, kd:kd + 1], xsb[:, kd, ts],
                    ALU.mult, ALU.add), dve, 1)
                ph_use[bk]["dve"] = dve.v

    def emit_l0a_fused():
        emit_ln(xsb, xn)
        seacc = view(SCRF, 2048, (T,))
        P.wait("pe", wde)
        P.wait("pe", xde)
        pe_claim(ph_use[0])
        pe_claim(ph_use[1])
        P.do("pe", lambda E: E.matmul(PH[:, 0, 0:128], f_qbd, f_kbd,
                                      start=True, stop=True))
        P.do("pe", lambda E: E.matmul(PH[:, 1, 0:128], f_kbd, f_qbd,
                                      start=True, stop=True), pe, 1)
        mb = pe.v
        P.wait("dve", pe, mb)
        P.do("dve", lambda E: E.tensor_copy(f_mblk, PH[:, 0, 0:128]), dve, 1)
        P.do("dve", lambda E: E.tensor_copy(f_mblkT, PH[:, 1, 0:128]), dve, 1)
        ph_use[0]["dve"] = ph_use[1]["dve"] = dve.v
        mev = dve.v
        P.wait("pe", dve, mev)
        pe_claim(py_use[0])
        pe_claim(py_use[1])
        P.do("pe", lambda E: E.matmul(PY[:, 0, 0:1], f_mblk, cones[:, 0:1],
                                      start=True, stop=True))
        P.do("pe", lambda E: E.matmul(PY[:, 1, 0:1], f_mblkT, cones[:, 0:1],
                                      start=True, stop=True), pe, 1)
        wv_ = pe.v
        P.wait("dve", pe, wv_)
        P.do("dve", lambda E: E.tensor_copy(wcolf[:, :], PY[:, 0, 0:1]), dve, 1)
        P.do("dve", lambda E: E.tensor_tensor(wcolf[:, :], wcolf[:, :],
                                              PY[:, 1, 0:1], ALU.add), dve, 1)
        py_use[0]["dve"] = py_use[1]["dve"] = dve.v
        P.do("dve", lambda E: E.tensor_copy(f_m2x[:, 1:2], wcolf[:, :]), dve, 1)
        P.do("dve", lambda E: E.tensor_copy(f_m2x[:, 2:3], fbc[:, 4:5]), dve, 1)
        for kd in range(KD):
            P.do("dve", lambda E, kd=kd: E.tensor_scalar(
                f_wsel[:, kd, :], headsb[:, kd, :], wcolf[:, 0:1], None,
                ALU.mult), dve, 1)
        wsel_done = dve.v
        P.wait("pe", dve, wsel_done)
        pe_claim(ps1_use)
        P.do("pe", lambda E: E.matmul(PS1[0:1, 0:1], f_m2x[:, 1:2], f_m2x[:, 2:3],
                                      start=True, stop=True), pe, 1)
        m2s = pe.v
        P.wait("dve", pe, m2s)
        P.do("dve", lambda E: E.tensor_copy(f_m2x[0:1, 0:1], PS1[0:1, 0:1]), dve, 1)
        ps1_use["dve"] = dve.v
        m2e = dve.v
        P.wait("pe", dve, m2e)
        pe_claim(py_use[0])
        P.do("pe", lambda E: E.matmul(PY[:, 0, 0:1], conesrow[:, :],
                                      f_m2x[0:1, 0:1], start=True, stop=True), pe, 1)
        m2b = pe.v
        P.wait("dve", pe, m2b)
        P.do("dve", lambda E: E.tensor_scalar(m11f[:, :], PY[:, 0, 0:1], 0.5, None,
                                              ALU.mult), dve, 1)
        py_use[0]["dve"] = dve.v
        P.do("dve", lambda E: E.memset(seacc[0:16, :], 0.0), dve, 1)
        P.do("dve", lambda E: E.memset(f_em[0:16, :], 0.0), dve, 1)
        xb_pool_use = [0, 0]
        xb_pe_use = [0, 0]
        for ci in range(NE):
            for tt in range(NTT):
                ts = TTS[tt]
                par = (ci * NTT + tt) % 2
                P.wait("pool", dve, xb_pool_use[par])
                P.wait("pool", pe, xb_pe_use[par])
                P.do("pool", lambda E, ci=ci, ts=ts, par=par: E.dma_start(
                    out=f_xb[:, par, :, :],
                    in_=dr["xall"][ci].rearrange("(kd p) t -> p kd t", p=128)[:, :, ts]),
                    sls[par], 16)
                P.wait("pe", sls[par])
                last_stat = [0]
                for kd in range(KD):
                    zb = kd % 2
                    pe_claim(ph_use[zb])
                    if kd == 0:
                        pe_claim(ps1_use)
                        pe_claim(py_use[0])
                        pe_claim(py_use[1])
                        pe_claim(ph_use[2])
                    P.do("pe", lambda E, kd=kd, zb=zb, par=par: E.matmul(
                        PH[:, zb, :], f_mblkT, f_xb[:, par, kd, :],
                        start=True, stop=True), pe, 1)
                    zmm = pe.v
                    P.wait("dve", pe, zmm)
                    P.wait("dve", pe, last_stat[0])
                    P.do("dve", lambda E, kd=kd, zb=zb, par=par: E.tensor_tensor(
                        f_xz, f_xb[:, par, kd, :], PH[:, zb, :], ALU.mult), dve, 1)
                    ph_use[zb]["dve"] = dve.v
                    P.do("dve", lambda E, kd=kd, par=par: E.tensor_tensor(
                        f_sq, f_xb[:, par, kd, :], f_xb[:, par, kd, :],
                        ALU.mult), dve, 1)
                    sx = dve.v
                    P.wait("pe", dve, sx)
                    P.do("pe", lambda E, kd=kd, par=par: E.matmul(
                        PS1[0:16, :], cones[:, :], f_xb[:, par, kd, :],
                        start=(kd == 0), stop=(kd == KD - 1)))
                    P.do("pe", lambda E, kd=kd: E.matmul(
                        PY[0:16, 0, :], cones[:, :], f_sq,
                        start=(kd == 0), stop=(kd == KD - 1)))
                    P.do("pe", lambda E, kd=kd: E.matmul(
                        PY[0:16, 1, :], headsb[:, kd, :], f_xz,
                        start=(kd == 0), stop=(kd == KD - 1)))
                    P.do("pe", lambda E, kd=kd, par=par: E.matmul(
                        PH[0:16, 2, 0:512], f_wsel[:, kd, :], f_xb[:, par, kd, :],
                        start=(kd == 0), stop=(kd == KD - 1)), pe, 1)
                    last_stat[0] = pe.v
                stats_done = pe.v
                xb_pool_use[par] = dve.v
                xb_pe_use[par] = pe.v
                P.wait("act", pe, stats_done)
                P.do("act", lambda E: E.activation(st16[0:16, 0, :], PS1[0:16, :],
                                                   AF.Copy, scale=1.0 / 1024.0), act, 1)
                P.do("act", lambda E: E.activation(st16[0:16, 1, :], PY[0:16, 0, :],
                                                   AF.Copy, scale=1.0 / 1024.0), act, 1)
                P.wait("act", dve)
                P.do("act", lambda E: E.activation(PH[0:16, 3, 0:512], st16[0:16, 0, :],
                                                   AF.Square), act, 1)
                m2d = act.v
                P.wait("dve", act, m2d)
                P.do("dve", lambda E: E.tensor_tensor(st16[0:16, 1, :], st16[0:16, 1, :],
                                                      PH[0:16, 3, 0:512],
                                                      ALU.subtract), dve, 1)
                vd = dve.v
                P.wait("act", dve, vd)
                P.do("act", lambda E: E.activation(st16[0:16, 1, :], st16[0:16, 1, :],
                                                   AF.Ln, bias=fbc[0:16, 0:1]), act, 1)
                P.do("act", lambda E: E.activation(st16[0:16, 1, :], st16[0:16, 1, :],
                                                   AF.Exp, scale=-1.0), act, 1)
                ivd = act.v
                P.wait("dve", act, ivd)
                P.do("dve", lambda E: E.tensor_tensor(stg32[0:16, 0, :], st16[0:16, 0, :],
                                                      PH[0:16, 2, 0:512],
                                                      ALU.mult), dve, 1)
                P.do("dve", lambda E: E.tensor_tensor(stg32[0:16, 0, :], PY[0:16, 1, :],
                                                      stg32[0:16, 0, :],
                                                      ALU.subtract), dve, 1)
                P.do("dve", lambda E: E.scalar_tensor_tensor(
                    stg32[0:16, 0, :], PH[0:16, 3, 0:512], m11f[0:16, 0:1],
                    stg32[0:16, 0, :], ALU.mult, ALU.add), dve, 1)
                P.do("dve", lambda E: E.tensor_tensor(stg32[0:16, 0, :],
                                                      stg32[0:16, 0, :],
                                                      st16[0:16, 1, :], ALU.mult),
                     dve, 1)
                sdone = dve.v
                ps1_use["dve"] = py_use[0]["dve"] = py_use[1]["dve"] = dve.v
                ph_use[2]["dve"] = ph_use[3]["dve"] = dve.v
                ps1_use["act"] = py_use[0]["act"] = act.v
                P.wait("act", dve, sdone)
                P.do("act", lambda E: E.activation(f_et[0:16, :], stg32[0:16, 0, :],
                                                   AF.Exp, scale=1.0 / 32.0), act, 1)
                etd = act.v
                P.wait("dve", act, etd)
                P.do("dve", lambda E, ts=ts: E.tensor_tensor(
                    seacc[0:16, ts], seacc[0:16, ts], f_et[0:16, :], ALU.add), dve, 1)
                P.do("dve", lambda E, ci=ci, ts=ts: E.scalar_tensor_tensor(
                    f_em[0:16, ts], f_et[0:16, :], miselb[:, ci:ci + 1],
                    f_em[0:16, ts], ALU.mult, ALU.add), dve, 1)
        for tt in range(NTT):
            ts = TTS[tt]
            P.wait("act", dve)
            P.do("act", lambda E, ts=ts: E.activation(st16[0:16, 0, :], seacc[0:16, ts],
                                                      AF.Ln), act, 1)
            P.do("act", lambda E: E.activation(st16[0:16, 0, :], st16[0:16, 0, :],
                                               AF.Exp, scale=-1.0), act, 1)
            ivse = act.v
            P.wait("dve", act, ivse)
            P.do("dve", lambda E, ts=ts: E.tensor_tensor(
                l0_probs[0:16, ts], f_em[0:16, ts], st16[0:16, 0, :], ALU.mult),
                dve, 1)
        probs_done = dve.v
        for tt in range(NTT):
            ts = TTS[tt]
            for kd in range(KD):
                pe_claim(ph_use[2])
                P.do("pe", lambda E, kd=kd, ts=ts: E.matmul(
                    PH[:, 2, :], l0_wblk[:, 2, :], xn[:, kd, ts],
                    start=True, stop=True), pe, 1)
                v_mm = pe.v
                P.wait("act", pe, v_mm)
                P.do("act", lambda E, kd=kd, tt=tt: E.activation(
                    l0_vs, PH[:, 2, :], AF.Copy,
                    accum_out=vsump_v[:, kd, tt:tt + 1]), act, 1)
                ph_use[2]["act"] = act.v
        P.wait("dve", act)
        P.do("dve", lambda E: E.tensor_tensor(vsum8_v, vsump_v[:, :, 0],
                                              vsump_v[:, :, 1], ALU.add), dve, 1)
        P.do("dve", lambda E: E.tensor_tensor(vsc2_v, vsump_v[:, :, 2],
                                              vsump_v[:, :, 3], ALU.add), dve, 1)
        P.do("dve", lambda E: E.tensor_tensor(vsum8_v, vsum8_v, vsc2_v, ALU.add),
             dve, 1)
        vs_done = dve.v
        P.wait("pe", dve, max(probs_done, vs_done))
        for tt in range(NTT):
            ts = TTS[tt]
            for kd in range(KD):
                bk = kd % 4
                pe_claim(ph_use[bk])
                P.do("pe", lambda E, kd=kd, ts=ts, bk=bk: E.matmul(
                    PH[:, bk, :], selb[:, kd, :], l0_probs[0:16, ts],
                    start=True, stop=True), pe, 1)
                mm_done = pe.v
                P.wait("dve", pe, mm_done)
                P.do("dve", lambda E, kd=kd, ts=ts, bk=bk: E.scalar_tensor_tensor(
                    xsb[:, kd, ts], PH[:, bk, :], vsum8_v[:, kd:kd + 1], xsb[:, kd, ts],
                    ALU.mult, ALU.add), dve, 1)
                ph_use[bk]["dve"] = dve.v

    def emit_ffn(L):
        P.wait("sp", pe)
        P.do("sp", lambda E: E.dma_start(out=_fs(wrsb), in_=_fs(dr[f"w_l{L}wr"])),
             S["wrs"], 16)
        P.wait("pe", S["wrs"])

        def router(kd, tt, stage_ap):
            if kd == 0:
                pe_claim(ph_use[tt])
            P.do("pe", lambda E, kd=kd, tt=tt, ap=stage_ap: E.matmul(
                PH[0:8, tt, :], wrsb[:, kd, :], ap,
                start=(kd == 0), stop=(kd == KD - 1)), pe, 1)

        emit_ln(xsb, xn, router=router)
        cols_act = [0]
        if L == 1:
            pass
        for tt in range(NTT):
            ts = TTS[tt]
            P.wait("dve", pe)
            P.do("dve", lambda E, tt=tt: E.tensor_scalar(
                lftt_v[0:8, tt % 2, :], PH[0:8, tt, :], brsb[:, L:L + 1], None,
                ALU.add), dve, 1)
            ph_use[tt]["dve"] = dve.v
            lf_done = dve.v
            for ci in range(4):
                ct = tt * 4 + ci
                cs = slice(ci * 128, (ci + 1) * 128)
                slot = ci % 2
                P.wait("pe", dve, lf_done)
                pe_claim(py_use[slot])
                P.do("pe", lambda E, cs=cs, slot=slot, tt=tt: E.transpose(
                    PY[:, slot, 0:8], lftt_v[0:8, tt % 2, cs], idf[0:8, 0:8]), pe, 1)
                tr_done = pe.v
                P.wait("dve", pe, tr_done)
                P.wait("dve", act, cols_act[0])
                P.do("dve", lambda E, slot=slot: E.tensor_reduce(
                    cols_v[:, 0:1], PY[:, slot, 0:8], AX.X, ALU.max), dve, 1)
                if L == 1:
                    P.do("dve", lambda E, slot=slot: E.tensor_reduce(
                        cols_v[:, 1:2], PY[:, slot, 0:8], AX.X, ALU.max,
                        negate=True), dve, 1)
                P.do("dve", lambda E, slot=slot: E.tensor_scalar(
                    ohtm[:, :], PY[:, slot, 0:8], cols_v[:, 0:1], None,
                    ALU.is_equal), dve, 1)
                oh_done = dve.v
                py_use[slot]["dve"] = dve.v
                if L == 0:
                    P.wait("act", dve, oh_done)
                    P.do("act", lambda E, ct=ct: E.activation(
                        gcolb[:, ct:ct + 1], cols_v[:, 0:1], AF.Sigmoid), act, 1)
                    cols_act[0] = act.v
                else:
                    P.wait("act", dve, oh_done)
                    P.do("act", lambda E, slot=slot: E.activation(
                        ex_v[:, :], PY[:, slot, 0:8], AF.Exp,
                        bias=cols_v[:, 1:2]), act, 1)
                    exd = act.v
                    cols_act[0] = act.v
                    py_use[slot]["act"] = act.v
                    P.wait("dve", act, exd)
                    P.do("dve", lambda E, ct=ct: E.tensor_reduce(
                        secol_v[:, ct:ct + 1], ex_v[:, :], AX.X, ALU.add), dve, 1)
                P.wait("pe", dve, oh_done)
                P.wait("pe", dve, pt_oh_use[0])
                P.do("pe", lambda E: E.transpose(pt_oh[:, :], ohtm[:, :], idb[:, :]), pe, 1)
                trb = pe.v
                P.wait("dve", pe, trb)
                P.do("dve", lambda E, ct=ct: E.tensor_copy(
                    ohfm_ct[0:8, ct, :], pt_oh[0:8, :]), dve, 1)
                pt_oh_use[0] = dve.v
        if L == 1:
            P.do("dve", lambda E: E.reciprocal(ginv_v[:, :], secol_v[:, :]), dve, 1)
            P.do("dve", lambda E: E.tensor_copy(gcolb[:, :], ginv_v[:, :]), dve, 1)
        g_done = dve.v
        P.wait("pe", dve, g_done)
        P.wait("pe", act)
        grow_pe_use = [0]
        for tt in range(NTT):
            for ci in range(4):
                ct = tt * 4 + ci
                P.wait("pe", dve, ptrow_use[0])
                P.do("pe", lambda E, ct=ct: E.transpose(
                    ptrow[:, :], gcolb[:, ct:ct + 1], idb[:, :]), pe, 1)
                trg = pe.v
                P.wait("dve", pe, trg)
                P.wait("dve", pe, grow_pe_use[0])
                P.do("dve", lambda E, ci=ci: E.tensor_copy(
                    grow_sm[0:1, ci * 128:(ci + 1) * 128], ptrow[0:1, :]), dve, 1)
                ptrow_use[0] = dve.v
            gr = dve.v
            P.wait("pe", dve, gr)
            pe_claim(ps1_use)
            P.do("pe", lambda E: E.matmul(PS1[:, :], conesrow[:, :], grow_sm[0:1, :],
                                          start=True, stop=True), pe, 1)
            gb = pe.v
            grow_pe_use[0] = pe.v
            P.wait("dve", pe, gb)
            P.do("dve", lambda E, tt=tt: E.tensor_copy(gateb_v[:, TTS[tt]], PS1[:, :]),
                 dve, 1)
            ps1_use["dve"] = dve.v
        if L == 1:
            emit_ln(xn, xn, affine=True)
        oh_ready = dve.v
        w1_dr, w2_dr = dr[f"w_l{L}w1"], dr[f"w_l{L}w2"]
        w1_pe_use = [pe.v, pe.v]
        w2_pe_use = [pe.v, pe.v]
        w1n, w2n = [0], [0]
        def emit_gmask(e):
            gv = gmask_v if e % 2 == 0 else gmask2_v
            P.wait("pe", dve, oh_ready)
            for tt in range(NTT):
                pe_claim(ps1_use)
                P.do("pe", lambda E, e=e, tt=tt: E.matmul(
                    PS1[:, :], eones[:, e, :], ohfm_v[0:8, TTS[tt]],
                    start=True, stop=True), pe, 1)
                gm = pe.v
                P.wait("dve", pe, gm)
                P.do("dve", lambda E, tt=tt, gv=gv: E.tensor_tensor(
                    gv[:, TTS[tt]], PS1[:, :], gateb_v[:, TTS[tt]], ALU.mult),
                    dve, 1)
                ps1_use["dve"] = dve.v
            return dve.v

        gmask_done = emit_gmask(0)
        for e in range(NE):
            for fq in range(NFQ):
                for fc8 in range(8):
                    fc = fq * 8 + fc8
                    par = w1n[0] % 2
                    P.wait("pool", pe, w1_pe_use[par])
                    P.do("pool", lambda E, e=e, fc=fc, par=par: E.dma_start(
                        out=w1c_v[:, par, :, :], in_=w1_dr[e, fc, :, :, :]),
                        w1s[par], 16)
                    w1n[0] += 1
                    P.wait("pe", w1s[par])
                    for tp in range(2):
                        for tt in (2 * tp, 2 * tp + 1):
                            pe_claim(ph_use[tt])
                            for kd in range(KD):
                                P.do("pe", lambda E, kd=kd, tt=tt, par=par: E.matmul(
                                    PH[:, tt, :], w1c_v[:, par, kd, :], xn[:, kd, TTS[tt]],
                                    start=(kd == 0), stop=(kd == KD - 1)),
                                    pe if kd == KD - 1 else None, 1)
                        mmv = pe.v
                        P.wait("act", pe, mmv)
                        php = PH[:, 2 * tp:2 * tp + 2, :].rearrange("p a b -> p (a b)")
                        if L == 0:
                            P.do("act", lambda E, fc8=fc8, tp=tp, php=php: E.activation(
                                hq_v[:, fc8, TTP[tp]], php, AF.Gelu), act, 1)
                        else:
                            P.do("act", lambda E, fc8=fc8, tp=tp, e=e, fc=fc, php=php:
                                 E.activation(
                                     hq_v[:, fc8, TTP[tp]], php, AF.Relu,
                                     bias=b1sb[:, e, fc:fc + 1]), act, 1)
                        ph_use[2 * tp]["act"] = act.v
                        ph_use[2 * tp + 1]["act"] = act.v
                    w1_pe_use[par] = pe.v
                h_done = act.v
                for dout in range(8):
                    par = w2n[0] % 2
                    P.wait("pool", pe, w2_pe_use[par])
                    P.do("pool", lambda E, e=e, fq=fq, dout=dout, par=par: E.dma_start(
                        out=w2c_v[:, par, :, :], in_=w2_dr[e, fq, dout, :, :, :]),
                        w2s[par], 16)
                    w2n[0] += 1
                    P.wait("pe", w2s[par])
                    P.wait("pe", act, h_done)
                    if dout == 0 and fq == 0:
                        P.wait("pe", dve, gmask_done)
                    for tp in range(2):
                        for tt in (2 * tp, 2 * tp + 1):
                            pe_claim(ph_use[tt])
                            for fc8 in range(8):
                                P.do("pe", lambda E, fc8=fc8, tt=tt, par=par: E.matmul(
                                    PH[:, tt, :], w2c_v[:, par, fc8, :],
                                    hq_v[:, fc8, TTS[tt]],
                                    start=(fc8 == 0), stop=(fc8 == 7)),
                                    pe if fc8 == 7 else None, 1)
                        mmv = pe.v
                        P.wait("dve", pe, mmv)
                        php = PH[:, 2 * tp:2 * tp + 2, :].rearrange("p a b -> p (a b)")
                        if fq == 0:
                            P.do("dve", lambda E, dout=dout, tp=tp, php=php:
                                 E.tensor_copy(yac_v[:, dout, TTP[tp]], php), dve, 1)
                        elif fq < 3 or L == 0:
                            P.do("dve", lambda E, dout=dout, tp=tp, php=php:
                                 E.tensor_tensor(
                                     yac_v[:, dout, TTP[tp]], yac_v[:, dout, TTP[tp]],
                                     php, ALU.add), dve, 1)
                        else:
                            P.do("dve", lambda E, dout=dout, tp=tp, e=e, php=php:
                                 E.scalar_tensor_tensor(
                                     yac_v[:, dout, TTP[tp]], php,
                                     b2sb[:, e, dout:dout + 1], yac_v[:, dout, TTP[tp]],
                                     ALU.add, ALU.add), dve, 1)
                        ph_use[2 * tp]["dve"] = dve.v
                        ph_use[2 * tp + 1]["dve"] = dve.v
                    w2_pe_use[par] = pe.v
            if e + 1 < NE:
                gmask_done = emit_gmask(e + 1)
            gv = gmask_v if e % 2 == 0 else gmask2_v
            for dout in range(8):
                P.do("dve", lambda E, dout=dout, gv=gv: E.tensor_tensor(
                    yac_v[:, dout, :], yac_v[:, dout, :], gv[:, :],
                    ALU.mult), dve, 1)
                P.do("dve", lambda E, dout=dout: E.tensor_tensor(
                    xsb[:, dout, :], xsb[:, dout, :], yac_v[:, dout, :],
                    ALU.add), dve, 1)

    def emit_l1a():
        if L1A_TRUNC < 0:
            return
        emit_ln(xsb, xn)
        if L1A_TRUNC < 1:
            return
        P.do("dve", lambda E: E.memset(Sf_v, 0.0), dve, 1)
        P.do("dve", lambda E: E.memset(sbf_v, 0.0), dve, 1)
        P.do("dve", lambda E: E.memset(kds_v, 0.0), dve, 1)
        for tt in range(NTT):
            ts = TTS[tt]
            pe_claim(ps1_use)
            for kd in range(KD):
                P.do("pe", lambda E, kd=kd, ts=ts: E.matmul(
                    PS1[0:16, :], wfTsb[:, kd, :], xn[:, kd, ts],
                    start=(kd == 0), stop=(kd == KD - 1)),
                    pe if kd == KD - 1 else None, 1)
            f_done = pe.v
            P.wait("dve", pe, f_done)
            P.do("dve", lambda E: E.tensor_scalar(st16[0:16, 1, :], PS1[0:16, :],
                                                  dlt[:, 0:1], 0.0, ALU.add, ALU.min),
                 dve, 1)
            min_done = dve.v
            ps1_use["dve"] = dve.v
            P.wait("act", pe, f_done)
            P.wait("act", dve, min_done)
            P.do("act", lambda E: E.activation(st16[0:16, 0, :], PS1[0:16, :], AF.Abs,
                                               bias=dlt[:, 0:1]), act, 1)
            P.do("act", lambda E: E.activation(st16[0:16, 0, :], st16[0:16, 0, :],
                                               AF.Exp, scale=-1.0), act, 1)
            P.do("act", lambda E: E.activation(st16[0:16, 0, :], st16[0:16, 0, :],
                                               AF.Ln, bias=fbc[0:16, 2:3]), act, 1)
            t3_done = act.v
            ps1_use["act"] = t3_done
            P.wait("dve", act, t3_done)
            P.do("dve", lambda E: E.tensor_tensor(st16[0:16, 1, :], st16[0:16, 1, :],
                                                  st16[0:16, 0, :], ALU.subtract), dve, 1)
            for ci in range(4):
                c0 = ci * 128
                P.do("dve", lambda E, c0=c0, ts=ts: E.tensor_tensor_scan(
                    lbbf_v[0:16, ts.start + c0:ts.start + c0 + 128],
                    st16[0:16, 1, c0:c0 + 128], st16[0:16, 1, c0:c0 + 128],
                    0.0, ALU.add, ALU.bypass), dve, 1)
        lb_done = dve.v
        if L1A_TRUNC < 2:
            return
        P.wait("act", dve, lb_done)
        P.do("act", lambda E: E.activation(elbbf_v[0:16, :], lbbf_v[0:16, :], AF.Exp),
             act, 1)
        P.do("dve", lambda E: E.tensor_copy(totf_v[0:16, :], lbbf_v[0:16, 127::128]),
             dve, 1)
        tot_done = dve.v
        P.wait("act", dve, tot_done)
        P.do("act", lambda E: E.activation(etot_bf[0:16, :], totf_v[0:16, :], AF.Exp),
             act, 1)
        etot_done = act.v
        for c in range(16):
            P.do("dve", lambda E, c=c: E.tensor_scalar(
                sfx_v[0:16, c * 128:(c + 1) * 128], lbbf_v[0:16, c * 128:(c + 1) * 128],
                totf_v[0:16, c:c + 1], -1.0, ALU.subtract, ALU.mult), dve, 1)
        sfx_done = dve.v
        P.wait("pe", act, etot_done)
        for kd in range(KD):
            bk = kd % 2
            pe_claim(py_use[bk])
            P.do("pe", lambda E, kd=kd, bk=bk: E.matmul(
                PY[:, bk, 0:16], selb[:, kd, :], etot_bf[0:16, :],
                start=True, stop=True), pe, 1)
            bc = pe.v
            P.wait("dve", pe, bc)
            P.do("dve", lambda E, kd=kd, bk=bk: E.tensor_copy(
                etotbc_v[:, kd, :], PY[:, bk, 0:16]), dve, 1)
            py_use[bk]["dve"] = dve.v
        P.wait("pe", dve, sfx_done)
        for c in range(16):
            P.wait("pe", act)
            P.do("pe", lambda E, c=c: E.transpose(
                pt_lb[:, :], lbbf_v[0:16, c * 128:(c + 1) * 128], idb[0:16, 0:16]), pe, 1)
            t1 = pe.v
            P.wait("act", pe, t1)
            P.do("act", lambda E, c=c: E.activation(enltm_v[:, c, :], pt_lb[:, :],
                                                    AF.Exp, scale=-1.0), act, 1)
            en_done = act.v
            P.wait("pe", act, en_done)
            P.do("pe", lambda E, c=c: E.transpose(
                pt_lb[:, :], sfx_v[0:16, c * 128:(c + 1) * 128], idb[0:16, 0:16]), pe, 1)
            t2 = pe.v
            P.wait("act", pe, t2)
            P.do("act", lambda E, c=c: E.activation(ktab_v[:, c, :], pt_lb[:, :],
                                                    AF.Exp), act, 1)
        tables_act = act.v
        if L1A_TRUNC < 3:
            return

        wq_dr, wk_dr, wv_dr, wo_dr = (dr["w_l1wq"], dr["w_l1wk"], dr["w_l1wv"],
                                      dr["w_l1wo"])
        ws_pe_use = [pe.v, pe.v]
        wsn = [0]

        def stream_w(src):
            par = wsn[0] % 2
            P.wait("pool", pe, ws_pe_use[par])
            P.do("pool", lambda E, src=src, par=par: E.dma_start(
                out=wstream_v[:, par, :, :], in_=_fs(src)), wss[par], 16)
            wsn[0] += 1
            return par, wss[par].v

        def pA(h):
            return PH[:, h // 4, (h % 4) * 128:(h % 4) * 128 + 128]

        def pO(h):
            half, r = 64 * (h % 2), h // 2
            return PY[half:half + 64, r // 4, (r % 4) * 128:(r % 4) * 128 + 128]

        def pS(h):
            half, r = 64 * (h % 2), h // 2
            return PS1[half:half + 64, r * 64:r * 64 + 64]

        def hsl(h):
            return slice(64 * (h % 2), 64 * (h % 2) + 64)

        first_wave = True
        for w in range(NW):
            wts = slice(w * NT, (w + 1) * NT)
            par, wv_ = stream_w(wq_dr)
            P.wait("pe", wss[par], wv_)
            for ec in range(KD):
                bk = ec % 4
                pe_claim(ph_use[bk])
                for kd in range(KD):
                    P.do("pe", lambda E, ec=ec, kd=kd, bk=bk, par=par, wts=wts: E.matmul(
                        PH[:, bk, :], wstream_v[:, par, kd, ec * 128:(ec + 1) * 128],
                        xn[:, kd, wts], start=(kd == 0), stop=(kd == KD - 1)),
                        pe if kd == KD - 1 else None, 1)
                qd = pe.v
                P.wait("act", pe, qd)
                P.do("act", lambda E, ec=ec, bk=bk: E.activation(
                    qfm_v[:, ec, :], PH[:, bk, :], AF.Silu), act, 1)
                ph_use[bk]["act"] = act.v
            ws_pe_use[par] = pe.v
            par, wv_ = stream_w(wk_dr)
            P.wait("pe", wss[par], wv_)
            for ec in range(KD):
                bk = ec % 4
                pe_claim(ph_use[bk])
                for kd in range(KD):
                    P.do("pe", lambda E, ec=ec, kd=kd, bk=bk, par=par, wts=wts: E.matmul(
                        PH[:, bk, :], wstream_v[:, par, kd, ec * 128:(ec + 1) * 128],
                        xn[:, kd, wts], start=(kd == 0), stop=(kd == KD - 1)),
                        pe if kd == KD - 1 else None, 1)
                kdn = pe.v
                P.wait("act", pe, kdn)
                P.do("act", lambda E, ec=ec, bk=bk: E.activation(
                    kfm_v[:, ec, :], PH[:, bk, :], AF.Silu), act, 1)
                ph_use[bk]["act"] = act.v
                kf_done = act.v
                P.wait("dve", act, kf_done)
                P.do("dve", lambda E, ec=ec: E.tensor_scalar(
                    kmsk_v[:, 0, ec, :], kfm_v[:, ec, :], fbc[:, 4:5], None,
                    ALU.mult), dve, 1)
                P.do("dve", lambda E, ec=ec: E.tensor_scalar(
                    kmsk_v[:, 1, ec, :], kfm_v[:, ec, :], fbc[:, 5:6], None,
                    ALU.mult), dve, 1)
            for ct in range(CPW):
                c0 = wts.start + ct * 128
                for eh in range(2):
                    bk = (ct * 2 + eh) % 4
                    pe_claim(ph_use[bk])
                    for kd in range(KD):
                        P.do("pe", lambda E, kd=kd, bk=bk, par=par, c0=c0, eh=eh:
                             E.matmul(PH[:, bk, :], xn[:, kd, c0:c0 + 128],
                                      wstream_v[:, par, kd, eh * 512:(eh + 1) * 512],
                                      start=(kd == 0), stop=(kd == KD - 1)),
                             pe if kd == KD - 1 else None, 1)
                    ktd = pe.v
                    P.wait("act", pe, ktd)
                    P.do("act", lambda E, ct=ct, eh=eh, bk=bk: E.activation(
                        ktm_v[:, ct, eh * 512:(eh + 1) * 512], PH[:, bk, :], AF.Silu),
                        act, 1)
                    ph_use[bk]["act"] = act.v
            ws_pe_use[par] = pe.v
            par, wv_ = stream_w(wv_dr)
            P.wait("pe", wss[par], wv_)
            for ct in range(CPW):
                c0 = wts.start + ct * 128
                for eh in range(2):
                    bk = (ct * 2 + eh) % 4
                    pe_claim(ph_use[bk])
                    for kd in range(KD):
                        P.do("pe", lambda E, kd=kd, bk=bk, par=par, c0=c0, eh=eh:
                             E.matmul(PH[:, bk, :], xn[:, kd, c0:c0 + 128],
                                      wstream_v[:, par, kd, eh * 512:(eh + 1) * 512],
                                      start=(kd == 0), stop=(kd == KD - 1)),
                             pe if kd == KD - 1 else None, 1)
                    vtd = pe.v
                    P.wait("dve", pe, vtd)
                    P.do("dve", lambda E, ct=ct, eh=eh, bk=bk: E.tensor_copy(
                        vtm_v[:, ct, eh * 512:(eh + 1) * 512], PH[:, bk, :]), dve, 1)
                    ph_use[bk]["dve"] = dve.v
            ws_pe_use[par] = pe.v
            proj_act = act.v
            proj_dve = dve.v
            if L1A_TRUNC < 4:
                continue
            def pA(k):
                return PH[:, k // 4, (k % 4) * 128:(k % 4) * 128 + 128]

            def pO(k):
                return PY[:, k // 4, (k % 4) * 128:(k % 4) * 128 + 128]

            def pS(k):
                return PS1[:, k * 64:k * 64 + 64]

            def hsl(h):
                return slice(64 * (h % 2), 64 * (h % 2) + 64)

            if first_wave:
                P.wait("dve", act, tables_act)
                first_wave = False
            import os as _o8
            _scan_stop = int(_o8.environ.get("SCAN_STOP", "99"))
            for cl in range(CPW):
                c = w * CPW + cl
                if c > _scan_stop:
                    continue
                csl = slice(cl * 128, (cl + 1) * 128)
                P.wait("pe", act, proj_act)
                P.wait("pe", dve, dve.v)
                for kd in range(KD):
                    if c > int(_o8.environ.get("SKIP_ELB_AFTER", "99")):
                        continue
                    bk = kd % 2
                    P.wait("pe", dve)
                    P.do("pe", lambda E, kd=kd, bk=bk, csl=csl, w=w: E.matmul(
                        PH[:, bk, 0:128], selb[:, kd, :],
                        elbbf_v[0:16, w * NT + cl * 128:w * NT + cl * 128 + 128]
                        if False else elbbf_v[0:16, csl.start + w * NT:csl.stop + w * NT],
                        start=True, stop=True), pe, 1)
                    eb = pe.v
                    P.wait("dve", pe, eb)
                    P.do("dve", lambda E, kd=kd, bk=bk: E.tensor_copy(
                        elbbc_v[:, kd, :], PH[:, bk, 0:128]), dve, 1)
                for hg in range(2):
                    hb = hg * 8
                    P.wait("pe", dve, dve.v)
                    for k in range(8):
                        h = hb + k
                        P.do("pe", lambda E, h=h, k=k, hg=hg, csl=csl: E.matmul(
                            pA(hg * 8 + k), kmsk_v[:, h % 2, h // 2, csl],
                            qfm_v[:, h // 2, csl], start=True, stop=True),
                            pe if k == 7 else None, 1)
                    mma_done = pe.v
                    P.wait("dve", pe, mma_done)
                    for k in range(8):
                        h = hb + k
                        if c > int(_o8.environ.get("SKIP_A2_AFTER", "99")):
                            continue
                        P.do("dve", lambda E, h=h, k=k, hg=hg, c=c: E.scalar_tensor_tensor(
                            a2_v[:, h, :], pA(hg * 8 + k), enltm_v[:, c, h:h + 1],
                            maskt[:, :], ALU.mult, ALU.mult), dve, 1)
                    a2_done = dve.v
                    P.wait("pe", dve, a2_done)
                    for k in range(8):
                        h = hb + k
                        P.do("pe", lambda E, h=h, k=k, cl=cl: E.matmul(
                            pO(k), vtm_v[:, cl, (h // 2) * 128:(h // 2) * 128 + 128],
                            a2_v[:, h, :], start=True, stop=True),
                            pe if k == 7 else None, 1)
                    mmoi_done = pe.v
                    for pr in range(hb // 2, hb // 2 + 4):
                        P.do("dve", lambda E, pr=pr, csl=csl: E.tensor_tensor(
                            qs_v[:, pr % 4, :], qfm_v[:, pr, csl],
                            elbbc_v[:, pr, :], ALU.mult), dve, 1)
                    qs_done = dve.v
                    P.wait("pe", dve, qs_done)
                    for k in range(8):
                        h = hb + k
                        P.do("pe", lambda E, h=h, k=k, hg=hg: E.matmul(
                            pA(hg * 8 + k), sbf_v[:, h, :], qs_v[:, (h // 2) % 4, :],
                            start=True, stop=True), pe if k == 7 else None, 1)
                    mmoe_done = pe.v
                    P.wait("dve", pe, mmoe_done)
                    for k in range(8):
                        h = hb + k
                        P.do("dve", lambda E, h=h, k=k, csl=csl: E.tensor_tensor(
                            obf_v[hsl(h), h // 2, csl], pO(k)[hsl(h), :],
                            elbbc_v[hsl(h), h // 2, :], ALU.mult), dve, 1)
                    for k in range(8):
                        h = hb + k
                        P.do("dve", lambda E, h=h, k=k, hg=hg, csl=csl: E.tensor_tensor(
                            obf_v[hsl(h), h // 2, csl], obf_v[hsl(h), h // 2, csl],
                            pA(hg * 8 + k)[hsl(h), :], ALU.add), dve, 1)
                    for k in range(8):
                        h = hb + k
                        if c > int(_o8.environ.get("SKIP_KDS_AFTER", "99")):
                            continue
                        P.do("dve", lambda E, h=h, cl=cl, c=c: E.tensor_scalar(
                            kds_v[:, h // 2 % 4, hsl(h)],
                            ktm_v[:, cl, h * 64:h * 64 + 64],
                            ktab_v[:, c, h:h + 1], None, ALU.mult), dve, 1)
                    kds_done = dve.v
                    P.wait("pe", dve, kds_done)
                    for k in range(8):
                        h = hb + k
                        P.do("pe", lambda E, h=h, k=k, cl=cl: E.matmul(
                            pS(k), kds_v[:, h // 2 % 4, :],
                            vtm_v[:, cl, h * 64:h * 64 + 64],
                            start=True, stop=True), pe if k == 7 else None, 1)
                    mms_done = pe.v
                    P.wait("dve", pe, mms_done)
                    for k in range(8):
                        h = hb + k
                        if c > int(_o8.environ.get("SKIP_SF_AFTER", "99")):
                            continue
                        P.do("dve", lambda E, h=h, k=k, c=c: E.scalar_tensor_tensor(
                            Sf_v[hsl(h), h // 2, :], Sf_v[hsl(h), h // 2, :],
                            etotbc_v[hsl(h), h // 2, c:c + 1], pS(k)[hsl(h), :],
                            ALU.mult, ALU.add), dve, 1)
                    for k in range(8):
                        h = hb + k
                        if c > int(_o8.environ.get("SKIP_SBF_AFTER", "99")):
                            continue
                        P.do("dve", lambda E, h=h: E.tensor_copy(
                            sbf_v[hsl(h), h, hsl(h)], Sf_v[hsl(h), h // 2, :]), dve, 1)
                for i in range(4):
                    ph_use[i]["dve"] = dve.v
                py_use[0]["dve"] = py_use[1]["dve"] = dve.v
                ps1_use["dve"] = dve.v
            if L1A_TRUNC < 5:
                continue
            scan_done = dve.v
            sq_pe = [0, 0]
            pe_claim(ps1_use)
            P.wait("act", dve, scan_done)
            for kd in range(KD):
                par2 = kd % 2
                P.wait("act", pe, sq_pe[par2])
                P.do("act", lambda E, kd=kd, par2=par2: E.activation(
                    sqt[:, par2, :], obf_v[:, kd, :], AF.Square), act, 1)
                sqd = act.v
                P.wait("pe", act, sqd)
                P.do("pe", lambda E, par2=par2, kd=kd: E.matmul(
                    PS1[0:16, :], cones[:, :], sqt[:, par2, :],
                    start=(kd == 0), stop=(kd == KD - 1)), pe, 1)
                sq_pe[par2] = pe.v
            oss_done = pe.v
            P.wait("act", pe, oss_done)
            P.do("act", lambda E: E.activation(st16[0:16, 0, :], PS1[0:16, :], AF.Copy,
                                               scale=1.0 / 1024.0), act, 1)
            ps1_use["act"] = act.v
            P.do("act", lambda E: E.activation(st16[0:16, 0, :], st16[0:16, 0, :],
                                               AF.Ln, bias=fbc[0:16, 1:2]), act, 1)
            P.wait("act", pe, rows_war[0])
            P.do("act", lambda E: E.activation(rows_sm[0:1, 1, :], st16[0:1, 0, :],
                                               AF.Exp, scale=-0.5), act, 1)
            orstd_done = act.v
            P.wait("pe", act, orstd_done)
            pe_claim(py_use[0])
            P.do("pe", lambda E: E.matmul(PY[:, 0, :], conesrow_f[:, :],
                                          rows_sm[0:1, 1, :], start=True, stop=True),
                 pe, 1)
            ob = pe.v
            rows_war[0] = pe.v
            P.wait("dve", pe, ob)
            import os as _o9
            for kd in range(KD):
                if _o9.environ.get("SKIP_RMS") == "1":
                    continue
                P.do("dve", lambda E, kd=kd: E.scalar_tensor_tensor(
                    obf_v[:, kd, :], PY[:, 0, :], nwsb[:, kd:kd + 1], obf_v[:, kd, :],
                    ALU.mult, ALU.mult), dve, 1)
            py_use[0]["dve"] = dve.v
            on_done = dve.v
            par, wv_ = stream_w(wo_dr)
            P.wait("pe", wss[par], wv_)
            P.wait("pe", dve, on_done)
            for ec in range(KD):
                bk = ec % 4
                pe_claim(ph_use[bk])
                for kd in range(KD):
                    P.do("pe", lambda E, ec=ec, kd=kd, bk=bk, par=par: E.matmul(
                        PH[:, bk, :], wstream_v[:, par, kd, ec * 128:(ec + 1) * 128],
                        obf_v[:, kd, :], start=(kd == 0), stop=(kd == KD - 1)),
                        pe if kd == KD - 1 else None, 1)
                od = pe.v
                P.wait("dve", pe, od)
                P.do("dve", lambda E, ec=ec, bk=bk, wts=wts: E.tensor_tensor(
                    xsb[:, ec, wts], xsb[:, ec, wts], PH[:, bk, :], ALU.add), dve, 1)
                ph_use[bk]["dve"] = dve.v
            ws_pe_use[par] = pe.v

    emits = {"l0a": emit_l0a, "l0a_scores": emit_l0a, "l0a_f": emit_l0a_fused,
             "l0f": lambda: emit_ffn(0), "l1f": lambda: emit_ffn(1), "l1a": emit_l1a}
    for b in blocks:
        emits[b]()
        dump("x_" + b, xsb[:, :, :], [128, KD, T], BF16)

    P.wait("pool", pe)
    P.wait("pool", dve)
    P.wait("pool", act)
    P.do("pool", lambda E: E.dma_start(
        out=xout.rearrange("(kd p) t -> p kd t", p=128), in_=xsb[:, :, :]), ode, 16)
    P.wait("pool", ode)
    return P, dumps_made



def _consts():
    c = {}
    c["c_ones"] = np.ones((128, 16), np.float32)
    c["c_onesrow"] = np.ones((1, 128), np.float32)
    sel = np.zeros((16, 8, 128), np.float32)
    for kd in range(8):
        sel[2 * kd, kd, 0:64] = 1.0
        sel[2 * kd + 1, kd, 64:128] = 1.0
    c["c_sel"] = sel
    hs = np.zeros((128, 8, 16), np.float32)
    for kd in range(8):
        for p in range(128):
            hs[p, kd, 2 * kd + (p >= 64)] = 1.0
    c["c_headsel"] = hs
    ss = np.zeros((128, 16), np.float32)
    for p in range(128):
        ss[p, p % 16] = 1.0
    c["c_ss"] = ss
    eo = np.zeros((8, 8, 128), np.float32)
    for e in range(8):
        eo[e, e, :] = 1.0
    c["c_eones"] = eo
    c["c_id"] = np.eye(128, dtype=np.float32)
    m = np.zeros((128, 128), np.float32)
    for j in range(128):
        m[j, j:] = 1.0
    c["c_mask"] = m
    fb = np.zeros((128, 6), np.float32)
    fb[:, 0] = 1e-5
    fb[:, 1] = 1e-6
    fb[:, 2] = 1.0
    fb[0:64, 4] = 1.0
    fb[64:128, 5] = 1.0
    c["c_fb"] = fb
    return c


def _stage_weights(inp):
    st = {}
    A = np.ascontiguousarray

    def w1tile(W1):
        return A(W1.reshape(8, 8, 128, 32, 128).transpose(0, 3, 2, 1, 4))

    def w2tile(W2):
        return A(W2.reshape(8, 4, 8, 128, 8, 128).transpose(0, 1, 4, 3, 2, 5))

    blk = np.zeros((128, 3, 128), np.float32)
    for i, nm in enumerate(("L0_attn_Wq", "L0_attn_Wk", "L0_attn_Wv")):
        Wt = np.asarray(inp[nm], np.float32).T
        blk[0:64, i, 0:64] = Wt
        blk[64:128, i, 64:128] = Wt
    st["w_l0blk"] = blk
    qbd = np.zeros((128, 128), np.float32)
    kbd = np.zeros((128, 128), np.float32)
    for bb in range(2):
        sl = slice(64 * bb, 64 * bb + 64)
        qbd[sl, sl] = np.asarray(inp["L0_attn_Wq"], np.float32)
        kbd[sl, sl] = np.asarray(inp["L0_attn_Wk"], np.float32)
    st["w_l0qbd"] = qbd
    st["w_l0kbd"] = kbd
    st["w_l0wr"] = A(np.asarray(inp["L0_ffn_Wr"], np.float32).reshape(8, 128, 8)
                     .transpose(1, 0, 2))
    st["w_l0w1"] = w1tile(np.asarray(inp["L0_ffn_W1"], np.float32))
    st["w_l0w2"] = w2tile(np.asarray(inp["L0_ffn_W2"], np.float32))
    st["p_br0"] = np.asarray(inp["L0_ffn_br"], np.float32).reshape(8, 1)
    for nm, key in (("Wq", "wq"), ("Wk", "wk"), ("Wv", "wv"), ("Wo", "wo")):
        Wt = np.asarray(inp["L1_attn_" + nm], np.float32).T
        st["w_l1" + key] = A(Wt.reshape(8, 128, 1024).transpose(1, 0, 2))
    st["w_l1wf"] = A(np.asarray(inp["L1_attn_Wf"], np.float32).T
                     .reshape(8, 128, 16).transpose(1, 0, 2))
    st["p_delta"] = np.asarray(inp["L1_attn_delta"], np.float32).reshape(16, 1)
    st["p_nw"] = A(np.asarray(inp["L1_attn_norm_weight"], np.float32).reshape(8, 128).T)
    st["w_l1wr"] = A(np.asarray(inp["L1_ffn_Wr"], np.float32).reshape(8, 128, 8)
                     .transpose(1, 0, 2))
    st["w_l1w1"] = w1tile(np.asarray(inp["L1_ffn_W1"], np.float32))
    st["w_l1w2"] = w2tile(np.asarray(inp["L1_ffn_W2"], np.float32))
    st["p_br1"] = np.asarray(inp["L1_ffn_br"], np.float32).reshape(8, 1)
    st["p_b1"] = A(np.asarray(inp["L1_ffn_b1"], np.float32).reshape(8, 32, 128)
                   .transpose(2, 0, 1))
    st["p_b2"] = A(np.asarray(inp["L1_ffn_b2"], np.float32).reshape(8, 8, 128)
                   .transpose(2, 0, 1))
    st["p_lng"] = A(np.stack(
        [np.asarray(inp["L1_ffn_ln_g"], np.float32).reshape(8, 128).T,
         np.asarray(inp["L1_ffn_ln_b"], np.float32).reshape(8, 128).T], axis=1))
    return st


_PROG_CACHE = {}


def _get_prog(blocks, dump_names=()):
    key = (tuple(blocks), tuple(dump_names), L1A_TRUNC, LN_TRUNC)
    if key not in _PROG_CACHE:
        P, dumps = build_program(blocks, dump_names)
        nc = P.build()
        _PROG_CACHE[key] = (nc, dumps)
    return _PROG_CACHE[key]


def run_blocks(inp, blocks, xcur=None, dump_names=(), s_all=None):
    nc, dumps = _get_prog(tuple(blocks), tuple(dump_names))
    st = _stage_weights(inp)
    st.update(_consts())
    x = np.asarray(xcur if xcur is not None else inp["L0_attn_x"], np.float32)
    xall_fm = (np.ascontiguousarray(x.transpose(0, 2, 1))
               if any(b == "l0a_f" for b in blocks) else None)
    in_maps = []
    for ci in range(8):
        m = dict(st)
        m["xin"] = np.ascontiguousarray(x[ci].T)
        if "l0a" in blocks or "l0a_scores" in blocks or "l0a_f" in blocks:
            sel = np.zeros((128, 16), np.float32)
            for h in range(16):
                sel[16 * ci + h, h] = 1.0
            m["c_mysel"] = sel
        if "l0a_f" in blocks:
            mis = np.zeros((16, 8), np.float32)
            mis[:, ci] = 1.0
            m["c_misel"] = mis
            m["xall"] = xall_fm
        if s_all is not None:
            m["s_all"] = s_all
        in_maps.append(m)
    res = run_bass_kernel_spmd(nc, in_maps, core_ids=list(range(8)))
    keys = set()
    for r in res.results:
        keys |= set(r.keys())
    if "xout" in keys:
        out = np.stack([np.asarray(res.results[ci]["xout"]).T for ci in range(8)])
        out = out.astype(np.float32)
    else:
        out = None
    extra = {nm: [np.asarray(res.results[ci]["dump_" + nm]) for ci in range(8)]
             for nm in dumps}
    if "s_my" in keys:
        extra["s_my"] = [np.asarray(res.results[ci]["s_my"]) for ci in range(8)]
    return out, extra


def kernel(**inp):
    out, _ = run_blocks(inp, ("l0a_f", "l0f", "l1a", "l1f"))
    return out
